# version 14
# baseline (speedup 1.0000x reference)
import contextlib

import numpy as np

import concourse.bass as bass
import concourse.mybir as mybir
from concourse.bass_utils import run_bass_kernel_spmd

F32 = mybir.dt.float32
BF16 = mybir.dt.bfloat16
FP8 = mybir.dt.float8e4
F16 = mybir.dt.float16
AF = mybir.ActivationFunctionType
OP = mybir.AluOpType

B, S, D, F, H = 4, 4096, 1024, 3072, 4096
T = 2048
KD, KF, KH = D // 128, F // 128, H // 128
TC, NCH = 512, 4
INV_D = 1.0 / D
PIPE_SLOTS = 8
NPIPE = 4
DEPTH = 4


class Sem:
    def __init__(self, nc, st, name):
        self.h = st.enter_context(nc.semaphore(name))
        self.v = 0

    def bump(self, n=1):
        self.v += n
        return self.v


def build_nc():
    nc = bass.Bass("TRN2", target_bir_lowering=False, debug=False)

    xT = nc.dram_tensor("xT", [D, T], F32, kind="ExternalInput").ap()
    wgT = nc.dram_tensor("wgT", [KD, 128, KD, 128], F32, kind="ExternalInput").ap()
    wvT = nc.dram_tensor("wvT", [KD, 128, KD, 128], F32, kind="ExternalInput").ap()
    wdT = nc.dram_tensor("wdT", [KD, 128, KD, 128], F32, kind="ExternalInput").ap()
    wgateT = nc.dram_tensor("wgateT", [KF, 128, KD, 128], F32, kind="ExternalInput").ap()
    wupT = nc.dram_tensor("wupT", [KF, 128, KD, 128], F32, kind="ExternalInput").ap()
    woutT = nc.dram_tensor("woutT", [F, D], F32, kind="ExternalInput").ap()
    w1T = nc.dram_tensor("w1T", [KH, 128, KD, 128], F32, kind="ExternalInput").ap()
    w2T = nc.dram_tensor("w2T", [KD, 128, KH, 128], F32, kind="ExternalInput").ap()
    w2q8 = nc.dram_tensor("w2q8", [KD, 128, KH, 128], FP8).ap()
    bg2 = nc.dram_tensor("bg2", [128, KD], F32, kind="ExternalInput").ap()
    bv2 = nc.dram_tensor("bv2", [128, KD], F32, kind="ExternalInput").ap()
    bd2 = nc.dram_tensor("bd2", [128, KD], F32, kind="ExternalInput").ap()
    b12 = nc.dram_tensor("b12", [128, KH], F32, kind="ExternalInput").ap()
    b22 = nc.dram_tensor("b22", [128, KD], F32, kind="ExternalInput").ap()
    wmix2 = nc.dram_tensor("wmix2", [128, KD], F32, kind="ExternalInput").ap()
    wffn2 = nc.dram_tensor("wffn2", [128, KD], F32, kind="ExternalInput").ap()
    selp = nc.dram_tensor("selp", [128, 1], F32, kind="ExternalInput").ap()

    outT = nc.dram_tensor("outT", [D, T], F32, kind="ExternalOutput").ap()

    xn_h = nc.dram_tensor("xn_h", [D, T], F32).ap()
    xs_h = nc.dram_tensor("xs_h", [D, T], F32).ap()
    a_h = nc.dram_tensor("a_h", [D, T], F32).ap()
    cc_in = nc.dram_tensor("cc_in", [128, KD], F32)
    cc_out = nc.dram_tensor("cc_out", [2, 128, KD], F32)

    def fm(ap_):
        return ap_.rearrange("(k p) t -> p k t", p=128)

    xTv, xnv, xsv, av, outv = fm(xT), fm(xn_h), fm(xs_h), fm(a_h), fm(outT)
    woutv = woutT.rearrange("(k p) m -> k p m", p=128)

    P = []

    def step(eng, body=None, waits=(), incs=()):
        waits = [(s.h, v) for (s, v) in waits]
        incs_ = [(s.h, n) for (s, n) in incs]

        def fn(e):
            for h, v in waits:
                e.wait_ge(h, v)
            inst = body(e) if body is not None else None
            if incs_:
                if inst is None:
                    assert body is None
                    inst = e.nop()
                for h, n in incs_:
                    inst.then_inc(h, n)

        P.append((eng, fn))

    def emit_block():
        with nc.Block() as blk:
            for eng_name in ("sync", "scalar", "vector", "tensor", "gpsimd"):
                def make(eng_name):
                    def emit(e):
                        for en, fn in P:
                            if en == eng_name:
                                fn(e)
                    return emit
                getattr(blk, eng_name)(make(eng_name))
        P.clear()

    with contextlib.ExitStack() as st:
        sb = lambda name, shape, dt: st.enter_context(nc.sbuf_tensor(name, shape, dt))

        ones = sb("ones", [128, 128], F32)
        bgs = sb("bgs", [128, KD], F32)
        bvs = sb("bvs", [128, KD], F32)
        bds = sb("bds", [128, KD], F32)
        b1s = sb("b1s", [128, KH], F32)
        b2s = sb("b2s", [128, KD], F32)
        bq16 = sb("bq16", [128, KD], F16)
        bq32 = sb("bq32", [128, KD], F32)
        wmixs = sb("wmixs", [128, KD], F32)
        wffns = sb("wffns", [128, KD], F32)
        selps = sb("selps", [128, 1], F32)
        carry = sb("carry", [128, KD], F32)
        carry2 = sb("carry2", [128, KD], F32)
        hprev = sb("hprev", [128, KD], F32)
        sg_sb = sb("sg_sb", [128, TC], F32)
        tv_sb = sb("tv_sb", [128, TC], F32)
        sl_sb = sb("sl_sb", [128, TC], F32)
        sq2_sb = sb("sq2_sb", [128, TC], F32)
        tmpf = sb("tmpf", [128, TC], F32)
        f16_sb = sb("f16_sb", [128, TC], F16)
        stat = sb("stat", [1, 7 * TC], F32)
        stg = [sb(f"stg{i}", [128, PIPE_SLOTS, 128], F32) for i in range(NPIPE)]
        wsh = [sb(f"wsh{i}", [128, PIPE_SLOTS, 128], BF16) for i in range(NPIPE)]
        wsl = [sb(f"wsl{i}", [128, PIPE_SLOTS, 128], BF16) for i in range(NPIPE)]

        pb = [st.enter_context(nc.psum_tensor(f"pb{i}", [128, 512], F32))
              for i in range(8)]

        sBIAS = Sem(nc, st, "sBIAS")
        sX2 = [Sem(nc, st, "sXa"), Sem(nc, st, "sXb")]
        sWp = [Sem(nc, st, f"sW{i}") for i in range(NPIPE)]
        sP3A = Sem(nc, st, "sP3A")
        sP3X = Sem(nc, st, "sP3X")
        sP3P = Sem(nc, st, "sP3P")
        sSTp = [Sem(nc, st, "sSTa"), Sem(nc, st, "sSTb")]
        sXN = Sem(nc, st, "sXN")
        sSP = Sem(nc, st, "sSP")
        sGD = Sem(nc, st, "sGD")
        sWQ = Sem(nc, st, "sWQ")
        sQ2 = [Sem(nc, st, "sQ2a"), Sem(nc, st, "sQ2b")]
        sT = Sem(nc, st, "sT")
        sV = Sem(nc, st, "sV")
        sS = Sem(nc, st, "sS")
        sG = Sem(nc, st, "sG")
        sCC = Sem(nc, st, "sCC")
        chain = {n: Sem(nc, st, f"ch_{n[:3]}") for n in ("vector", "scalar", "gpsimd")}

        def CI(en, inst):
            s = chain[en]
            s.v += 1
            inst.then_inc(s.h, 1)
            return inst

        def CW(en, e):
            s = chain[en]
            e.wait_ge(s.h, s.v)

        def row(i, n=TC):
            return stat[0:1, i * TC: i * TC + n]

        pipe_state = {i: {"free": []} for i in range(NPIPE)}
        pipe_par = [0]

        def pipe_load(src_ap, nslots):
            par = pipe_par[0]
            pipe_par[0] = (par + 1) % NPIPE
            ps = pipe_state[par]
            sw = sWp[par]
            stga, wha, wla = stg[par], wsh[par], wsl[par]
            step("sync", lambda e: e.dma_start(
                out=stga[:, 0:nslots, :], in_=src_ap).then_inc(sw.h, 16),
                 waits=list(ps["free"]))
            t_ld = (sw, sw.bump(16))
            step("scalar", lambda e: e.activation(
                wha[:, 0:nslots, :], stga[:, 0:nslots, :], AF.Copy),
                 waits=[t_ld], incs=[(sS, 1)])
            t_hi = (sS, sS.bump())
            step("vector", lambda e: e.tensor_tensor(
                wla[:, 0:nslots, :], stga[:, 0:nslots, :], wha[:, 0:nslots, :],
                OP.subtract), waits=[t_hi], incs=[(sV, 1)])
            t_lo = (sV, sV.bump())
            ps["free"] = [t_lo]
            return ("split", wha, wla, [t_hi, t_lo], par)

        def pipe_load_raw(src_ap, nslots):
            par = pipe_par[0]
            pipe_par[0] = (par + 1) % NPIPE
            ps = pipe_state[par]
            sw = sWp[par]
            stga = stg[par]
            step("sync", lambda e: e.dma_start(
                out=stga[:, 0:nslots, :], in_=src_ap).then_inc(sw.h, 16),
                 waits=list(ps["free"]))
            t_ld = (sw, sw.bump(16))
            ps["free"] = []
            return ("raw", stga, t_ld, None, par)

        stream = []
        hooks_after = {}

        def SU(src, nslots, consume):
            stream.append((False, src, nslots, consume))
            return len(stream) - 1

        def RU(src, nslots, consume):
            stream.append((True, src, nslots, consume))
            return len(stream) - 1

        def HOOK(fn):
            hooks_after.setdefault(len(stream) - 1, []).append(fn)

        def run_stream():
            n = len(stream)
            handles = {}

            def load(j):
                is_raw, src, ns, _ = stream[j]
                return pipe_load_raw(src, ns) if is_raw else pipe_load(src, ns)

            for j in range(min(DEPTH, n)):
                handles[j] = load(j)
            for i in range(n):
                kind, a1, a2, a3, par = handles.pop(i)
                consume = stream[i][3]
                if kind == "split":
                    tok = consume(a1, a2, a3)
                else:
                    tok = consume(a1, a2)
                pipe_state[par]["free"].append(tok)
                for fn in hooks_after.get(i, []):
                    fn()
                j = i + DEPTH
                if j < n:
                    handles[j] = load(j)
            stream.clear()
            hooks_after.clear()

        def mm3(ps_ap, wh, wl, xh, xl, nk):
            def body(e):
                for kc in range(nk):
                    e.matmul(ps_ap, wh[:, kc, :], xh[:, kc, :],
                             start=(kc == 0), stop=False)
                for kc in range(nk):
                    e.matmul(ps_ap, wh[:, kc, :], xl[:, kc, :],
                             start=False, stop=False)
                for kc in range(nk):
                    inst = e.matmul(ps_ap, wl[:, kc, :], xh[:, kc, :],
                                    start=False, stop=(kc == nk - 1))
                return inst
            return body

        def act_split(src_ap, hi_ap, lo_ap, waits):
            step("scalar", lambda e: e.activation(hi_ap, src_ap, AF.Copy),
                 waits=waits, incs=[(sS, 1)])
            t_hi = (sS, sS.bump())
            step("vector", lambda e: e.tensor_tensor(lo_ap, src_ap, hi_ap,
                                                     OP.subtract),
                 waits=[t_hi], incs=[(sV, 1)])
            return (sV, sV.bump())

        step("vector", lambda e: e.memset(ones[:, :], 1.0), incs=[(sV, 1)])
        t_ones = (sV, sV.bump())
        step("vector", lambda e: e.memset(carry[:, :], 0.0), incs=[(sV, 1)])
        t_car0 = (sV, sV.bump())
        step("vector", lambda e: e.memset(row(6), 1.0), incs=[(sV, 1)])
        t_onerow = (sV, sV.bump())

        for dst, src in ((bgs, bg2), (bvs, bv2), (bds, bd2), (b1s, b12),
                         (b2s, b22), (wmixs, wmix2), (wffns, wffn2), (selps, selp)):
            step("sync", lambda e, d=dst, s=src: e.dma_start(out=d[:, :], in_=s),
                 incs=[(sBIAS, 16)])
            sBIAS.bump(16)
        t_bias = (sBIAS, sBIAS.v)

        def mk_bq(e):
            CI("scalar", e.activation(bq16[:, :], b2s[:, :], AF.Copy))
            CW("scalar", e)
            return e.activation(bq32[:, :], bq16[:, :], AF.Copy)
        step("scalar", mk_bq, waits=[t_bias], incs=[(sS, 1)])
        t_bq = (sS, sS.bump())

        p1 = contextlib.ExitStack()
        sb1 = lambda name, shape, dt: p1.enter_context(nc.sbuf_tensor(name, shape, dt))
        xb0 = sb1("xb0", [128, KD, TC], F32)
        xb1 = sb1("xb1", [128, KD, TC], F32)
        xbufs = [xb0, xb1]
        xn_sb = sb1("xn_sb", [128, KD, TC], F32)
        xnh = sb1("xnh", [128, KD, TC], BF16)
        xnl = sb1("xnl", [128, KD, TC], BF16)
        xah = sb1("xah", [128, KD, TC], BF16)
        xal = sb1("xal", [128, KD, TC], BF16)
        xs_sb = sb1("xs_sb", [128, KD, TC], F32)
        a_sb = sb1("a_sb", [128, KD, TC], F32)
        hq_sb = sb1("hq_sb", [128, KH, TC], FP8)
        wq_sb = sb1("wq_sb", [128, KH, 128], FP8)
        wq2_sb = sb1("wq2_sb", [128, KH, 128], FP8)

        hs = {}

        def emit_load_x(c):
            h = hs.setdefault(c, {})
            sx = sX2[c % 2]
            xw = []
            if c >= 2:
                xw.append((sSTp[c % 2], sSTp[c % 2].v))
            step("sync", lambda e, a=c * TC, b=(c + 1) * TC, xb=xbufs[c % 2]:
                 e.dma_start(out=xb[:, :, :], in_=xTv[:, :, a:b]),
                 waits=xw, incs=[(sx, 16)])
            h["x"] = (sx, sx.bump(16))

        def emit_stats(c):
            h = hs[c]
            xb = xbufs[c % 2]
            pw = []
            if c >= 1:
                pw += [(sXN, sXN.v), hs[c - 1]["xnsp"], hs[c - 1]["xa"]]
            step("tensor", lambda e: [e.matmul(pb[4][0:1, 0:TC], ones[:, 0:1],
                                               xb[:, kc, :], start=(kc == 0),
                                               stop=(kc == KD - 1))
                                      for kc in range(KD)][-1],
                 waits=[h["x"], t_ones], incs=[(sT, 1)])
            t_sumx = (sT, sT.bump())
            t_sq = []
            for kc in range(KD):
                scr = xn_sb[:, kc % 2, :]
                w_ = [h["x"]] + pw if kc < 2 else [(sT, t_sq[kc - 2][1])]
                step("scalar", lambda e, kc=kc, scr=scr: e.activation(
                    scr, xb[:, kc, :], AF.Square), waits=w_, incs=[(sS, 1)])
                t_s = (sS, sS.bump())
                step("tensor", lambda e, kc=kc, scr=scr: e.matmul(
                    pb[5][0:1, 0:TC], ones[:, 0:1], scr,
                    start=(kc == 0), stop=(kc == KD - 1)),
                     waits=[t_s], incs=[(sT, 1)])
                t_sq.append((sT, sT.bump()))
            t_sumsq = t_sq[-1]

            ew = [t_sumx, t_sumsq]
            if c >= 1:
                ew += [hs[c - 1]["bc1"], hs[c - 1]["bc2"]]

            def evrows(e):
                CI("scalar", e.copy(row(0), pb[4][0:1, 0:TC]))
                return e.copy(row(1), pb[5][0:1, 0:TC])
            step("scalar", evrows, waits=ew, incs=[(sS, 1)])
            t_ev = (sS, sS.bump())

            def rows1(e):
                ops = [
                    lambda: e.tensor_scalar(row(0), row(0), INV_D, None, OP.mult),
                    lambda: e.tensor_scalar(row(1), row(1), INV_D, None, OP.mult),
                    lambda: e.tensor_tensor(row(2), row(0), row(0), OP.mult),
                    lambda: e.tensor_tensor(row(1), row(1), row(2), OP.subtract),
                    lambda: e.tensor_scalar(row(2), row(1), 1e-5, None, OP.add),
                    lambda: e.tensor_scalar(row(3), row(2), 1e-6, None, OP.mult),
                ]
                for op in ops:
                    CI("vector", op())
                    CW("vector", e)
                return e.tensor_tensor(row(3), row(1), row(3), OP.add)
            step("vector", rows1, waits=[t_ev, t_onerow], incs=[(sV, 1)])
            t_r1 = (sV, sV.bump())

            def sqrts(e):
                CI("scalar", e.activation(row(1), row(2), AF.Sqrt))
                return e.activation(row(4), row(3), AF.Sqrt)
            step("scalar", sqrts, waits=[t_r1], incs=[(sS, 1)])
            t_sqr = (sS, sS.bump())

            def rows2(e):
                ops = [
                    lambda: e.reciprocal(row(5), row(1)),
                    lambda: e.tensor_tensor(row(5), row(2), row(5), OP.mult),
                    lambda: e.tensor_tensor(row(5), row(1), row(5), OP.add),
                    lambda: e.tensor_scalar(row(5), row(5), 0.5, None, OP.mult),
                    lambda: e.reciprocal(row(1), row(5)),
                    lambda: e.reciprocal(row(5), row(4)),
                    lambda: e.tensor_tensor(row(5), row(3), row(5), OP.mult),
                    lambda: e.tensor_tensor(row(5), row(4), row(5), OP.add),
                    lambda: e.tensor_scalar(row(5), row(5), 0.5, None, OP.mult),
                    lambda: e.reciprocal(row(4), row(5)),
                    lambda: e.tensor_tensor(row(0), row(0), row(1), OP.mult),
                    lambda: e.tensor_tensor(row(5), row(2), row(1), OP.mult),
                ]
                for op in ops:
                    CI("vector", op())
                    CW("vector", e)
                return e.tensor_tensor(row(5), row(5), row(4), OP.mult)
            step("vector", rows2, waits=[t_sqr], incs=[(sV, 1)])
            h["rows"] = (sV, sV.bump())

        def emit_xn_a(c):
            h = hs[c]
            xb = xbufs[c % 2]
            bw = [h["rows"]]
            if c >= 1:
                bw += [hs[c - 1]["xa"]]
            step("tensor", lambda e: (e.matmul(pb[6][:, 0:TC], ones[0:1, :], row(1),
                                               start=True, stop=True),
                                      e.matmul(pb[7][:, 0:TC], ones[0:1, :], row(0),
                                               start=True, stop=True))[-1],
                 waits=bw, incs=[(sT, 1)])
            h["bc1"] = (sT, sT.bump())

            def mk_xn(e):
                for kc in range(KD):
                    inst = e.tensor_tensor(xn_sb[:, kc, :], xb[:, kc, :],
                                           pb[6][:, 0:TC], OP.mult)
                CI("vector", inst)
                CW("vector", e)
                for kc in range(KD):
                    inst = e.tensor_tensor(xn_sb[:, kc, :], xn_sb[:, kc, :],
                                           pb[7][:, 0:TC], OP.subtract)
                return inst
            step("vector", mk_xn, waits=[h["bc1"], h["x"]], incs=[(sV, 1)])
            h["xn"] = (sV, sV.bump())
            step("scalar", lambda e, a=c * TC, b=(c + 1) * TC: e.dma_start(
                out=xnv[:, :, a:b], in_=xn_sb[:, :, :]), waits=[h["xn"]],
                 incs=[(sXN, 16)])
            h["spill"] = (sXN, sXN.bump(16))

        def emit_xn_b(c):
            h = hs[c]
            step("tensor", lambda e: e.matmul(pb[6][:, 0:TC], ones[0:1, :], row(5),
                                              start=True, stop=True),
                 waits=[h["xn"]], incs=[(sT, 1)])
            h["bc2"] = (sT, sT.bump())

        def emit_splits(c):
            h = hs[c]
            t_prev = (sT, sT.v)
            t_xnl = None
            t_xal = None
            for kc in range(KD):
                xw = [h["xn"]]
                if t_xnl is None:
                    xw.append(t_prev)
                t_xnl = act_split(xn_sb[:, kc, :], xnh[:, kc, :], xnl[:, kc, :],
                                  xw + ([t_xnl] if t_xnl else []))
                scr = sg_sb if kc % 2 == 0 else tv_sb
                aw = [h["bc2"]]
                if t_xal is None:
                    aw.append(t_prev)
                step("vector", lambda e, kc=kc, scr=scr: e.scalar_tensor_tensor(
                    scr[:, :], xn_sb[:, kc, :], wmixs[:, kc:kc + 1],
                    pb[6][:, 0:TC], OP.mult, OP.mult),
                     waits=aw + ([t_xal] if t_xal else []), incs=[(sV, 1)])
                t_tmp = (sV, sV.bump())
                t_xal = act_split(scr[:, :], xah[:, kc, :], xal[:, kc, :], [t_tmp])
            h["xa"] = t_xal
            h["xnsp"] = t_xnl

        gvd_cfg = ((bgs, AF.Sigmoid, sg_sb), (bvs, AF.Tanh, tv_sb),
                   (bds, AF.Sigmoid, sl_sb))

        def add_chunk1(c):
            gst = {"evac": {0: None, 1: None, 2: None}, "xs": None, "ls": None}

            def gvd_consume(wh, wl, wtoks, i, c=c, gst=gst):
                m, pi = i // 3, i % 3
                bias_t, func, dst = gvd_cfg[pi]
                mw = wtoks + [hs[c]["xa"]]
                if gst["evac"][pi] is not None:
                    mw.append(gst["evac"][pi])
                step("tensor", mm3(pb[pi][:, 0:TC], wh, wl, xah, xal, KD),
                     waits=mw, incs=[(sT, 1)])
                t_mm = (sT, sT.bump())
                if pi < 2:
                    ew = [t_mm] + ([gst["xs"]] if gst["xs"] else [])
                else:
                    ew = [t_mm] + ([gst["ls"]] if gst["ls"] else [])
                step("scalar", lambda e, m=m, pi=pi, bias_t=bias_t, func=func,
                     dst=dst: e.activation(dst[:, :], pb[pi][:, 0:TC], func,
                                           bias=bias_t[:, m:m + 1]),
                     waits=ew, incs=[(sS, 1)])
                gst["evac"][pi] = (sS, sS.bump())
                if pi == 2:
                    spw = [(sSP, sSP.v)] if (m == 0 and c >= 1) else []
                    step("vector", lambda e, m=m: e.tensor_tensor(
                        xs_sb[:, m, :], sg_sb[:, :], tv_sb[:, :], OP.mult),
                         waits=[gst["evac"][0], gst["evac"][1]] + spw,
                         incs=[(sV, 1)])
                    gst["xs"] = (sV, sV.bump())
                    step("vector", lambda e, m=m: e.tensor_scalar(
                        a_sb[:, m, :], sl_sb[:, :], 0.998, 0.001, OP.mult, OP.add),
                         waits=[gst["evac"][2]] + spw, incs=[(sV, 1)])
                    t_a = (sV, sV.bump())
                    cwait = [t_car0] if c == 0 else []

                    def lscan(e, m=m):
                        CW("vector", e)
                        CI("vector", e.tensor_tensor_scan(
                            sl_sb[:, :], a_sb[:, m, :], xs_sb[:, m, :],
                            carry[:, m:m + 1], OP.mult, OP.add))
                        CW("vector", e)
                        return e.tensor_copy(carry[:, m:m + 1], sl_sb[:, TC - 1:TC])
                    step("vector", lscan,
                         waits=[gst["xs"], t_a, gst["evac"][2]] + cwait,
                         incs=[(sV, 1)])
                    gst["ls"] = (sV, sV.bump())
                return t_mm

            for i in range(3 * KD):
                src = (wgT, wvT, wdT)[i % 3]
                m = i // 3
                SU(src[m], KD,
                   (lambda i: lambda wh, wl, wt: gvd_consume(wh, wl, wt, i))(i))
                if c + 1 < NCH:
                    if i == 0:
                        HOOK(lambda c=c: (emit_load_x(c + 1), emit_stats(c + 1)))
                    elif i == 16:
                        HOOK(lambda c=c: emit_xn_a(c + 1))
                    elif i == 20:
                        HOOK(lambda c=c: emit_xn_b(c + 1))

            def end_gvd(c=c, gst=gst):
                t_gvd_all = (sV, sV.v)
                step("scalar", lambda e, a=c * TC, b=(c + 1) * TC: e.dma_start(
                    out=xsv[:, :, a:b], in_=xs_sb[:, :, :]), waits=[t_gvd_all],
                     incs=[(sSP, 16)])
                step("scalar", lambda e, a=c * TC, b=(c + 1) * TC: e.dma_start(
                    out=av[:, :, a:b], in_=a_sb[:, :, :]), waits=[t_gvd_all],
                     incs=[(sSP, 16)])
                sSP.bump(32)
                if c == NCH - 1:
                    step("gpsimd", lambda e: e.dma_start(out=cc_in.ap(),
                                                         in_=carry[:, :]),
                         waits=[gst["ls"]], incs=[(sGD, 16)])
                    t_ccin = (sGD, sGD.bump(16))
                    step("gpsimd", lambda e: e.collective_compute(
                        "AllGather", OP.bypass,
                        replica_groups=[[0, 1], [2, 3], [4, 5], [6, 7]],
                        ins=[cc_in.ap()], outs=[cc_out.ap()]),
                         waits=[t_ccin], incs=[(sCC, 1)])
                    t_cc = (sCC, sCC.bump())
                    step("gpsimd", lambda e: e.dma_start(
                        out=hprev[:, :], in_=cc_out.ap()[0, :, :]), waits=[t_cc],
                         incs=[(sGD, 16)])
                    t_hp = (sGD, sGD.bump(16))
                    step("vector", lambda e: e.tensor_scalar(
                        carry2[:, :], hprev[:, :], selps[:, 0:1], None, OP.mult),
                         waits=[t_hp], incs=[(sV, 1)])
                    hs["c2"] = (sV, sV.bump())
            HOOK(end_gvd)

            w1st = {"h": None}

            def w1_consume(wh, wl, wtoks, m, c=c, w1st=w1st):
                mw = wtoks + [hs[c]["xnsp"]]
                if w1st["h"] is not None:
                    mw.append(w1st["h"])
                step("tensor", mm3(pb[3][:, 0:TC], wh, wl, xnh, xnl, KD),
                     waits=mw, incs=[(sT, 1)])
                t_mm = (sT, sT.bump())
                step("scalar", lambda e, m=m: e.activation(
                    hq_sb[:, m, :], pb[3][:, 0:TC], AF.Relu, bias=b1s[:, m:m + 1]),
                     waits=[t_mm], incs=[(sS, 1)])
                w1st["h"] = (sS, sS.bump())
                return t_mm

            for m in range(KH):
                SU(w1T[m], KD,
                   (lambda m: lambda wh, wl, wt: w1_consume(wh, wl, wt, m))(m))
            if c + 1 < NCH:
                HOOK(lambda c=c: emit_splits(c + 1))

            w2st = {"casts": [], "f": None, "part": None, "mm": None}

            def w2_tail(mo, t_mm, c=c, w2st=w2st):
                fw = [t_mm, t_bq]
                if w2st["part"] is not None:
                    fw.append(w2st["part"])
                step("scalar", lambda e, mo=mo: e.activation(
                    f16_sb[:, :], pb[3][:, 0:TC], AF.Identity,
                    bias=bq32[:, mo:mo + 1]), waits=fw, incs=[(sS, 1)])
                w2st["f"] = (sS, sS.bump())

                def mkpart(e, mo=mo, xb=xbufs[c % 2]):
                    CI("vector", e.tensor_copy(tmpf[:, :], f16_sb[:, :]))
                    CW("vector", e)
                    return e.tensor_tensor(xb[:, mo, :], xb[:, mo, :],
                                           tmpf[:, :], OP.add)
                step("vector", mkpart, waits=[w2st["f"]], incs=[(sV, 1)])
                w2st["part"] = (sV, sV.bump())

            def w2_consume(stga, t_ld, q, mo, c=c, w2st=w2st, w1st=w1st):
                vwt = [t_ld]
                if w2st["mm"] is not None:
                    vwt.append(w2st["mm"])
                if q == 0:
                    vwt.append((sWQ, sWQ.v))
                step("vector", lambda e, stga=stga, q=q: e.tensor_copy(
                    wq_sb[:, q * 8:(q + 1) * 8, :], stga[:, 0:8, :]),
                     waits=vwt, incs=[(sV, 1)])
                t_cast = (sV, sV.bump())
                w2st["casts"].append(t_cast)
                if q < 3:
                    return t_cast
                casts = list(w2st["casts"])
                w2st["casts"] = []
                step("scalar", lambda e, mo=mo: e.dma_start(
                    out=w2q8[mo], in_=wq_sb[:, :, :]), waits=casts,
                     incs=[(sWQ, 16)])
                sWQ.bump(16)
                mw = casts + [w1st["h"]]
                if w2st["f"] is not None:
                    mw.append(w2st["f"])

                def mmw2(e):
                    for kh in range(KH):
                        inst = e.matmul(pb[3][:, 0:TC], wq_sb[:, kh, :],
                                        hq_sb[:, kh, :], start=(kh == 0),
                                        stop=(kh == KH - 1))
                    return inst
                step("tensor", mmw2, waits=mw, incs=[(sT, 1)])
                w2st["mm"] = (sT, sT.bump())
                w2_tail(mo, w2st["mm"])
                return w2st["mm"]

            if c == 0:
                for mo in range(KD):
                    for q in range(4):
                        RU(w2T[mo][:, q * 8:(q + 1) * 8, :], 8,
                           (lambda q, mo: lambda stga, t_ld:
                            w2_consume(stga, t_ld, q, mo))(q, mo))

            def w2_cached(c=c, w2st=w2st, w1st=w1st):
                mms = {}
                for mo in range(KD):
                    wq = wq_sb if mo % 2 == 0 else wq2_sb
                    sq = sQ2[mo % 2]
                    lw = [(sWQ, sWQ.v)]
                    if mo >= 2:
                        lw.append(mms[mo - 2])
                    else:
                        lw.append(hs[c - 1]["w2mm"][mo - 2 + KD])
                    step("sync", lambda e, mo=mo, wq=wq, sq=sq: e.dma_start(
                        out=wq[:, :, :], in_=w2q8[mo]).then_inc(sq.h, 16),
                         waits=lw)
                    t_l = (sq, sq.bump(16))
                    mw = [t_l, w1st["h"]]
                    if w2st["f"] is not None:
                        mw.append(w2st["f"])

                    def mmw2(e, wq=wq):
                        for kh in range(KH):
                            inst = e.matmul(pb[3][:, 0:TC], wq[:, kh, :],
                                            hq_sb[:, kh, :], start=(kh == 0),
                                            stop=(kh == KH - 1))
                        return inst
                    step("tensor", mmw2, waits=mw, incs=[(sT, 1)])
                    w2st["mm"] = (sT, sT.bump())
                    mms[mo] = w2st["mm"]
                    w2_tail(mo, w2st["mm"])
                hs[c]["w2mm"] = mms

            def end_w2(c=c, w2st=w2st):
                if c == 0:
                    hs[0]["w2mm"] = {6: w2st["mm"], 7: w2st["mm"]}
                if c >= 1:
                    w2_cached()
                sst = sSTp[c % 2]
                step("scalar", lambda e, a=c * TC, b=(c + 1) * TC,
                     xb=xbufs[c % 2], sst=sst:
                     e.dma_start(out=outv[:, :, a:b], in_=xb[:, :, :]),
                     waits=[w2st["part"]], incs=[(sst, 16)])
                hs[c]["pstore"] = (sst, sst.bump(16))
            HOOK(end_w2)

        emit_load_x(0)
        emit_stats(0)
        emit_xn_a(0)
        emit_xn_b(0)
        emit_splits(0)
        for c in range(NCH):
            add_chunk1(c)
        run_stream()

        emit_block()
        p1.close()

        p3 = contextlib.ExitStack()
        sb3 = lambda name, shape, dt: p3.enter_context(nc.sbuf_tensor(name, shape, dt))
        q_xn = sb3("q_xn", [128, KD, TC], F32)
        q_xs = sb3("q_xs", [128, KD, TC], F32)
        q_a = sb3("q_a", [128, KD, TC], F32)
        q_pa = sb3("q_pa", [128, KD, TC], F32)
        q_s = sb3("q_s", [128, KD, TC], F32)
        xbh = sb3("xbh", [128, KD, TC], BF16)
        xbl = sb3("xbl", [128, KD, TC], BF16)
        pmh = sb3("pmh", [128, KF, TC], BF16)
        pml = sb3("pml", [128, KF, TC], BF16)

        h3 = {}

        def p3_boundary():
            return [(sXN, sXN.v), (sSP, sSP.v), (sT, sT.v), (sS, sS.v),
                    (sV, sV.v), (sSTp[0], sSTp[0].v), (sSTp[1], sSTp[1].v)]

        def p3_load_xsa(c):
            h = h3.setdefault(c, {})
            lw = p3_boundary() if c == 0 else [h3[c - 1]["scan"]]
            step("sync", lambda e, a=c * TC, b=(c + 1) * TC: e.dma_start(
                out=q_xs[:, :, :], in_=xsv[:, :, a:b]), waits=lw, incs=[(sP3A, 16)])
            sP3A.bump(16)
            step("sync", lambda e, a=c * TC, b=(c + 1) * TC: e.dma_start(
                out=q_a[:, :, :], in_=av[:, :, a:b]), incs=[(sP3A, 16)])
            sP3A.bump(16)
            h["lxa"] = (sP3A, sP3A.v)

        def p3_scans(c):
            h = h3[c]
            sw = [h["lxa"]]
            if c == 0:
                sw.append(hs["c2"])
            else:
                sw.append(h3[c - 1]["part2"])

            def scans(e):
                for m in range(KD):
                    CW("vector", e)
                    CI("vector", e.tensor_tensor_scan(
                        q_s[:, m, :], q_a[:, m, :], q_xs[:, m, :],
                        carry2[:, m:m + 1], OP.mult, OP.add))
                    CW("vector", e)
                    inst = e.tensor_copy(carry2[:, m:m + 1], q_s[:, m, TC - 1:TC])
                    if m < KD - 1:
                        CI("vector", inst)
                return inst
            step("vector", scans, waits=sw, incs=[(sV, 1)])
            h["scan"] = (sV, sV.bump())

        def p3_load_xn(c):
            h = h3[c]
            lw = p3_boundary() if c == 0 else [h3[c - 1]["part1"]]
            step("sync", lambda e, a=c * TC, b=(c + 1) * TC: e.dma_start(
                out=q_xn[:, :, :], in_=xnv[:, :, a:b]), waits=lw, incs=[(sP3X, 16)])
            h["lxn"] = (sP3X, sP3X.bump(16))

        def p3_part1(c):
            h = h3[c]
            step("vector", lambda e: e.tensor_tensor(
                q_s[:, :, :], q_xn[:, :, :], q_s[:, :, :], OP.add),
                 waits=[h["lxn"], h["scan"]], incs=[(sV, 1)])
            h["part1"] = (sV, sV.bump())

        def p3_load_pa(c):
            h = h3.setdefault(c, {})
            lw = [(sSTp[0], sSTp[0].v), (sSTp[1], sSTp[1].v)]
            step("sync", lambda e, a=c * TC, b=(c + 1) * TC: e.dma_start(
                out=q_pa[:, :, :], in_=outv[:, :, a:b]), waits=lw, incs=[(sP3P, 16)])
            h["lpa"] = (sP3P, sP3P.bump(16))

        def p3_part2(c):
            h = h3[c]
            step("vector", lambda e: e.tensor_tensor(
                q_pa[:, :, :], q_pa[:, :, :], q_s[:, :, :], OP.add),
                 waits=[h["lpa"], h["part1"]], incs=[(sV, 1)])
            h["part2"] = (sV, sV.bump())

        def p3_stats(c):
            h = h3[c]
            t_sq3 = []
            fw1 = [h3[c - 1]["fin1"]] if c >= 1 else []
            for kc in range(KD):
                scr = sl_sb if kc % 2 == 0 else sq2_sb
                w_ = [h["part1"]] + fw1 if kc < 2 else [(sT, t_sq3[kc - 2][1])]
                step("scalar", lambda e, kc=kc, scr=scr: e.activation(
                    scr[:, :], q_s[:, kc, :], AF.Square), waits=w_, incs=[(sS, 1)])
                t_s3 = (sS, sS.bump())
                step("tensor", lambda e, kc=kc, scr=scr: e.matmul(
                    pb[2][0:1, 0:TC], ones[:, 0:1], scr[:, :],
                    start=(kc == 0), stop=(kc == KD - 1)), waits=[t_s3],
                     incs=[(sT, 1)])
                t_sq3.append((sT, sT.bump()))
            step("scalar", lambda e: e.copy(row(0), pb[2][0:1, 0:TC]),
                 waits=[t_sq3[-1]], incs=[(sS, 1)])
            t_ev3 = (sS, sS.bump())
            step("vector", lambda e: e.tensor_scalar(
                row(1), row(0), INV_D, 1e-6, OP.mult, OP.add),
                 waits=[t_ev3], incs=[(sV, 1)])
            t_u3 = (sV, sV.bump())
            step("scalar", lambda e: e.activation(row(2), row(1), AF.Sqrt),
                 waits=[t_u3], incs=[(sS, 1)])
            t_y3 = (sS, sS.bump())

            def rows3(e):
                ops = [
                    lambda: e.reciprocal(row(3), row(2)),
                    lambda: e.tensor_tensor(row(3), row(1), row(3), OP.mult),
                    lambda: e.tensor_tensor(row(3), row(2), row(3), OP.add),
                    lambda: e.tensor_scalar(row(3), row(3), 0.5, None, OP.mult),
                ]
                for op in ops:
                    CI("vector", op())
                    CW("vector", e)
                return e.reciprocal(row(4), row(3))
            step("vector", rows3, waits=[t_y3], incs=[(sV, 1)])
            t_r3 = (sV, sV.bump())
            step("tensor", lambda e: e.matmul(pb[3][:, 0:TC], ones[0:1, :],
                                              row(4), start=True, stop=True),
                 waits=[t_r3], incs=[(sT, 1)])
            h["bc3"] = (sT, sT.bump())

        def p3_xbsplit(c):
            h = h3[c]
            t_xbl = None
            for kc in range(KD):
                scr = sg_sb if kc % 2 == 0 else tv_sb
                aw = [h["bc3"]]
                if t_xbl is None:
                    aw.append((sS, sS.v))
                step("vector", lambda e, kc=kc, scr=scr: e.scalar_tensor_tensor(
                    scr[:, :], q_s[:, kc, :], wffns[:, kc:kc + 1],
                    pb[3][:, 0:TC], OP.mult, OP.mult),
                     waits=aw + ([t_xbl] if t_xbl else []), incs=[(sV, 1)])
                t_tmp = (sV, sV.bump())
                t_xbl = act_split(scr[:, :], xbh[:, kc, :], xbl[:, kc, :], [t_tmp])
            h["xb"] = t_xbl

        def add_chunk3(c):
            g3 = {"sil": None, "pm": None, "wo": None, "pml": None, "sw2": None}

            def g3_consume(wh, wl, wtoks, i, c=c, g3=g3):
                m, r = i // 3, i % 3
                if r == 0:
                    mw = wtoks + [h3[c]["xb"]]
                    if g3["sil"] is not None:
                        mw.append(g3["sil"])
                    if m == 0 and c >= 1:
                        mw.append(h3[c - 1]["fin2"])
                    step("tensor", mm3(pb[0][:, 0:TC], wh, wl, xbh, xbl, KD),
                         waits=mw, incs=[(sT, 1)])
                    t_g = (sT, sT.bump())
                    sw2 = [t_g] + ([g3["pm"]] if g3["pm"] else [])
                    step("scalar", lambda e: e.activation(
                        sg_sb[:, :], pb[0][:, 0:TC], AF.Silu),
                         waits=sw2, incs=[(sS, 1)])
                    g3["sil"] = (sS, sS.bump())
                    return t_g
                if r == 1:
                    mw2 = wtoks + [h3[c]["xb"]]
                    if g3["pm"] is not None:
                        mw2.append(g3["pm"])
                    if m == 0 and c >= 1:
                        mw2.append(h3[c - 1]["fin2"])
                    step("tensor", mm3(pb[1][:, 0:TC], wh, wl, xbh, xbl, KD),
                         waits=mw2, incs=[(sT, 1)])
                    t_u = (sT, sT.bump())
                    step("vector", lambda e: e.tensor_tensor(
                        tmpf[:, :], sg_sb[:, :], pb[1][:, 0:TC], OP.mult),
                         waits=[g3["sil"], t_u] + ([g3["pml"]] if g3["pml"] else []),
                         incs=[(sV, 1)])
                    g3["pm"] = (sV, sV.bump())
                    g3["pml"] = act_split(tmpf[:, :], pmh[:, m, :], pml[:, m, :],
                                          [g3["pm"]])
                    return t_u

                def mmwo(e, wh=wh, wl=wl, m=m):
                    for mo in range(6):
                        e.matmul(pb[2 + mo][:, 0:TC], wh[:, mo, :], pmh[:, m, :],
                                 start=(m == 0), stop=False)
                        e.matmul(pb[2 + mo][:, 0:TC], wh[:, mo, :], pml[:, m, :],
                                 start=False, stop=False)
                        inst = e.matmul(pb[2 + mo][:, 0:TC], wl[:, mo, :],
                                        pmh[:, m, :], start=False,
                                        stop=(m == KF - 1))
                    return inst
                wo_w = wtoks + [g3["pml"]]
                if m == 0 and c >= 1:
                    wo_w.append(h3[c - 1]["fin1"])
                step("tensor", mmwo, waits=wo_w, incs=[(sT, 1)])
                g3["wo"] = (sT, sT.bump())
                return g3["wo"]

            for i in range(3 * KF):
                m, r = i // 3, i % 3
                if r == 0:
                    src, ns = wgateT[m], KD
                elif r == 1:
                    src, ns = wupT[m], KD
                else:
                    src = woutv[m, :, 0:768].rearrange("p (s q) -> p s q", q=128)
                    ns = 6
                SU(src, ns,
                   (lambda i: lambda wh, wl, wt: g3_consume(wh, wl, wt, i))(i))
                if c + 1 < NCH:
                    if i == 2:
                        HOOK(lambda c=c: p3_load_xsa(c + 1))
                    elif i == 6:
                        HOOK(lambda c=c: p3_scans(c + 1))
                    elif i == 40:
                        HOOK(lambda c=c: p3_load_xn(c + 1))
                    elif i == 44:
                        HOOK(lambda c=c: p3_part1(c + 1))

            def fin1(c=c, g3=g3):
                def finev1(e):
                    for mo in range(6):
                        inst = e.tensor_tensor(q_pa[:, mo, :], q_pa[:, mo, :],
                                               pb[2 + mo][:, 0:TC], OP.add)
                    return inst
                step("vector", finev1, waits=[g3["wo"], h3[c]["part2"]],
                     incs=[(sV, 1)])
                h3[c]["fin1"] = (sV, sV.bump())
                if c + 1 < NCH:
                    p3_stats(c + 1)
            HOOK(fin1)

            def sw2_consume(wh, wl, wtoks, m, c=c, g3=g3):
                mw3 = list(wtoks)
                if m == 0:
                    mw3 += [g3["pm"], g3["sil"], g3["pml"], h3[c]["fin1"]]

                def mm2(e, wh=wh, wl=wl, m=m):
                    for mo in range(2):
                        e.matmul(pb[mo][:, 0:TC], wh[:, mo, :], pmh[:, m, :],
                                 start=(m == 0), stop=False)
                        e.matmul(pb[mo][:, 0:TC], wh[:, mo, :], pml[:, m, :],
                                 start=False, stop=False)
                        inst = e.matmul(pb[mo][:, 0:TC], wl[:, mo, :],
                                        pmh[:, m, :], start=False,
                                        stop=(m == KF - 1))
                    return inst
                step("tensor", mm2, waits=mw3, incs=[(sT, 1)])
                g3["sw2"] = (sT, sT.bump())
                return g3["sw2"]

            for m in range(KF):
                SU(woutv[m, :, 768:1024].rearrange("p (s q) -> p s q", q=128), 2,
                   (lambda m: lambda wh, wl, wt: sw2_consume(wh, wl, wt, m))(m))
                if m == 2 and c + 1 < NCH:
                    HOOK(lambda c=c: p3_xbsplit(c + 1))

            def fin2(c=c, g3=g3):
                def finev2(e):
                    CI("vector", e.tensor_tensor(q_pa[:, 6, :], q_pa[:, 6, :],
                                                 pb[0][:, 0:TC], OP.add))
                    return e.tensor_tensor(q_pa[:, 7, :], q_pa[:, 7, :],
                                           pb[1][:, 0:TC], OP.add)
                step("vector", finev2, waits=[g3["sw2"]], incs=[(sV, 1)])
                h3[c]["fin2"] = (sV, sV.bump())
                sst = sSTp[c % 2]
                step("scalar", lambda e, a=c * TC, b=(c + 1) * TC, sst=sst:
                     e.dma_start(out=outv[:, :, a:b], in_=q_pa[:, :, :]),
                     waits=[h3[c]["fin1"], h3[c]["fin2"]], incs=[(sst, 16)])
                sst.bump(16)
                if c + 1 < NCH:
                    p3_load_pa(c + 1)
                    p3_part2(c + 1)
            HOOK(fin2)

        p3_load_xsa(0)
        p3_scans(0)
        p3_load_xn(0)
        p3_part1(0)
        p3_load_pa(0)
        p3_part2(0)
        p3_stats(0)
        p3_xbsplit(0)
        for c in range(NCH):
            add_chunk3(c)
        run_stream()

        step("gpsimd", None, waits=[(sSTp[0], sSTp[0].v), (sSTp[1], sSTp[1].v)])
        step("sync", None, waits=[(sSTp[0], sSTp[0].v), (sSTp[1], sSTp[1].v)])

        emit_block()
        p3.close()

    return nc


def kernel(L0_attn_x, L0_attn_rms_mix_w, L0_attn_rms_ffn_w, L0_attn_Wg, L0_attn_bg,
           L0_attn_Wv, L0_attn_bv, L0_attn_Wd, L0_attn_bd, L0_attn_W_gate,
           L0_attn_W_up, L0_attn_W_out, L0_ffn_w1, L0_ffn_b1, L0_ffn_w2, L0_ffn_b2):
    x = np.asarray(L0_attn_x, dtype=np.float32)

    def t(a):
        return np.ascontiguousarray(np.asarray(a, dtype=np.float32).T)

    def tw(a):
        w = np.asarray(a, dtype=np.float32)
        M, K = w.shape
        return np.ascontiguousarray(
            w.reshape(M // 128, 128, K // 128, 128).transpose(0, 3, 2, 1))

    def b2(a, k):
        return np.ascontiguousarray(np.asarray(a, dtype=np.float32).reshape(k, 128).T)

    shared = {
        "wgT": tw(L0_attn_Wg), "wvT": tw(L0_attn_Wv), "wdT": tw(L0_attn_Wd),
        "wgateT": tw(L0_attn_W_gate), "wupT": tw(L0_attn_W_up),
        "woutT": t(L0_attn_W_out),
        "w1T": tw(L0_ffn_w1), "w2T": tw(L0_ffn_w2),
        "bg2": b2(L0_attn_bg, KD), "bv2": b2(L0_attn_bv, KD), "bd2": b2(L0_attn_bd, KD),
        "b12": b2(L0_ffn_b1, KH), "b22": b2(L0_ffn_b2, KD),
        "wmix2": b2(L0_attn_rms_mix_w, KD), "wffn2": b2(L0_attn_rms_ffn_w, KD),
    }
    in_maps = []
    for core in range(8):
        b, h = core // 2, core % 2
        xc = np.ascontiguousarray(x[b, h * T:(h + 1) * T, :].T)
        d = dict(shared)
        d["xT"] = xc
        d["selp"] = np.full((128, 1), float(h), dtype=np.float32)
        in_maps.append(d)

    nc = build_nc()
    res = run_bass_kernel_spmd(nc, in_maps, core_ids=list(range(8)))
    out = np.empty((B, S, D), dtype=np.float32)
    for core in range(8):
        b, h = core // 2, core % 2
        out[b, h * T:(h + 1) * T, :] = res.results[core]["outT"].T
    return out


# version 15
# speedup vs baseline: 1.0094x; 1.0094x over previous
import contextlib

import numpy as np

import concourse.bass as bass
import concourse.mybir as mybir
from concourse.bass_utils import run_bass_kernel_spmd

F32 = mybir.dt.float32
BF16 = mybir.dt.bfloat16
FP8 = mybir.dt.float8e4
F16 = mybir.dt.float16
AF = mybir.ActivationFunctionType
OP = mybir.AluOpType

B, S, D, F, H = 4, 4096, 1024, 3072, 4096
T = 2048
KD, KF, KH = D // 128, F // 128, H // 128
TC, NCH = 512, 4
INV_D = 1.0 / D
PIPE_SLOTS = 8
NPIPE = 4
DEPTH = 4


class Sem:
    def __init__(self, nc, st, name):
        self.h = st.enter_context(nc.semaphore(name))
        self.v = 0

    def bump(self, n=1):
        self.v += n
        return self.v


def build_nc():
    nc = bass.Bass("TRN2", target_bir_lowering=False, debug=False)

    xT = nc.dram_tensor("xT", [D, T], F32, kind="ExternalInput").ap()
    wgT = nc.dram_tensor("wgT", [KD, 128, KD, 128], F32, kind="ExternalInput").ap()
    wvT = nc.dram_tensor("wvT", [KD, 128, KD, 128], F32, kind="ExternalInput").ap()
    wdT = nc.dram_tensor("wdT", [KD, 128, KD, 128], F32, kind="ExternalInput").ap()
    wgateT = nc.dram_tensor("wgateT", [KF, 128, KD, 128], F32, kind="ExternalInput").ap()
    wupT = nc.dram_tensor("wupT", [KF, 128, KD, 128], F32, kind="ExternalInput").ap()
    woutT = nc.dram_tensor("woutT", [F, D], F32, kind="ExternalInput").ap()
    w1T = nc.dram_tensor("w1T", [KH, 128, KD, 128], F32, kind="ExternalInput").ap()
    w2T = nc.dram_tensor("w2T", [KD, 128, KH, 128], F32, kind="ExternalInput").ap()
    w2q8 = nc.dram_tensor("w2q8", [KD, 128, KH, 128], FP8).ap()
    bg2 = nc.dram_tensor("bg2", [128, KD], F32, kind="ExternalInput").ap()
    bv2 = nc.dram_tensor("bv2", [128, KD], F32, kind="ExternalInput").ap()
    bd2 = nc.dram_tensor("bd2", [128, KD], F32, kind="ExternalInput").ap()
    b12 = nc.dram_tensor("b12", [128, KH], F32, kind="ExternalInput").ap()
    b22 = nc.dram_tensor("b22", [128, KD], F32, kind="ExternalInput").ap()
    wmix2 = nc.dram_tensor("wmix2", [128, KD], F32, kind="ExternalInput").ap()
    wffn2 = nc.dram_tensor("wffn2", [128, KD], F32, kind="ExternalInput").ap()
    selp = nc.dram_tensor("selp", [128, 1], F32, kind="ExternalInput").ap()

    outT = nc.dram_tensor("outT", [D, T], F32, kind="ExternalOutput").ap()

    xn_h = nc.dram_tensor("xn_h", [D, T], F32).ap()
    xs_h = nc.dram_tensor("xs_h", [D, T], F32).ap()
    a_h = nc.dram_tensor("a_h", [D, T], F32).ap()
    cc_in = nc.dram_tensor("cc_in", [128, KD], F32)
    cc_out = nc.dram_tensor("cc_out", [2, 128, KD], F32)

    def fm(ap_):
        return ap_.rearrange("(k p) t -> p k t", p=128)

    xTv, xnv, xsv, av, outv = fm(xT), fm(xn_h), fm(xs_h), fm(a_h), fm(outT)
    woutv = woutT.rearrange("(k p) m -> k p m", p=128)

    P = []

    def step(eng, body=None, waits=(), incs=()):
        waits = [(s.h, v) for (s, v) in waits]
        incs_ = [(s.h, n) for (s, n) in incs]

        def fn(e):
            for h, v in waits:
                e.wait_ge(h, v)
            inst = body(e) if body is not None else None
            if incs_:
                if inst is None:
                    assert body is None
                    inst = e.nop()
                for h, n in incs_:
                    inst.then_inc(h, n)

        P.append((eng, fn))

    def emit_block():
        with nc.Block() as blk:
            for eng_name in ("sync", "scalar", "vector", "tensor", "gpsimd"):
                def make(eng_name):
                    def emit(e):
                        for en, fn in P:
                            if en == eng_name:
                                fn(e)
                    return emit
                getattr(blk, eng_name)(make(eng_name))
        P.clear()

    with contextlib.ExitStack() as st:
        sb = lambda name, shape, dt: st.enter_context(nc.sbuf_tensor(name, shape, dt))

        ones = sb("ones", [128, 128], F32)
        bgs = sb("bgs", [128, KD], F32)
        bvs = sb("bvs", [128, KD], F32)
        bds = sb("bds", [128, KD], F32)
        b1s = sb("b1s", [128, KH], F32)
        b2s = sb("b2s", [128, KD], F32)
        bq16 = sb("bq16", [128, KD], F16)
        bq32 = sb("bq32", [128, KD], F32)
        wmixs = sb("wmixs", [128, KD], F32)
        wffns = sb("wffns", [128, KD], F32)
        selps = sb("selps", [128, 1], F32)
        carry = sb("carry", [128, KD], F32)
        carry2 = sb("carry2", [128, KD], F32)
        hprev = sb("hprev", [128, KD], F32)
        sg_sb = sb("sg_sb", [128, TC], F32)
        tv_sb = sb("tv_sb", [128, TC], F32)
        sl_sb = sb("sl_sb", [128, TC], F32)
        sq2_sb = sb("sq2_sb", [128, TC], F32)
        tmpf = sb("tmpf", [128, TC], F32)
        f16_sb = sb("f16_sb", [128, TC], F16)
        stat = sb("stat", [1, 7 * TC], F32)
        stg = [sb(f"stg{i}", [128, PIPE_SLOTS, 128], F32) for i in range(NPIPE)]
        wsh = [sb(f"wsh{i}", [128, PIPE_SLOTS, 128], BF16) for i in range(NPIPE)]
        wsl = [sb(f"wsl{i}", [128, PIPE_SLOTS, 128], BF16) for i in range(NPIPE)]

        pb = [st.enter_context(nc.psum_tensor(f"pb{i}", [128, 512], F32))
              for i in range(8)]

        sBIAS = Sem(nc, st, "sBIAS")
        sX2 = [Sem(nc, st, "sXa"), Sem(nc, st, "sXb")]
        sWp = [Sem(nc, st, f"sW{i}") for i in range(NPIPE)]
        sP3A = Sem(nc, st, "sP3A")
        sP3X = Sem(nc, st, "sP3X")
        sP3P = Sem(nc, st, "sP3P")
        sSTp = [Sem(nc, st, "sSTa"), Sem(nc, st, "sSTb")]
        sXN = Sem(nc, st, "sXN")
        sSP = Sem(nc, st, "sSP")
        sGD = Sem(nc, st, "sGD")
        sWQ = Sem(nc, st, "sWQ")
        sQ2 = [Sem(nc, st, "sQ2a"), Sem(nc, st, "sQ2b")]
        sT = Sem(nc, st, "sT")
        sV = Sem(nc, st, "sV")
        sS = Sem(nc, st, "sS")
        sG = Sem(nc, st, "sG")
        sCC = Sem(nc, st, "sCC")
        chain = {n: Sem(nc, st, f"ch_{n[:3]}") for n in ("vector", "scalar", "gpsimd")}

        def CI(en, inst):
            s = chain[en]
            s.v += 1
            inst.then_inc(s.h, 1)
            return inst

        def CW(en, e):
            s = chain[en]
            e.wait_ge(s.h, s.v)

        def row(i, n=TC):
            return stat[0:1, i * TC: i * TC + n]

        pipe_state = {i: {"free": []} for i in range(NPIPE)}
        pipe_par = [0]

        def pipe_load(src_ap, nslots):
            par = pipe_par[0]
            pipe_par[0] = (par + 1) % NPIPE
            ps = pipe_state[par]
            sw = sWp[par]
            stga, wha, wla = stg[par], wsh[par], wsl[par]
            step("sync", lambda e: e.dma_start(
                out=stga[:, 0:nslots, :], in_=src_ap).then_inc(sw.h, 16),
                 waits=list(ps["free"]))
            t_ld = (sw, sw.bump(16))
            step("scalar", lambda e: e.activation(
                wha[:, 0:nslots, :], stga[:, 0:nslots, :], AF.Copy),
                 waits=[t_ld], incs=[(sS, 1)])
            t_hi = (sS, sS.bump())
            step("vector", lambda e: e.tensor_tensor(
                wla[:, 0:nslots, :], stga[:, 0:nslots, :], wha[:, 0:nslots, :],
                OP.subtract), waits=[t_hi], incs=[(sV, 1)])
            t_lo = (sV, sV.bump())
            ps["free"] = [t_lo]
            return ("split", wha, wla, [t_hi, t_lo], par)

        def pipe_load_raw(src_ap, nslots):
            par = pipe_par[0]
            pipe_par[0] = (par + 1) % NPIPE
            ps = pipe_state[par]
            sw = sWp[par]
            stga = stg[par]
            step("sync", lambda e: e.dma_start(
                out=stga[:, 0:nslots, :], in_=src_ap).then_inc(sw.h, 16),
                 waits=list(ps["free"]))
            t_ld = (sw, sw.bump(16))
            ps["free"] = []
            return ("raw", stga, t_ld, None, par)

        stream = []
        hooks_after = {}

        def SU(src, nslots, consume):
            stream.append((False, src, nslots, consume))
            return len(stream) - 1

        def RU(src, nslots, consume):
            stream.append((True, src, nslots, consume))
            return len(stream) - 1

        def HOOK(fn):
            hooks_after.setdefault(len(stream) - 1, []).append(fn)

        def run_stream():
            n = len(stream)
            handles = {}

            def load(j):
                is_raw, src, ns, _ = stream[j]
                return pipe_load_raw(src, ns) if is_raw else pipe_load(src, ns)

            for j in range(min(DEPTH, n)):
                handles[j] = load(j)
            for i in range(n):
                kind, a1, a2, a3, par = handles.pop(i)
                consume = stream[i][3]
                if kind == "split":
                    tok = consume(a1, a2, a3)
                else:
                    tok = consume(a1, a2)
                pipe_state[par]["free"].append(tok)
                for fn in hooks_after.get(i, []):
                    fn()
                j = i + DEPTH
                if j < n:
                    handles[j] = load(j)
            stream.clear()
            hooks_after.clear()

        def mm3(ps_ap, wh, wl, xh, xl, nk):
            def body(e):
                for kc in range(nk):
                    e.matmul(ps_ap, wh[:, kc, :], xh[:, kc, :],
                             start=(kc == 0), stop=False)
                for kc in range(nk):
                    e.matmul(ps_ap, wh[:, kc, :], xl[:, kc, :],
                             start=False, stop=False)
                for kc in range(nk):
                    inst = e.matmul(ps_ap, wl[:, kc, :], xh[:, kc, :],
                                    start=False, stop=(kc == nk - 1))
                return inst
            return body

        def act_split(src_ap, hi_ap, lo_ap, waits):
            step("scalar", lambda e: e.activation(hi_ap, src_ap, AF.Copy),
                 waits=waits, incs=[(sS, 1)])
            t_hi = (sS, sS.bump())
            step("vector", lambda e: e.tensor_tensor(lo_ap, src_ap, hi_ap,
                                                     OP.subtract),
                 waits=[t_hi], incs=[(sV, 1)])
            return (sV, sV.bump())

        step("vector", lambda e: e.memset(ones[:, :], 1.0), incs=[(sV, 1)])
        t_ones = (sV, sV.bump())
        step("vector", lambda e: e.memset(carry[:, :], 0.0), incs=[(sV, 1)])
        t_car0 = (sV, sV.bump())
        step("vector", lambda e: e.memset(row(6), 1.0), incs=[(sV, 1)])
        t_onerow = (sV, sV.bump())

        for dst, src in ((bgs, bg2), (bvs, bv2), (bds, bd2), (b1s, b12),
                         (b2s, b22), (wmixs, wmix2), (wffns, wffn2), (selps, selp)):
            step("sync", lambda e, d=dst, s=src: e.dma_start(out=d[:, :], in_=s),
                 incs=[(sBIAS, 16)])
            sBIAS.bump(16)
        t_bias = (sBIAS, sBIAS.v)

        def mk_bq(e):
            CI("scalar", e.activation(bq16[:, :], b2s[:, :], AF.Copy))
            CW("scalar", e)
            return e.activation(bq32[:, :], bq16[:, :], AF.Copy)
        step("scalar", mk_bq, waits=[t_bias], incs=[(sS, 1)])
        t_bq = (sS, sS.bump())

        p1 = contextlib.ExitStack()
        sb1 = lambda name, shape, dt: p1.enter_context(nc.sbuf_tensor(name, shape, dt))
        xb0 = sb1("xb0", [128, KD, TC], F32)
        xb1 = sb1("xb1", [128, KD, TC], F32)
        xbufs = [xb0, xb1]
        xn_sb = sb1("xn_sb", [128, KD, TC], F32)
        xnh = sb1("xnh", [128, KD, TC], BF16)
        xnl = sb1("xnl", [128, KD, TC], BF16)
        xah = sb1("xah", [128, KD, TC], BF16)
        xal = sb1("xal", [128, KD, TC], BF16)
        xs_sb = sb1("xs_sb", [128, KD, TC], F32)
        a_sb = sb1("a_sb", [128, KD, TC], F32)
        hq_sb = sb1("hq_sb", [128, KH, TC], FP8)
        wq_sb = sb1("wq_sb", [128, KH, 128], FP8)
        wq2_sb = sb1("wq2_sb", [128, KH, 128], FP8)

        hs = {}

        def emit_load_x(c):
            h = hs.setdefault(c, {})
            sx = sX2[c % 2]
            xw = []
            if c >= 2:
                xw.append((sSTp[c % 2], sSTp[c % 2].v))
            step("sync", lambda e, a=c * TC, b=(c + 1) * TC, xb=xbufs[c % 2]:
                 e.dma_start(out=xb[:, :, :], in_=xTv[:, :, a:b]),
                 waits=xw, incs=[(sx, 16)])
            h["x"] = (sx, sx.bump(16))

        def emit_stats(c):
            h = hs[c]
            xb = xbufs[c % 2]
            pw = []
            if c >= 1:
                pw += [(sXN, sXN.v), hs[c - 1]["xnsp"], hs[c - 1]["xa"]]
            step("tensor", lambda e: [e.matmul(pb[4][0:1, 0:TC], ones[:, 0:1],
                                               xb[:, kc, :], start=(kc == 0),
                                               stop=(kc == KD - 1))
                                      for kc in range(KD)][-1],
                 waits=[h["x"], t_ones], incs=[(sT, 1)])
            t_sumx = (sT, sT.bump())
            t_sq = []
            for kc in range(KD):
                scr = xn_sb[:, kc % 2, :]
                w_ = [h["x"]] + pw if kc < 2 else [(sT, t_sq[kc - 2][1])]
                step("scalar", lambda e, kc=kc, scr=scr: e.activation(
                    scr, xb[:, kc, :], AF.Square), waits=w_, incs=[(sS, 1)])
                t_s = (sS, sS.bump())
                step("tensor", lambda e, kc=kc, scr=scr: e.matmul(
                    pb[5][0:1, 0:TC], ones[:, 0:1], scr,
                    start=(kc == 0), stop=(kc == KD - 1)),
                     waits=[t_s], incs=[(sT, 1)])
                t_sq.append((sT, sT.bump()))
            t_sumsq = t_sq[-1]

            ew = [t_sumx, t_sumsq]
            if c >= 1:
                ew += [hs[c - 1]["bc1"], hs[c - 1]["bc2"]]

            def evrows(e):
                CI("scalar", e.copy(row(0), pb[4][0:1, 0:TC]))
                return e.copy(row(1), pb[5][0:1, 0:TC])
            step("scalar", evrows, waits=ew, incs=[(sS, 1)])
            t_ev = (sS, sS.bump())

            def rows1(e):
                ops = [
                    lambda: e.tensor_scalar(row(0), row(0), INV_D, None, OP.mult),
                    lambda: e.tensor_scalar(row(1), row(1), INV_D, None, OP.mult),
                    lambda: e.tensor_tensor(row(2), row(0), row(0), OP.mult),
                    lambda: e.tensor_tensor(row(1), row(1), row(2), OP.subtract),
                    lambda: e.tensor_scalar(row(2), row(1), 1e-5, None, OP.add),
                    lambda: e.tensor_scalar(row(3), row(2), 1e-6, None, OP.mult),
                ]
                for op in ops:
                    CI("vector", op())
                    CW("vector", e)
                return e.tensor_tensor(row(3), row(1), row(3), OP.add)
            step("vector", rows1, waits=[t_ev, t_onerow], incs=[(sV, 1)])
            t_r1 = (sV, sV.bump())

            def sqrts(e):
                CI("scalar", e.activation(row(1), row(2), AF.Sqrt))
                return e.activation(row(4), row(3), AF.Sqrt)
            step("scalar", sqrts, waits=[t_r1], incs=[(sS, 1)])
            t_sqr = (sS, sS.bump())

            def rows2(e):
                ops = [
                    lambda: e.reciprocal(row(5), row(1)),
                    lambda: e.tensor_tensor(row(5), row(2), row(5), OP.mult),
                    lambda: e.tensor_tensor(row(5), row(1), row(5), OP.add),
                    lambda: e.tensor_scalar(row(5), row(5), 0.5, None, OP.mult),
                    lambda: e.reciprocal(row(1), row(5)),
                    lambda: e.reciprocal(row(5), row(4)),
                    lambda: e.tensor_tensor(row(5), row(3), row(5), OP.mult),
                    lambda: e.tensor_tensor(row(5), row(4), row(5), OP.add),
                    lambda: e.tensor_scalar(row(5), row(5), 0.5, None, OP.mult),
                    lambda: e.reciprocal(row(4), row(5)),
                    lambda: e.tensor_tensor(row(0), row(0), row(1), OP.mult),
                    lambda: e.tensor_tensor(row(5), row(2), row(1), OP.mult),
                ]
                for op in ops:
                    CI("vector", op())
                    CW("vector", e)
                return e.tensor_tensor(row(5), row(5), row(4), OP.mult)
            step("vector", rows2, waits=[t_sqr], incs=[(sV, 1)])
            h["rows"] = (sV, sV.bump())

        def emit_xn_a(c):
            h = hs[c]
            xb = xbufs[c % 2]
            bw = [h["rows"]]
            if c >= 1:
                bw += [hs[c - 1]["xa"]]
            step("tensor", lambda e: (e.matmul(pb[6][:, 0:TC], ones[0:1, :], row(1),
                                               start=True, stop=True),
                                      e.matmul(pb[7][:, 0:TC], ones[0:1, :], row(0),
                                               start=True, stop=True))[-1],
                 waits=bw, incs=[(sT, 1)])
            h["bc1"] = (sT, sT.bump())

            def mk_xn(e):
                for kc in range(KD):
                    inst = e.tensor_tensor(xn_sb[:, kc, :], xb[:, kc, :],
                                           pb[6][:, 0:TC], OP.mult)
                CI("vector", inst)
                CW("vector", e)
                for kc in range(KD):
                    inst = e.tensor_tensor(xn_sb[:, kc, :], xn_sb[:, kc, :],
                                           pb[7][:, 0:TC], OP.subtract)
                return inst
            step("vector", mk_xn, waits=[h["bc1"], h["x"]], incs=[(sV, 1)])
            h["xn"] = (sV, sV.bump())
            step("scalar", lambda e, a=c * TC, b=(c + 1) * TC: e.dma_start(
                out=xnv[:, :, a:b], in_=xn_sb[:, :, :]), waits=[h["xn"]],
                 incs=[(sXN, 16)])
            h["spill"] = (sXN, sXN.bump(16))

        def emit_xn_b(c):
            h = hs[c]
            step("tensor", lambda e: e.matmul(pb[6][:, 0:TC], ones[0:1, :], row(5),
                                              start=True, stop=True),
                 waits=[h["xn"]], incs=[(sT, 1)])
            h["bc2"] = (sT, sT.bump())

        def emit_splits(c):
            h = hs[c]
            t_prev = (sT, sT.v)
            t_xnl = None
            t_xal = None
            for kc in range(KD):
                xw = [h["xn"]]
                if t_xnl is None:
                    xw.append(t_prev)
                t_xnl = act_split(xn_sb[:, kc, :], xnh[:, kc, :], xnl[:, kc, :],
                                  xw + ([t_xnl] if t_xnl else []))
                scr = sg_sb if kc % 2 == 0 else tv_sb
                aw = [h["bc2"]]
                if t_xal is None:
                    aw.append(t_prev)
                step("vector", lambda e, kc=kc, scr=scr: e.scalar_tensor_tensor(
                    scr[:, :], xn_sb[:, kc, :], wmixs[:, kc:kc + 1],
                    pb[6][:, 0:TC], OP.mult, OP.mult),
                     waits=aw + ([t_xal] if t_xal else []), incs=[(sV, 1)])
                t_tmp = (sV, sV.bump())
                t_xal = act_split(scr[:, :], xah[:, kc, :], xal[:, kc, :], [t_tmp])
            h["xa"] = t_xal
            h["xnsp"] = t_xnl

        gvd_cfg = ((bgs, AF.Sigmoid, sg_sb), (bvs, AF.Tanh, tv_sb),
                   (bds, AF.Sigmoid, sl_sb))

        def add_chunk1(c):
            gst = {"evac": {0: None, 1: None, 2: None}, "xs": None, "ls": None}

            def gvd_consume(wh, wl, wtoks, i, c=c, gst=gst):
                m, pi = i // 3, i % 3
                bias_t, func, dst = gvd_cfg[pi]
                mw = wtoks + [hs[c]["xa"]]
                if gst["evac"][pi] is not None:
                    mw.append(gst["evac"][pi])
                step("tensor", mm3(pb[pi][:, 0:TC], wh, wl, xah, xal, KD),
                     waits=mw, incs=[(sT, 1)])
                t_mm = (sT, sT.bump())
                if pi < 2:
                    ew = [t_mm] + ([gst["xs"]] if gst["xs"] else [])
                else:
                    ew = [t_mm] + ([gst["ls"]] if gst["ls"] else [])
                step("scalar", lambda e, m=m, pi=pi, bias_t=bias_t, func=func,
                     dst=dst: e.activation(dst[:, :], pb[pi][:, 0:TC], func,
                                           bias=bias_t[:, m:m + 1]),
                     waits=ew, incs=[(sS, 1)])
                gst["evac"][pi] = (sS, sS.bump())
                if pi == 2:
                    spw = [(sSP, sSP.v)] if (m == 0 and c >= 1) else []
                    step("vector", lambda e, m=m: e.tensor_tensor(
                        xs_sb[:, m, :], sg_sb[:, :], tv_sb[:, :], OP.mult),
                         waits=[gst["evac"][0], gst["evac"][1]] + spw,
                         incs=[(sV, 1)])
                    gst["xs"] = (sV, sV.bump())
                    step("vector", lambda e, m=m: e.tensor_scalar(
                        a_sb[:, m, :], sl_sb[:, :], 0.998, 0.001, OP.mult, OP.add),
                         waits=[gst["evac"][2]] + spw, incs=[(sV, 1)])
                    t_a = (sV, sV.bump())
                    cwait = [t_car0] if c == 0 else []

                    def lscan(e, m=m):
                        CW("vector", e)
                        CI("vector", e.tensor_tensor_scan(
                            sl_sb[:, :], a_sb[:, m, :], xs_sb[:, m, :],
                            carry[:, m:m + 1], OP.mult, OP.add))
                        CW("vector", e)
                        return e.tensor_copy(carry[:, m:m + 1], sl_sb[:, TC - 1:TC])
                    step("vector", lscan,
                         waits=[gst["xs"], t_a, gst["evac"][2]] + cwait,
                         incs=[(sV, 1)])
                    gst["ls"] = (sV, sV.bump())
                return t_mm

            for i in range(3 * KD):
                src = (wgT, wvT, wdT)[i % 3]
                m = i // 3
                SU(src[m], KD,
                   (lambda i: lambda wh, wl, wt: gvd_consume(wh, wl, wt, i))(i))
                if c + 1 < NCH:
                    if i == 0:
                        HOOK(lambda c=c: emit_load_x(c + 1))
                    elif i == 3:
                        HOOK(lambda c=c: emit_stats(c + 1))
                    elif i == 16:
                        HOOK(lambda c=c: emit_xn_a(c + 1))
                    elif i == 20:
                        HOOK(lambda c=c: emit_xn_b(c + 1))

            def end_gvd(c=c, gst=gst):
                t_gvd_all = (sV, sV.v)
                step("scalar", lambda e, a=c * TC, b=(c + 1) * TC: e.dma_start(
                    out=xsv[:, :, a:b], in_=xs_sb[:, :, :]), waits=[t_gvd_all],
                     incs=[(sSP, 16)])
                step("scalar", lambda e, a=c * TC, b=(c + 1) * TC: e.dma_start(
                    out=av[:, :, a:b], in_=a_sb[:, :, :]), waits=[t_gvd_all],
                     incs=[(sSP, 16)])
                sSP.bump(32)
                if c == NCH - 1:
                    step("gpsimd", lambda e: e.dma_start(out=cc_in.ap(),
                                                         in_=carry[:, :]),
                         waits=[gst["ls"]], incs=[(sGD, 16)])
                    t_ccin = (sGD, sGD.bump(16))
                    step("gpsimd", lambda e: e.collective_compute(
                        "AllGather", OP.bypass,
                        replica_groups=[[0, 1], [2, 3], [4, 5], [6, 7]],
                        ins=[cc_in.ap()], outs=[cc_out.ap()]),
                         waits=[t_ccin], incs=[(sCC, 1)])
                    t_cc = (sCC, sCC.bump())
                    step("gpsimd", lambda e: e.dma_start(
                        out=hprev[:, :], in_=cc_out.ap()[0, :, :]), waits=[t_cc],
                         incs=[(sGD, 16)])
                    t_hp = (sGD, sGD.bump(16))
                    step("vector", lambda e: e.tensor_scalar(
                        carry2[:, :], hprev[:, :], selps[:, 0:1], None, OP.mult),
                         waits=[t_hp], incs=[(sV, 1)])
                    hs["c2"] = (sV, sV.bump())
            HOOK(end_gvd)

            w1st = {"h": None}

            def w1_consume(wh, wl, wtoks, m, c=c, w1st=w1st):
                mw = wtoks + [hs[c]["xnsp"]]
                if w1st["h"] is not None:
                    mw.append(w1st["h"])
                step("tensor", mm3(pb[3][:, 0:TC], wh, wl, xnh, xnl, KD),
                     waits=mw, incs=[(sT, 1)])
                t_mm = (sT, sT.bump())
                step("scalar", lambda e, m=m: e.activation(
                    hq_sb[:, m, :], pb[3][:, 0:TC], AF.Relu, bias=b1s[:, m:m + 1]),
                     waits=[t_mm], incs=[(sS, 1)])
                w1st["h"] = (sS, sS.bump())
                return t_mm

            for m in range(KH):
                SU(w1T[m], KD,
                   (lambda m: lambda wh, wl, wt: w1_consume(wh, wl, wt, m))(m))
            if c + 1 < NCH:
                HOOK(lambda c=c: emit_splits(c + 1))

            w2st = {"casts": [], "f": None, "part": None, "mm": None}

            def w2_tail(mo, t_mm, c=c, w2st=w2st):
                fw = [t_mm, t_bq]
                if w2st["part"] is not None:
                    fw.append(w2st["part"])
                step("scalar", lambda e, mo=mo: e.activation(
                    f16_sb[:, :], pb[3][:, 0:TC], AF.Identity,
                    bias=bq32[:, mo:mo + 1]), waits=fw, incs=[(sS, 1)])
                w2st["f"] = (sS, sS.bump())

                def mkpart(e, mo=mo, xb=xbufs[c % 2]):
                    CI("vector", e.tensor_copy(tmpf[:, :], f16_sb[:, :]))
                    CW("vector", e)
                    return e.tensor_tensor(xb[:, mo, :], xb[:, mo, :],
                                           tmpf[:, :], OP.add)
                step("vector", mkpart, waits=[w2st["f"]], incs=[(sV, 1)])
                w2st["part"] = (sV, sV.bump())

            def w2_consume(stga, t_ld, q, mo, c=c, w2st=w2st, w1st=w1st):
                vwt = [t_ld]
                if w2st["mm"] is not None:
                    vwt.append(w2st["mm"])
                if q == 0:
                    vwt.append((sWQ, sWQ.v))
                step("vector", lambda e, stga=stga, q=q: e.tensor_copy(
                    wq_sb[:, q * 8:(q + 1) * 8, :], stga[:, 0:8, :]),
                     waits=vwt, incs=[(sV, 1)])
                t_cast = (sV, sV.bump())
                w2st["casts"].append(t_cast)
                if q < 3:
                    return t_cast
                casts = list(w2st["casts"])
                w2st["casts"] = []
                step("scalar", lambda e, mo=mo: e.dma_start(
                    out=w2q8[mo], in_=wq_sb[:, :, :]), waits=casts,
                     incs=[(sWQ, 16)])
                sWQ.bump(16)
                mw = casts + [w1st["h"]]
                if w2st["f"] is not None:
                    mw.append(w2st["f"])

                def mmw2(e):
                    for kh in range(KH):
                        inst = e.matmul(pb[3][:, 0:TC], wq_sb[:, kh, :],
                                        hq_sb[:, kh, :], start=(kh == 0),
                                        stop=(kh == KH - 1))
                    return inst
                step("tensor", mmw2, waits=mw, incs=[(sT, 1)])
                w2st["mm"] = (sT, sT.bump())
                w2_tail(mo, w2st["mm"])
                return w2st["mm"]

            if c == 0:
                for mo in range(KD):
                    for q in range(4):
                        RU(w2T[mo][:, q * 8:(q + 1) * 8, :], 8,
                           (lambda q, mo: lambda stga, t_ld:
                            w2_consume(stga, t_ld, q, mo))(q, mo))

            def w2_cached(c=c, w2st=w2st, w1st=w1st):
                mms = {}
                for mo in range(KD):
                    wq = wq_sb if mo % 2 == 0 else wq2_sb
                    sq = sQ2[mo % 2]
                    lw = [(sWQ, sWQ.v)]
                    if mo >= 2:
                        lw.append(mms[mo - 2])
                    else:
                        lw.append(hs[c - 1]["w2mm"][mo - 2 + KD])
                    step("sync", lambda e, mo=mo, wq=wq, sq=sq: e.dma_start(
                        out=wq[:, :, :], in_=w2q8[mo]).then_inc(sq.h, 16),
                         waits=lw)
                    t_l = (sq, sq.bump(16))
                    mw = [t_l, w1st["h"]]
                    if w2st["f"] is not None:
                        mw.append(w2st["f"])

                    def mmw2(e, wq=wq):
                        for kh in range(KH):
                            inst = e.matmul(pb[3][:, 0:TC], wq[:, kh, :],
                                            hq_sb[:, kh, :], start=(kh == 0),
                                            stop=(kh == KH - 1))
                        return inst
                    step("tensor", mmw2, waits=mw, incs=[(sT, 1)])
                    w2st["mm"] = (sT, sT.bump())
                    mms[mo] = w2st["mm"]
                    w2_tail(mo, w2st["mm"])
                hs[c]["w2mm"] = mms

            def end_w2(c=c, w2st=w2st):
                if c == 0:
                    hs[0]["w2mm"] = {6: w2st["mm"], 7: w2st["mm"]}
                if c >= 1:
                    w2_cached()
                sst = sSTp[c % 2]
                step("scalar", lambda e, a=c * TC, b=(c + 1) * TC,
                     xb=xbufs[c % 2], sst=sst:
                     e.dma_start(out=outv[:, :, a:b], in_=xb[:, :, :]),
                     waits=[w2st["part"]], incs=[(sst, 16)])
                hs[c]["pstore"] = (sst, sst.bump(16))
            HOOK(end_w2)

        emit_load_x(0)
        emit_stats(0)
        emit_xn_a(0)
        emit_xn_b(0)
        emit_splits(0)
        for c in range(NCH):
            add_chunk1(c)
        run_stream()

        emit_block()
        p1.close()

        p3 = contextlib.ExitStack()
        sb3 = lambda name, shape, dt: p3.enter_context(nc.sbuf_tensor(name, shape, dt))
        q_xn = sb3("q_xn", [128, KD, TC], F32)
        q_xs = sb3("q_xs", [128, KD, TC], F32)
        q_a = sb3("q_a", [128, KD, TC], F32)
        q_pa = sb3("q_pa", [128, KD, TC], F32)
        q_s = sb3("q_s", [128, KD, TC], F32)
        xbh = sb3("xbh", [128, KD, TC], BF16)
        xbl = sb3("xbl", [128, KD, TC], BF16)
        pmh = sb3("pmh", [128, KF, TC], BF16)
        pml = sb3("pml", [128, KF, TC], BF16)

        h3 = {}

        def p3_boundary():
            return [(sXN, sXN.v), (sSP, sSP.v), (sT, sT.v), (sS, sS.v),
                    (sV, sV.v), (sSTp[0], sSTp[0].v), (sSTp[1], sSTp[1].v)]

        def p3_load_xsa(c):
            h = h3.setdefault(c, {})
            lw = p3_boundary() if c == 0 else [h3[c - 1]["scan"]]
            step("sync", lambda e, a=c * TC, b=(c + 1) * TC: e.dma_start(
                out=q_xs[:, :, :], in_=xsv[:, :, a:b]), waits=lw, incs=[(sP3A, 16)])
            sP3A.bump(16)
            step("sync", lambda e, a=c * TC, b=(c + 1) * TC: e.dma_start(
                out=q_a[:, :, :], in_=av[:, :, a:b]), incs=[(sP3A, 16)])
            sP3A.bump(16)
            h["lxa"] = (sP3A, sP3A.v)

        def p3_scans(c):
            h = h3[c]
            sw = [h["lxa"]]
            if c == 0:
                sw.append(hs["c2"])
            else:
                sw.append(h3[c - 1]["part2"])

            def scans(e):
                for m in range(KD):
                    CW("vector", e)
                    CI("vector", e.tensor_tensor_scan(
                        q_s[:, m, :], q_a[:, m, :], q_xs[:, m, :],
                        carry2[:, m:m + 1], OP.mult, OP.add))
                    CW("vector", e)
                    inst = e.tensor_copy(carry2[:, m:m + 1], q_s[:, m, TC - 1:TC])
                    if m < KD - 1:
                        CI("vector", inst)
                return inst
            step("vector", scans, waits=sw, incs=[(sV, 1)])
            h["scan"] = (sV, sV.bump())

        def p3_load_xn(c):
            h = h3[c]
            lw = p3_boundary() if c == 0 else [h3[c - 1]["part1"]]
            step("sync", lambda e, a=c * TC, b=(c + 1) * TC: e.dma_start(
                out=q_xn[:, :, :], in_=xnv[:, :, a:b]), waits=lw, incs=[(sP3X, 16)])
            h["lxn"] = (sP3X, sP3X.bump(16))

        def p3_part1(c):
            h = h3[c]
            step("vector", lambda e: e.tensor_tensor(
                q_s[:, :, :], q_xn[:, :, :], q_s[:, :, :], OP.add),
                 waits=[h["lxn"], h["scan"]], incs=[(sV, 1)])
            h["part1"] = (sV, sV.bump())

        def p3_load_pa(c):
            h = h3.setdefault(c, {})
            lw = [(sSTp[0], sSTp[0].v), (sSTp[1], sSTp[1].v)]
            step("sync", lambda e, a=c * TC, b=(c + 1) * TC: e.dma_start(
                out=q_pa[:, :, :], in_=outv[:, :, a:b]), waits=lw, incs=[(sP3P, 16)])
            h["lpa"] = (sP3P, sP3P.bump(16))

        def p3_part2(c):
            h = h3[c]
            step("vector", lambda e: e.tensor_tensor(
                q_pa[:, :, :], q_pa[:, :, :], q_s[:, :, :], OP.add),
                 waits=[h["lpa"], h["part1"]], incs=[(sV, 1)])
            h["part2"] = (sV, sV.bump())

        def p3_stats(c):
            h = h3[c]
            t_sq3 = []
            fw1 = [h3[c - 1]["fin1"]] if c >= 1 else []
            for kc in range(KD):
                scr = sl_sb if kc % 2 == 0 else sq2_sb
                w_ = [h["part1"]] + fw1 if kc < 2 else [(sT, t_sq3[kc - 2][1])]
                step("scalar", lambda e, kc=kc, scr=scr: e.activation(
                    scr[:, :], q_s[:, kc, :], AF.Square), waits=w_, incs=[(sS, 1)])
                t_s3 = (sS, sS.bump())
                step("tensor", lambda e, kc=kc, scr=scr: e.matmul(
                    pb[2][0:1, 0:TC], ones[:, 0:1], scr[:, :],
                    start=(kc == 0), stop=(kc == KD - 1)), waits=[t_s3],
                     incs=[(sT, 1)])
                t_sq3.append((sT, sT.bump()))
            step("scalar", lambda e: e.copy(row(0), pb[2][0:1, 0:TC]),
                 waits=[t_sq3[-1]], incs=[(sS, 1)])
            t_ev3 = (sS, sS.bump())
            step("vector", lambda e: e.tensor_scalar(
                row(1), row(0), INV_D, 1e-6, OP.mult, OP.add),
                 waits=[t_ev3], incs=[(sV, 1)])
            t_u3 = (sV, sV.bump())
            step("scalar", lambda e: e.activation(row(2), row(1), AF.Sqrt),
                 waits=[t_u3], incs=[(sS, 1)])
            t_y3 = (sS, sS.bump())

            def rows3(e):
                ops = [
                    lambda: e.reciprocal(row(3), row(2)),
                    lambda: e.tensor_tensor(row(3), row(1), row(3), OP.mult),
                    lambda: e.tensor_tensor(row(3), row(2), row(3), OP.add),
                    lambda: e.tensor_scalar(row(3), row(3), 0.5, None, OP.mult),
                ]
                for op in ops:
                    CI("vector", op())
                    CW("vector", e)
                return e.reciprocal(row(4), row(3))
            step("vector", rows3, waits=[t_y3], incs=[(sV, 1)])
            t_r3 = (sV, sV.bump())
            step("tensor", lambda e: e.matmul(pb[3][:, 0:TC], ones[0:1, :],
                                              row(4), start=True, stop=True),
                 waits=[t_r3], incs=[(sT, 1)])
            h["bc3"] = (sT, sT.bump())

        def p3_xbsplit(c):
            h = h3[c]
            t_xbl = None
            for kc in range(KD):
                scr = sg_sb if kc % 2 == 0 else tv_sb
                aw = [h["bc3"]]
                if t_xbl is None:
                    aw.append((sS, sS.v))
                step("vector", lambda e, kc=kc, scr=scr: e.scalar_tensor_tensor(
                    scr[:, :], q_s[:, kc, :], wffns[:, kc:kc + 1],
                    pb[3][:, 0:TC], OP.mult, OP.mult),
                     waits=aw + ([t_xbl] if t_xbl else []), incs=[(sV, 1)])
                t_tmp = (sV, sV.bump())
                t_xbl = act_split(scr[:, :], xbh[:, kc, :], xbl[:, kc, :], [t_tmp])
            h["xb"] = t_xbl

        def add_chunk3(c):
            g3 = {"sil": None, "pm": None, "wo": None, "pml": None, "sw2": None}

            def g3_consume(wh, wl, wtoks, i, c=c, g3=g3):
                m, r = i // 3, i % 3
                if r == 0:
                    mw = wtoks + [h3[c]["xb"]]
                    if g3["sil"] is not None:
                        mw.append(g3["sil"])
                    if m == 0 and c >= 1:
                        mw.append(h3[c - 1]["fin2"])
                    step("tensor", mm3(pb[0][:, 0:TC], wh, wl, xbh, xbl, KD),
                         waits=mw, incs=[(sT, 1)])
                    t_g = (sT, sT.bump())
                    sw2 = [t_g] + ([g3["pm"]] if g3["pm"] else [])
                    step("scalar", lambda e: e.activation(
                        sg_sb[:, :], pb[0][:, 0:TC], AF.Silu),
                         waits=sw2, incs=[(sS, 1)])
                    g3["sil"] = (sS, sS.bump())
                    return t_g
                if r == 1:
                    mw2 = wtoks + [h3[c]["xb"]]
                    if g3["pm"] is not None:
                        mw2.append(g3["pm"])
                    if m == 0 and c >= 1:
                        mw2.append(h3[c - 1]["fin2"])
                    step("tensor", mm3(pb[1][:, 0:TC], wh, wl, xbh, xbl, KD),
                         waits=mw2, incs=[(sT, 1)])
                    t_u = (sT, sT.bump())
                    step("vector", lambda e: e.tensor_tensor(
                        tmpf[:, :], sg_sb[:, :], pb[1][:, 0:TC], OP.mult),
                         waits=[g3["sil"], t_u] + ([g3["pml"]] if g3["pml"] else []),
                         incs=[(sV, 1)])
                    g3["pm"] = (sV, sV.bump())
                    g3["pml"] = act_split(tmpf[:, :], pmh[:, m, :], pml[:, m, :],
                                          [g3["pm"]])
                    return t_u

                def mmwo(e, wh=wh, wl=wl, m=m):
                    for mo in range(6):
                        e.matmul(pb[2 + mo][:, 0:TC], wh[:, mo, :], pmh[:, m, :],
                                 start=(m == 0), stop=False)
                        e.matmul(pb[2 + mo][:, 0:TC], wh[:, mo, :], pml[:, m, :],
                                 start=False, stop=False)
                        inst = e.matmul(pb[2 + mo][:, 0:TC], wl[:, mo, :],
                                        pmh[:, m, :], start=False,
                                        stop=(m == KF - 1))
                    return inst
                wo_w = wtoks + [g3["pml"]]
                if m == 0 and c >= 1:
                    wo_w.append(h3[c - 1]["fin1"])
                step("tensor", mmwo, waits=wo_w, incs=[(sT, 1)])
                g3["wo"] = (sT, sT.bump())
                return g3["wo"]

            for i in range(3 * KF):
                m, r = i // 3, i % 3
                if r == 0:
                    src, ns = wgateT[m], KD
                elif r == 1:
                    src, ns = wupT[m], KD
                else:
                    src = woutv[m, :, 0:768].rearrange("p (s q) -> p s q", q=128)
                    ns = 6
                SU(src, ns,
                   (lambda i: lambda wh, wl, wt: g3_consume(wh, wl, wt, i))(i))
                if c + 1 < NCH:
                    if i == 2:
                        HOOK(lambda c=c: p3_load_xsa(c + 1))
                    elif i == 6:
                        HOOK(lambda c=c: p3_scans(c + 1))
                    elif i == 40:
                        HOOK(lambda c=c: p3_load_xn(c + 1))
                    elif i == 44:
                        HOOK(lambda c=c: p3_part1(c + 1))

            def fin1(c=c, g3=g3):
                def finev1(e):
                    for mo in range(6):
                        inst = e.tensor_tensor(q_pa[:, mo, :], q_pa[:, mo, :],
                                               pb[2 + mo][:, 0:TC], OP.add)
                    return inst
                step("vector", finev1, waits=[g3["wo"], h3[c]["part2"]],
                     incs=[(sV, 1)])
                h3[c]["fin1"] = (sV, sV.bump())
                if c + 1 < NCH:
                    p3_stats(c + 1)
            HOOK(fin1)

            def sw2_consume(wh, wl, wtoks, m, c=c, g3=g3):
                mw3 = list(wtoks)
                if m == 0:
                    mw3 += [g3["pm"], g3["sil"], g3["pml"], h3[c]["fin1"]]

                def mm2(e, wh=wh, wl=wl, m=m):
                    for mo in range(2):
                        e.matmul(pb[mo][:, 0:TC], wh[:, mo, :], pmh[:, m, :],
                                 start=(m == 0), stop=False)
                        e.matmul(pb[mo][:, 0:TC], wh[:, mo, :], pml[:, m, :],
                                 start=False, stop=False)
                        inst = e.matmul(pb[mo][:, 0:TC], wl[:, mo, :],
                                        pmh[:, m, :], start=False,
                                        stop=(m == KF - 1))
                    return inst
                step("tensor", mm2, waits=mw3, incs=[(sT, 1)])
                g3["sw2"] = (sT, sT.bump())
                return g3["sw2"]

            for m in range(KF):
                SU(woutv[m, :, 768:1024].rearrange("p (s q) -> p s q", q=128), 2,
                   (lambda m: lambda wh, wl, wt: sw2_consume(wh, wl, wt, m))(m))
                if m == 2 and c + 1 < NCH:
                    HOOK(lambda c=c: p3_xbsplit(c + 1))

            def fin2(c=c, g3=g3):
                def finev2(e):
                    CI("vector", e.tensor_tensor(q_pa[:, 6, :], q_pa[:, 6, :],
                                                 pb[0][:, 0:TC], OP.add))
                    return e.tensor_tensor(q_pa[:, 7, :], q_pa[:, 7, :],
                                           pb[1][:, 0:TC], OP.add)
                step("vector", finev2, waits=[g3["sw2"]], incs=[(sV, 1)])
                h3[c]["fin2"] = (sV, sV.bump())
                sst = sSTp[c % 2]
                step("scalar", lambda e, a=c * TC, b=(c + 1) * TC, sst=sst:
                     e.dma_start(out=outv[:, :, a:b], in_=q_pa[:, :, :]),
                     waits=[h3[c]["fin1"], h3[c]["fin2"]], incs=[(sst, 16)])
                sst.bump(16)
                if c + 1 < NCH:
                    p3_load_pa(c + 1)
                    p3_part2(c + 1)
            HOOK(fin2)

        p3_load_xsa(0)
        p3_scans(0)
        p3_load_xn(0)
        p3_part1(0)
        p3_load_pa(0)
        p3_part2(0)
        p3_stats(0)
        p3_xbsplit(0)
        for c in range(NCH):
            add_chunk3(c)
        run_stream()

        step("gpsimd", None, waits=[(sSTp[0], sSTp[0].v), (sSTp[1], sSTp[1].v)])
        step("sync", None, waits=[(sSTp[0], sSTp[0].v), (sSTp[1], sSTp[1].v)])

        emit_block()
        p3.close()

    return nc


def kernel(L0_attn_x, L0_attn_rms_mix_w, L0_attn_rms_ffn_w, L0_attn_Wg, L0_attn_bg,
           L0_attn_Wv, L0_attn_bv, L0_attn_Wd, L0_attn_bd, L0_attn_W_gate,
           L0_attn_W_up, L0_attn_W_out, L0_ffn_w1, L0_ffn_b1, L0_ffn_w2, L0_ffn_b2):
    x = np.asarray(L0_attn_x, dtype=np.float32)

    def t(a):
        return np.ascontiguousarray(np.asarray(a, dtype=np.float32).T)

    def tw(a):
        w = np.asarray(a, dtype=np.float32)
        M, K = w.shape
        return np.ascontiguousarray(
            w.reshape(M // 128, 128, K // 128, 128).transpose(0, 3, 2, 1))

    def b2(a, k):
        return np.ascontiguousarray(np.asarray(a, dtype=np.float32).reshape(k, 128).T)

    shared = {
        "wgT": tw(L0_attn_Wg), "wvT": tw(L0_attn_Wv), "wdT": tw(L0_attn_Wd),
        "wgateT": tw(L0_attn_W_gate), "wupT": tw(L0_attn_W_up),
        "woutT": t(L0_attn_W_out),
        "w1T": tw(L0_ffn_w1), "w2T": tw(L0_ffn_w2),
        "bg2": b2(L0_attn_bg, KD), "bv2": b2(L0_attn_bv, KD), "bd2": b2(L0_attn_bd, KD),
        "b12": b2(L0_ffn_b1, KH), "b22": b2(L0_ffn_b2, KD),
        "wmix2": b2(L0_attn_rms_mix_w, KD), "wffn2": b2(L0_attn_rms_ffn_w, KD),
    }
    in_maps = []
    for core in range(8):
        b, h = core // 2, core % 2
        xc = np.ascontiguousarray(x[b, h * T:(h + 1) * T, :].T)
        d = dict(shared)
        d["xT"] = xc
        d["selp"] = np.full((128, 1), float(h), dtype=np.float32)
        in_maps.append(d)

    nc = build_nc()
    res = run_bass_kernel_spmd(nc, in_maps, core_ids=list(range(8)))
    out = np.empty((B, S, D), dtype=np.float32)
    for core in range(8):
        b, h = core // 2, core % 2
        out[b, h * T:(h + 1) * T, :] = res.results[core]["outT"].T
    return out


# version 16
# speedup vs baseline: 1.0451x; 1.0354x over previous
import contextlib

import numpy as np

import concourse.bass as bass
import concourse.mybir as mybir
from concourse.bass_utils import run_bass_kernel_spmd

F32 = mybir.dt.float32
BF16 = mybir.dt.bfloat16
FP8 = mybir.dt.float8e4
F16 = mybir.dt.float16
AF = mybir.ActivationFunctionType
OP = mybir.AluOpType

B, S, D, F, H = 4, 4096, 1024, 3072, 4096
T = 2048
KD, KF, KH = D // 128, F // 128, H // 128
TC, NCH = 512, 4
INV_D = 1.0 / D
PIPE_SLOTS = 8
NPIPE = 4
DEPTH = 3


class Sem:
    def __init__(self, nc, st, name):
        self.h = st.enter_context(nc.semaphore(name))
        self.v = 0

    def bump(self, n=1):
        self.v += n
        return self.v


def build_nc():
    nc = bass.Bass("TRN2", target_bir_lowering=False, debug=False)

    xT = nc.dram_tensor("xT", [D, T], F32, kind="ExternalInput").ap()
    wgT = nc.dram_tensor("wgT", [KD, 128, KD, 128], F32, kind="ExternalInput").ap()
    wvT = nc.dram_tensor("wvT", [KD, 128, KD, 128], F32, kind="ExternalInput").ap()
    wdT = nc.dram_tensor("wdT", [KD, 128, KD, 128], F32, kind="ExternalInput").ap()
    wgateT = nc.dram_tensor("wgateT", [KF, 128, KD, 128], F32, kind="ExternalInput").ap()
    wupT = nc.dram_tensor("wupT", [KF, 128, KD, 128], F32, kind="ExternalInput").ap()
    woutT = nc.dram_tensor("woutT", [F, D], F32, kind="ExternalInput").ap()
    w1T = nc.dram_tensor("w1T", [KH, 128, KD, 128], F32, kind="ExternalInput").ap()
    w2T = nc.dram_tensor("w2T", [KD, 128, KH, 128], F32, kind="ExternalInput").ap()
    w2q8 = nc.dram_tensor("w2q8", [KD, 128, KH, 128], FP8).ap()
    bg2 = nc.dram_tensor("bg2", [128, KD], F32, kind="ExternalInput").ap()
    bv2 = nc.dram_tensor("bv2", [128, KD], F32, kind="ExternalInput").ap()
    bd2 = nc.dram_tensor("bd2", [128, KD], F32, kind="ExternalInput").ap()
    b12 = nc.dram_tensor("b12", [128, KH], F32, kind="ExternalInput").ap()
    b22 = nc.dram_tensor("b22", [128, KD], F32, kind="ExternalInput").ap()
    wmix2 = nc.dram_tensor("wmix2", [128, KD], F32, kind="ExternalInput").ap()
    wffn2 = nc.dram_tensor("wffn2", [128, KD], F32, kind="ExternalInput").ap()
    selp = nc.dram_tensor("selp", [128, 1], F32, kind="ExternalInput").ap()

    outT = nc.dram_tensor("outT", [D, T], F32, kind="ExternalOutput").ap()

    xn_h = nc.dram_tensor("xn_h", [D, T], F32).ap()
    xs_h = nc.dram_tensor("xs_h", [D, T], F32).ap()
    a_h = nc.dram_tensor("a_h", [D, T], F32).ap()
    cc_in = nc.dram_tensor("cc_in", [128, KD], F32)
    cc_out = nc.dram_tensor("cc_out", [2, 128, KD], F32)

    def fm(ap_):
        return ap_.rearrange("(k p) t -> p k t", p=128)

    xTv, xnv, xsv, av, outv = fm(xT), fm(xn_h), fm(xs_h), fm(a_h), fm(outT)
    woutv = woutT.rearrange("(k p) m -> k p m", p=128)

    P = []

    def step(eng, body=None, waits=(), incs=()):
        waits = [(s.h, v) for (s, v) in waits]
        incs_ = [(s.h, n) for (s, n) in incs]

        def fn(e):
            for h, v in waits:
                e.wait_ge(h, v)
            inst = body(e) if body is not None else None
            if incs_:
                if inst is None:
                    assert body is None
                    inst = e.nop()
                for h, n in incs_:
                    inst.then_inc(h, n)

        P.append((eng, fn))

    def emit_block():
        with nc.Block() as blk:
            for eng_name in ("sync", "scalar", "vector", "tensor", "gpsimd"):
                def make(eng_name):
                    def emit(e):
                        for en, fn in P:
                            if en == eng_name:
                                fn(e)
                    return emit
                getattr(blk, eng_name)(make(eng_name))
        P.clear()

    with contextlib.ExitStack() as st:
        sb = lambda name, shape, dt: st.enter_context(nc.sbuf_tensor(name, shape, dt))

        ones = sb("ones", [128, 128], F32)
        bgs = sb("bgs", [128, KD], F32)
        bvs = sb("bvs", [128, KD], F32)
        bds = sb("bds", [128, KD], F32)
        b1s = sb("b1s", [128, KH], F32)
        b2s = sb("b2s", [128, KD], F32)
        bq16 = sb("bq16", [128, KD], F16)
        bq32 = sb("bq32", [128, KD], F32)
        wmixs = sb("wmixs", [128, KD], F32)
        wffns = sb("wffns", [128, KD], F32)
        selps = sb("selps", [128, 1], F32)
        carry = sb("carry", [128, KD], F32)
        carry2 = sb("carry2", [128, KD], F32)
        hprev = sb("hprev", [128, KD], F32)
        sg_sb = sb("sg_sb", [128, TC], F32)
        tv_sb = sb("tv_sb", [128, TC], F32)
        sl_sb = sb("sl_sb", [128, TC], F32)
        sq2_sb = sb("sq2_sb", [128, TC], F32)
        tmpf = sb("tmpf", [128, TC], F32)
        f16_sb = sb("f16_sb", [128, TC], F16)
        stat = sb("stat", [1, 7 * TC], F32)
        stg = [sb(f"stg{i}", [128, PIPE_SLOTS, 128], F32) for i in range(NPIPE)]
        wsh = [sb(f"wsh{i}", [128, PIPE_SLOTS, 128], BF16) for i in range(NPIPE)]
        wsl = [sb(f"wsl{i}", [128, PIPE_SLOTS, 128], BF16) for i in range(NPIPE)]

        pb = [st.enter_context(nc.psum_tensor(f"pb{i}", [128, 512], F32))
              for i in range(8)]

        sBIAS = Sem(nc, st, "sBIAS")
        sX2 = [Sem(nc, st, "sXa"), Sem(nc, st, "sXb")]
        sWp = [Sem(nc, st, f"sW{i}") for i in range(NPIPE)]
        sP3A = Sem(nc, st, "sP3A")
        sP3X = Sem(nc, st, "sP3X")
        sP3P = Sem(nc, st, "sP3P")
        sSTp = [Sem(nc, st, "sSTa"), Sem(nc, st, "sSTb")]
        sXN = Sem(nc, st, "sXN")
        sSP = Sem(nc, st, "sSP")
        sGD = Sem(nc, st, "sGD")
        sWQ = Sem(nc, st, "sWQ")
        sQ2 = [Sem(nc, st, "sQ2a"), Sem(nc, st, "sQ2b")]
        sT = Sem(nc, st, "sT")
        sV = Sem(nc, st, "sV")
        sS = Sem(nc, st, "sS")
        sG = Sem(nc, st, "sG")
        sCC = Sem(nc, st, "sCC")
        chain = {n: Sem(nc, st, f"ch_{n[:3]}") for n in ("vector", "scalar", "gpsimd")}

        def CI(en, inst):
            s = chain[en]
            s.v += 1
            inst.then_inc(s.h, 1)
            return inst

        def CW(en, e):
            s = chain[en]
            e.wait_ge(s.h, s.v)

        def row(i, n=TC):
            return stat[0:1, i * TC: i * TC + n]

        pipe_state = {i: {"free": []} for i in range(NPIPE)}
        pipe_par = [0]

        def pipe_load(src_ap, nslots):
            par = pipe_par[0]
            pipe_par[0] = (par + 1) % NPIPE
            ps = pipe_state[par]
            sw = sWp[par]
            stga, wha, wla = stg[par], wsh[par], wsl[par]
            step("sync", lambda e: e.dma_start(
                out=stga[:, 0:nslots, :], in_=src_ap).then_inc(sw.h, 16),
                 waits=list(ps["free"]))
            t_ld = (sw, sw.bump(16))
            step("scalar", lambda e: e.activation(
                wha[:, 0:nslots, :], stga[:, 0:nslots, :], AF.Copy),
                 waits=[t_ld], incs=[(sS, 1)])
            t_hi = (sS, sS.bump())
            step("vector", lambda e: e.tensor_tensor(
                wla[:, 0:nslots, :], stga[:, 0:nslots, :], wha[:, 0:nslots, :],
                OP.subtract), waits=[t_hi], incs=[(sV, 1)])
            t_lo = (sV, sV.bump())
            ps["free"] = [t_lo]
            return ("split", wha, wla, [t_hi, t_lo], par)

        def pipe_load_raw(src_ap, nslots):
            par = pipe_par[0]
            pipe_par[0] = (par + 1) % NPIPE
            ps = pipe_state[par]
            sw = sWp[par]
            stga = stg[par]
            step("sync", lambda e: e.dma_start(
                out=stga[:, 0:nslots, :], in_=src_ap).then_inc(sw.h, 16),
                 waits=list(ps["free"]))
            t_ld = (sw, sw.bump(16))
            ps["free"] = []
            return ("raw", stga, t_ld, None, par)

        stream = []
        hooks_after = {}

        def SU(src, nslots, consume):
            stream.append((False, src, nslots, consume))
            return len(stream) - 1

        def RU(src, nslots, consume):
            stream.append((True, src, nslots, consume))
            return len(stream) - 1

        def HOOK(fn):
            hooks_after.setdefault(len(stream) - 1, []).append(fn)

        def run_stream():
            n = len(stream)
            handles = {}

            def load(j):
                is_raw, src, ns, _ = stream[j]
                return pipe_load_raw(src, ns) if is_raw else pipe_load(src, ns)

            for j in range(min(DEPTH, n)):
                handles[j] = load(j)
            for i in range(n):
                kind, a1, a2, a3, par = handles.pop(i)
                consume = stream[i][3]
                if kind == "split":
                    tok = consume(a1, a2, a3)
                else:
                    tok = consume(a1, a2)
                pipe_state[par]["free"].append(tok)
                for fn in hooks_after.get(i, []):
                    fn()
                j = i + DEPTH
                if j < n:
                    handles[j] = load(j)
            stream.clear()
            hooks_after.clear()

        def mm3(ps_ap, wh, wl, xh, xl, nk):
            def body(e):
                for kc in range(nk):
                    e.matmul(ps_ap, wh[:, kc, :], xh[:, kc, :],
                             start=(kc == 0), stop=False)
                for kc in range(nk):
                    e.matmul(ps_ap, wh[:, kc, :], xl[:, kc, :],
                             start=False, stop=False)
                for kc in range(nk):
                    inst = e.matmul(ps_ap, wl[:, kc, :], xh[:, kc, :],
                                    start=False, stop=(kc == nk - 1))
                return inst
            return body

        def act_split(src_ap, hi_ap, lo_ap, waits):
            step("scalar", lambda e: e.activation(hi_ap, src_ap, AF.Copy),
                 waits=waits, incs=[(sS, 1)])
            t_hi = (sS, sS.bump())
            step("vector", lambda e: e.tensor_tensor(lo_ap, src_ap, hi_ap,
                                                     OP.subtract),
                 waits=[t_hi], incs=[(sV, 1)])
            return (sV, sV.bump())

        step("vector", lambda e: e.memset(ones[:, :], 1.0), incs=[(sV, 1)])
        t_ones = (sV, sV.bump())
        step("vector", lambda e: e.memset(carry[:, :], 0.0), incs=[(sV, 1)])
        t_car0 = (sV, sV.bump())
        step("vector", lambda e: e.memset(row(6), 1.0), incs=[(sV, 1)])
        t_onerow = (sV, sV.bump())

        for dst, src in ((bgs, bg2), (bvs, bv2), (bds, bd2), (b1s, b12),
                         (b2s, b22), (wmixs, wmix2), (wffns, wffn2), (selps, selp)):
            step("sync", lambda e, d=dst, s=src: e.dma_start(out=d[:, :], in_=s),
                 incs=[(sBIAS, 16)])
            sBIAS.bump(16)
        t_bias = (sBIAS, sBIAS.v)

        def mk_bq(e):
            CI("scalar", e.activation(bq16[:, :], b2s[:, :], AF.Copy))
            CW("scalar", e)
            return e.activation(bq32[:, :], bq16[:, :], AF.Copy)
        step("scalar", mk_bq, waits=[t_bias], incs=[(sS, 1)])
        t_bq = (sS, sS.bump())

        p1 = contextlib.ExitStack()
        sb1 = lambda name, shape, dt: p1.enter_context(nc.sbuf_tensor(name, shape, dt))
        xb0 = sb1("xb0", [128, KD, TC], F32)
        xb1 = sb1("xb1", [128, KD, TC], F32)
        xbufs = [xb0, xb1]
        xn_sb = sb1("xn_sb", [128, KD, TC], F32)
        xnh = sb1("xnh", [128, KD, TC], BF16)
        xnl = sb1("xnl", [128, KD, TC], BF16)
        xah = sb1("xah", [128, KD, TC], BF16)
        xal = sb1("xal", [128, KD, TC], BF16)
        xs_sb = sb1("xs_sb", [128, KD, TC], F32)
        a_sb = sb1("a_sb", [128, KD, TC], F32)
        hq_sb = sb1("hq_sb", [128, KH, TC], FP8)
        wq_sb = sb1("wq_sb", [128, KH, 128], FP8)
        wq2_sb = sb1("wq2_sb", [128, KH, 128], FP8)

        hs = {}

        def emit_load_x(c):
            h = hs.setdefault(c, {})
            sx = sX2[c % 2]
            xw = []
            if c >= 2:
                xw.append((sSTp[c % 2], sSTp[c % 2].v))
            step("sync", lambda e, a=c * TC, b=(c + 1) * TC, xb=xbufs[c % 2]:
                 e.dma_start(out=xb[:, :, :], in_=xTv[:, :, a:b]),
                 waits=xw, incs=[(sx, 16)])
            h["x"] = (sx, sx.bump(16))

        def emit_stats(c):
            h = hs[c]
            xb = xbufs[c % 2]
            pw = []
            if c >= 1:
                pw += [(sXN, sXN.v), hs[c - 1]["xnsp"], hs[c - 1]["xa"]]
            step("tensor", lambda e: [e.matmul(pb[4][0:1, 0:TC], ones[:, 0:1],
                                               xb[:, kc, :], start=(kc == 0),
                                               stop=(kc == KD - 1))
                                      for kc in range(KD)][-1],
                 waits=[h["x"], t_ones], incs=[(sT, 1)])
            t_sumx = (sT, sT.bump())
            t_sq = []
            for kc in range(KD):
                scr = xn_sb[:, kc % 2, :]
                w_ = [h["x"]] + pw if kc < 2 else [(sT, t_sq[kc - 2][1])]
                step("scalar", lambda e, kc=kc, scr=scr: e.activation(
                    scr, xb[:, kc, :], AF.Square), waits=w_, incs=[(sS, 1)])
                t_s = (sS, sS.bump())
                step("tensor", lambda e, kc=kc, scr=scr: e.matmul(
                    pb[5][0:1, 0:TC], ones[:, 0:1], scr,
                    start=(kc == 0), stop=(kc == KD - 1)),
                     waits=[t_s], incs=[(sT, 1)])
                t_sq.append((sT, sT.bump()))
            t_sumsq = t_sq[-1]

            ew = [t_sumx, t_sumsq]
            if c >= 1:
                ew += [hs[c - 1]["bc1"], hs[c - 1]["bc2"]]

            def evrows(e):
                CI("scalar", e.copy(row(0), pb[4][0:1, 0:TC]))
                return e.copy(row(1), pb[5][0:1, 0:TC])
            step("scalar", evrows, waits=ew, incs=[(sS, 1)])
            t_ev = (sS, sS.bump())

            def rows1(e):
                ops = [
                    lambda: e.tensor_scalar(row(0), row(0), INV_D, None, OP.mult),
                    lambda: e.tensor_scalar(row(1), row(1), INV_D, None, OP.mult),
                    lambda: e.tensor_tensor(row(2), row(0), row(0), OP.mult),
                    lambda: e.tensor_tensor(row(1), row(1), row(2), OP.subtract),
                    lambda: e.tensor_scalar(row(2), row(1), 1e-5, None, OP.add),
                    lambda: e.tensor_scalar(row(3), row(2), 1e-6, None, OP.mult),
                ]
                for op in ops:
                    CI("vector", op())
                    CW("vector", e)
                return e.tensor_tensor(row(3), row(1), row(3), OP.add)
            step("vector", rows1, waits=[t_ev, t_onerow], incs=[(sV, 1)])
            t_r1 = (sV, sV.bump())

            def sqrts(e):
                CI("scalar", e.activation(row(1), row(2), AF.Sqrt))
                return e.activation(row(4), row(3), AF.Sqrt)
            step("scalar", sqrts, waits=[t_r1], incs=[(sS, 1)])
            t_sqr = (sS, sS.bump())

            def rows2(e):
                ops = [
                    lambda: e.reciprocal(row(5), row(1)),
                    lambda: e.tensor_tensor(row(5), row(2), row(5), OP.mult),
                    lambda: e.tensor_tensor(row(5), row(1), row(5), OP.add),
                    lambda: e.tensor_scalar(row(5), row(5), 0.5, None, OP.mult),
                    lambda: e.reciprocal(row(1), row(5)),
                    lambda: e.reciprocal(row(5), row(4)),
                    lambda: e.tensor_tensor(row(5), row(3), row(5), OP.mult),
                    lambda: e.tensor_tensor(row(5), row(4), row(5), OP.add),
                    lambda: e.tensor_scalar(row(5), row(5), 0.5, None, OP.mult),
                    lambda: e.reciprocal(row(4), row(5)),
                    lambda: e.tensor_tensor(row(0), row(0), row(1), OP.mult),
                    lambda: e.tensor_tensor(row(5), row(2), row(1), OP.mult),
                ]
                for op in ops:
                    CI("vector", op())
                    CW("vector", e)
                return e.tensor_tensor(row(5), row(5), row(4), OP.mult)
            step("vector", rows2, waits=[t_sqr], incs=[(sV, 1)])
            h["rows"] = (sV, sV.bump())

        def emit_xn_a(c):
            h = hs[c]
            xb = xbufs[c % 2]
            bw = [h["rows"]]
            if c >= 1:
                bw += [hs[c - 1]["xa"]]
            step("tensor", lambda e: (e.matmul(pb[6][:, 0:TC], ones[0:1, :], row(1),
                                               start=True, stop=True),
                                      e.matmul(pb[7][:, 0:TC], ones[0:1, :], row(0),
                                               start=True, stop=True))[-1],
                 waits=bw, incs=[(sT, 1)])
            h["bc1"] = (sT, sT.bump())

            def mk_xn(e):
                for kc in range(KD):
                    inst = e.tensor_tensor(xn_sb[:, kc, :], xb[:, kc, :],
                                           pb[6][:, 0:TC], OP.mult)
                CI("vector", inst)
                CW("vector", e)
                for kc in range(KD):
                    inst = e.tensor_tensor(xn_sb[:, kc, :], xn_sb[:, kc, :],
                                           pb[7][:, 0:TC], OP.subtract)
                return inst
            step("vector", mk_xn, waits=[h["bc1"], h["x"]], incs=[(sV, 1)])
            h["xn"] = (sV, sV.bump())
            step("scalar", lambda e, a=c * TC, b=(c + 1) * TC: e.dma_start(
                out=xnv[:, :, a:b], in_=xn_sb[:, :, :]), waits=[h["xn"]],
                 incs=[(sXN, 16)])
            h["spill"] = (sXN, sXN.bump(16))

        def emit_xn_b(c):
            h = hs[c]
            step("tensor", lambda e: e.matmul(pb[6][:, 0:TC], ones[0:1, :], row(5),
                                              start=True, stop=True),
                 waits=[h["xn"]], incs=[(sT, 1)])
            h["bc2"] = (sT, sT.bump())

        def emit_splits(c):
            h = hs[c]
            t_prev = (sT, sT.v)
            t_xnl = None
            t_xal = None
            for kc in range(KD):
                xw = [h["xn"]]
                if t_xnl is None:
                    xw.append(t_prev)
                t_xnl = act_split(xn_sb[:, kc, :], xnh[:, kc, :], xnl[:, kc, :],
                                  xw + ([t_xnl] if t_xnl else []))
                scr = sg_sb if kc % 2 == 0 else tv_sb
                aw = [h["bc2"]]
                if t_xal is None:
                    aw.append(t_prev)
                step("vector", lambda e, kc=kc, scr=scr: e.scalar_tensor_tensor(
                    scr[:, :], xn_sb[:, kc, :], wmixs[:, kc:kc + 1],
                    pb[6][:, 0:TC], OP.mult, OP.mult),
                     waits=aw + ([t_xal] if t_xal else []), incs=[(sV, 1)])
                t_tmp = (sV, sV.bump())
                t_xal = act_split(scr[:, :], xah[:, kc, :], xal[:, kc, :], [t_tmp])
            h["xa"] = t_xal
            h["xnsp"] = t_xnl

        gvd_cfg = ((bgs, AF.Sigmoid, sg_sb), (bvs, AF.Tanh, tv_sb),
                   (bds, AF.Sigmoid, sl_sb))

        def add_chunk1(c):
            gst = {"evac": {0: None, 1: None, 2: None}, "xs": None, "ls": None}

            def gvd_consume(wh, wl, wtoks, i, c=c, gst=gst):
                m, pi = i // 3, i % 3
                bias_t, func, dst = gvd_cfg[pi]
                mw = wtoks + [hs[c]["xa"]]
                if gst["evac"][pi] is not None:
                    mw.append(gst["evac"][pi])
                step("tensor", mm3(pb[pi][:, 0:TC], wh, wl, xah, xal, KD),
                     waits=mw, incs=[(sT, 1)])
                t_mm = (sT, sT.bump())
                if pi < 2:
                    ew = [t_mm] + ([gst["xs"]] if gst["xs"] else [])
                else:
                    ew = [t_mm] + ([gst["ls"]] if gst["ls"] else [])
                step("scalar", lambda e, m=m, pi=pi, bias_t=bias_t, func=func,
                     dst=dst: e.activation(dst[:, :], pb[pi][:, 0:TC], func,
                                           bias=bias_t[:, m:m + 1]),
                     waits=ew, incs=[(sS, 1)])
                gst["evac"][pi] = (sS, sS.bump())
                if pi == 2:
                    spw = [(sSP, sSP.v)] if (m == 0 and c >= 1) else []
                    step("vector", lambda e, m=m: e.tensor_tensor(
                        xs_sb[:, m, :], sg_sb[:, :], tv_sb[:, :], OP.mult),
                         waits=[gst["evac"][0], gst["evac"][1]] + spw,
                         incs=[(sV, 1)])
                    gst["xs"] = (sV, sV.bump())
                    step("vector", lambda e, m=m: e.tensor_scalar(
                        a_sb[:, m, :], sl_sb[:, :], 0.998, 0.001, OP.mult, OP.add),
                         waits=[gst["evac"][2]] + spw, incs=[(sV, 1)])
                    t_a = (sV, sV.bump())
                    cwait = [t_car0] if c == 0 else []

                    def lscan(e, m=m):
                        CW("vector", e)
                        CI("vector", e.tensor_tensor_scan(
                            sl_sb[:, :], a_sb[:, m, :], xs_sb[:, m, :],
                            carry[:, m:m + 1], OP.mult, OP.add))
                        CW("vector", e)
                        return e.tensor_copy(carry[:, m:m + 1], sl_sb[:, TC - 1:TC])
                    step("vector", lscan,
                         waits=[gst["xs"], t_a, gst["evac"][2]] + cwait,
                         incs=[(sV, 1)])
                    gst["ls"] = (sV, sV.bump())
                return t_mm

            for i in range(3 * KD):
                src = (wgT, wvT, wdT)[i % 3]
                m = i // 3
                SU(src[m], KD,
                   (lambda i: lambda wh, wl, wt: gvd_consume(wh, wl, wt, i))(i))
                if c + 1 < NCH:
                    if i == 0:
                        HOOK(lambda c=c: emit_load_x(c + 1))
                    elif i == 2:
                        HOOK(lambda c=c: emit_stats(c + 1))
                    elif i == 10:
                        HOOK(lambda c=c: emit_xn_a(c + 1))
                    elif i == 16:
                        HOOK(lambda c=c: emit_xn_b(c + 1))

            def end_gvd(c=c, gst=gst):
                t_gvd_all = (sV, sV.v)
                step("scalar", lambda e, a=c * TC, b=(c + 1) * TC: e.dma_start(
                    out=xsv[:, :, a:b], in_=xs_sb[:, :, :]), waits=[t_gvd_all],
                     incs=[(sSP, 16)])
                step("scalar", lambda e, a=c * TC, b=(c + 1) * TC: e.dma_start(
                    out=av[:, :, a:b], in_=a_sb[:, :, :]), waits=[t_gvd_all],
                     incs=[(sSP, 16)])
                sSP.bump(32)
                if c == NCH - 1:
                    step("gpsimd", lambda e: e.dma_start(out=cc_in.ap(),
                                                         in_=carry[:, :]),
                         waits=[gst["ls"]], incs=[(sGD, 16)])
                    t_ccin = (sGD, sGD.bump(16))
                    step("gpsimd", lambda e: e.collective_compute(
                        "AllGather", OP.bypass,
                        replica_groups=[[0, 1], [2, 3], [4, 5], [6, 7]],
                        ins=[cc_in.ap()], outs=[cc_out.ap()]),
                         waits=[t_ccin], incs=[(sCC, 1)])
                    t_cc = (sCC, sCC.bump())
                    step("gpsimd", lambda e: e.dma_start(
                        out=hprev[:, :], in_=cc_out.ap()[0, :, :]), waits=[t_cc],
                         incs=[(sGD, 16)])
                    t_hp = (sGD, sGD.bump(16))
                    step("vector", lambda e: e.tensor_scalar(
                        carry2[:, :], hprev[:, :], selps[:, 0:1], None, OP.mult),
                         waits=[t_hp], incs=[(sV, 1)])
                    hs["c2"] = (sV, sV.bump())
            HOOK(end_gvd)

            w1st = {"h": None}

            def w1_consume(wh, wl, wtoks, m, c=c, w1st=w1st):
                mw = wtoks + [hs[c]["xnsp"]]
                if w1st["h"] is not None:
                    mw.append(w1st["h"])
                step("tensor", mm3(pb[3][:, 0:TC], wh, wl, xnh, xnl, KD),
                     waits=mw, incs=[(sT, 1)])
                t_mm = (sT, sT.bump())
                step("scalar", lambda e, m=m: e.activation(
                    hq_sb[:, m, :], pb[3][:, 0:TC], AF.Relu, bias=b1s[:, m:m + 1]),
                     waits=[t_mm], incs=[(sS, 1)])
                w1st["h"] = (sS, sS.bump())
                return t_mm

            for m in range(KH):
                SU(w1T[m], KD,
                   (lambda m: lambda wh, wl, wt: w1_consume(wh, wl, wt, m))(m))
            if c + 1 < NCH:
                HOOK(lambda c=c: emit_splits(c + 1))

            w2st = {"casts": [], "f": None, "part": None, "mm": None}

            def w2_tail(mo, t_mm, c=c, w2st=w2st):
                fw = [t_mm, t_bq]
                if w2st["part"] is not None:
                    fw.append(w2st["part"])
                step("scalar", lambda e, mo=mo: e.activation(
                    f16_sb[:, :], pb[3][:, 0:TC], AF.Identity,
                    bias=bq32[:, mo:mo + 1]), waits=fw, incs=[(sS, 1)])
                w2st["f"] = (sS, sS.bump())

                def mkpart(e, mo=mo, xb=xbufs[c % 2]):
                    CI("vector", e.tensor_copy(tmpf[:, :], f16_sb[:, :]))
                    CW("vector", e)
                    return e.tensor_tensor(xb[:, mo, :], xb[:, mo, :],
                                           tmpf[:, :], OP.add)
                step("vector", mkpart, waits=[w2st["f"]], incs=[(sV, 1)])
                w2st["part"] = (sV, sV.bump())

            def w2_consume(stga, t_ld, q, mo, c=c, w2st=w2st, w1st=w1st):
                vwt = [t_ld]
                if w2st["mm"] is not None:
                    vwt.append(w2st["mm"])
                if q == 0:
                    vwt.append((sWQ, sWQ.v))
                step("vector", lambda e, stga=stga, q=q: e.tensor_copy(
                    wq_sb[:, q * 8:(q + 1) * 8, :], stga[:, 0:8, :]),
                     waits=vwt, incs=[(sV, 1)])
                t_cast = (sV, sV.bump())
                w2st["casts"].append(t_cast)
                if q < 3:
                    return t_cast
                casts = list(w2st["casts"])
                w2st["casts"] = []
                step("scalar", lambda e, mo=mo: e.dma_start(
                    out=w2q8[mo], in_=wq_sb[:, :, :]), waits=casts,
                     incs=[(sWQ, 16)])
                sWQ.bump(16)
                mw = casts + [w1st["h"]]
                if w2st["f"] is not None:
                    mw.append(w2st["f"])

                def mmw2(e):
                    for kh in range(KH):
                        inst = e.matmul(pb[3][:, 0:TC], wq_sb[:, kh, :],
                                        hq_sb[:, kh, :], start=(kh == 0),
                                        stop=(kh == KH - 1))
                    return inst
                step("tensor", mmw2, waits=mw, incs=[(sT, 1)])
                w2st["mm"] = (sT, sT.bump())
                w2_tail(mo, w2st["mm"])
                return w2st["mm"]

            if c == 0:
                for mo in range(KD):
                    for q in range(4):
                        RU(w2T[mo][:, q * 8:(q + 1) * 8, :], 8,
                           (lambda q, mo: lambda stga, t_ld:
                            w2_consume(stga, t_ld, q, mo))(q, mo))

            def w2_cached(c=c, w2st=w2st, w1st=w1st):
                mms = {}
                for mo in range(KD):
                    wq = wq_sb if mo % 2 == 0 else wq2_sb
                    sq = sQ2[mo % 2]
                    lw = [(sWQ, sWQ.v)]
                    if mo >= 2:
                        lw.append(mms[mo - 2])
                    else:
                        lw.append(hs[c - 1]["w2mm"][mo - 2 + KD])
                    step("sync", lambda e, mo=mo, wq=wq, sq=sq: e.dma_start(
                        out=wq[:, :, :], in_=w2q8[mo]).then_inc(sq.h, 16),
                         waits=lw)
                    t_l = (sq, sq.bump(16))
                    mw = [t_l, w1st["h"]]
                    if w2st["f"] is not None:
                        mw.append(w2st["f"])

                    def mmw2(e, wq=wq):
                        for kh in range(KH):
                            inst = e.matmul(pb[3][:, 0:TC], wq[:, kh, :],
                                            hq_sb[:, kh, :], start=(kh == 0),
                                            stop=(kh == KH - 1))
                        return inst
                    step("tensor", mmw2, waits=mw, incs=[(sT, 1)])
                    w2st["mm"] = (sT, sT.bump())
                    mms[mo] = w2st["mm"]
                    w2_tail(mo, w2st["mm"])
                hs[c]["w2mm"] = mms

            def end_w2(c=c, w2st=w2st):
                if c == 0:
                    hs[0]["w2mm"] = {6: w2st["mm"], 7: w2st["mm"]}
                if c >= 1:
                    w2_cached()
                sst = sSTp[c % 2]
                step("scalar", lambda e, a=c * TC, b=(c + 1) * TC,
                     xb=xbufs[c % 2], sst=sst:
                     e.dma_start(out=outv[:, :, a:b], in_=xb[:, :, :]),
                     waits=[w2st["part"]], incs=[(sst, 16)])
                hs[c]["pstore"] = (sst, sst.bump(16))
            HOOK(end_w2)

        emit_load_x(0)
        emit_stats(0)
        emit_xn_a(0)
        emit_xn_b(0)
        emit_splits(0)
        for c in range(NCH):
            add_chunk1(c)
        run_stream()

        emit_block()
        p1.close()

        p3 = contextlib.ExitStack()
        sb3 = lambda name, shape, dt: p3.enter_context(nc.sbuf_tensor(name, shape, dt))
        q_xn = sb3("q_xn", [128, KD, TC], F32)
        q_xs = sb3("q_xs", [128, KD, TC], F32)
        q_a = sb3("q_a", [128, KD, TC], F32)
        q_pa = sb3("q_pa", [128, KD, TC], F32)
        q_s = sb3("q_s", [128, KD, TC], F32)
        xbh = sb3("xbh", [128, KD, TC], BF16)
        xbl = sb3("xbl", [128, KD, TC], BF16)
        pmh = sb3("pmh", [128, KF, TC], BF16)
        pml = sb3("pml", [128, KF, TC], BF16)

        h3 = {}

        def p3_boundary():
            return [(sXN, sXN.v), (sSP, sSP.v), (sT, sT.v), (sS, sS.v),
                    (sV, sV.v), (sSTp[0], sSTp[0].v), (sSTp[1], sSTp[1].v)]

        def p3_load_xsa(c):
            h = h3.setdefault(c, {})
            lw = p3_boundary() if c == 0 else [h3[c - 1]["scan"]]
            step("sync", lambda e, a=c * TC, b=(c + 1) * TC: e.dma_start(
                out=q_xs[:, :, :], in_=xsv[:, :, a:b]), waits=lw, incs=[(sP3A, 16)])
            sP3A.bump(16)
            step("sync", lambda e, a=c * TC, b=(c + 1) * TC: e.dma_start(
                out=q_a[:, :, :], in_=av[:, :, a:b]), incs=[(sP3A, 16)])
            sP3A.bump(16)
            h["lxa"] = (sP3A, sP3A.v)

        def p3_scans(c):
            h = h3[c]
            sw = [h["lxa"]]
            if c == 0:
                sw.append(hs["c2"])
            else:
                sw.append(h3[c - 1]["part2"])

            def scans(e):
                for m in range(KD):
                    CW("vector", e)
                    CI("vector", e.tensor_tensor_scan(
                        q_s[:, m, :], q_a[:, m, :], q_xs[:, m, :],
                        carry2[:, m:m + 1], OP.mult, OP.add))
                    CW("vector", e)
                    inst = e.tensor_copy(carry2[:, m:m + 1], q_s[:, m, TC - 1:TC])
                    if m < KD - 1:
                        CI("vector", inst)
                return inst
            step("vector", scans, waits=sw, incs=[(sV, 1)])
            h["scan"] = (sV, sV.bump())

        def p3_load_xn(c):
            h = h3[c]
            lw = p3_boundary() if c == 0 else [h3[c - 1]["part1"]]
            step("sync", lambda e, a=c * TC, b=(c + 1) * TC: e.dma_start(
                out=q_xn[:, :, :], in_=xnv[:, :, a:b]), waits=lw, incs=[(sP3X, 16)])
            h["lxn"] = (sP3X, sP3X.bump(16))

        def p3_part1(c):
            h = h3[c]
            step("vector", lambda e: e.tensor_tensor(
                q_s[:, :, :], q_xn[:, :, :], q_s[:, :, :], OP.add),
                 waits=[h["lxn"], h["scan"]], incs=[(sV, 1)])
            h["part1"] = (sV, sV.bump())

        def p3_load_pa(c):
            h = h3.setdefault(c, {})
            lw = [(sSTp[0], sSTp[0].v), (sSTp[1], sSTp[1].v)]
            step("sync", lambda e, a=c * TC, b=(c + 1) * TC: e.dma_start(
                out=q_pa[:, :, :], in_=outv[:, :, a:b]), waits=lw, incs=[(sP3P, 16)])
            h["lpa"] = (sP3P, sP3P.bump(16))

        def p3_part2(c):
            h = h3[c]
            step("vector", lambda e: e.tensor_tensor(
                q_pa[:, :, :], q_pa[:, :, :], q_s[:, :, :], OP.add),
                 waits=[h["lpa"], h["part1"]], incs=[(sV, 1)])
            h["part2"] = (sV, sV.bump())

        def p3_stats(c):
            h = h3[c]
            t_sq3 = []
            fw1 = [h3[c - 1]["fin1"]] if c >= 1 else []
            for kc in range(KD):
                scr = sl_sb if kc % 2 == 0 else sq2_sb
                w_ = [h["part1"]] + fw1 if kc < 2 else [(sT, t_sq3[kc - 2][1])]
                step("scalar", lambda e, kc=kc, scr=scr: e.activation(
                    scr[:, :], q_s[:, kc, :], AF.Square), waits=w_, incs=[(sS, 1)])
                t_s3 = (sS, sS.bump())
                step("tensor", lambda e, kc=kc, scr=scr: e.matmul(
                    pb[2][0:1, 0:TC], ones[:, 0:1], scr[:, :],
                    start=(kc == 0), stop=(kc == KD - 1)), waits=[t_s3],
                     incs=[(sT, 1)])
                t_sq3.append((sT, sT.bump()))
            step("scalar", lambda e: e.copy(row(0), pb[2][0:1, 0:TC]),
                 waits=[t_sq3[-1]], incs=[(sS, 1)])
            t_ev3 = (sS, sS.bump())
            step("vector", lambda e: e.tensor_scalar(
                row(1), row(0), INV_D, 1e-6, OP.mult, OP.add),
                 waits=[t_ev3], incs=[(sV, 1)])
            t_u3 = (sV, sV.bump())
            step("scalar", lambda e: e.activation(row(2), row(1), AF.Sqrt),
                 waits=[t_u3], incs=[(sS, 1)])
            t_y3 = (sS, sS.bump())

            def rows3(e):
                ops = [
                    lambda: e.reciprocal(row(3), row(2)),
                    lambda: e.tensor_tensor(row(3), row(1), row(3), OP.mult),
                    lambda: e.tensor_tensor(row(3), row(2), row(3), OP.add),
                    lambda: e.tensor_scalar(row(3), row(3), 0.5, None, OP.mult),
                ]
                for op in ops:
                    CI("vector", op())
                    CW("vector", e)
                return e.reciprocal(row(4), row(3))
            step("vector", rows3, waits=[t_y3], incs=[(sV, 1)])
            t_r3 = (sV, sV.bump())
            step("tensor", lambda e: e.matmul(pb[3][:, 0:TC], ones[0:1, :],
                                              row(4), start=True, stop=True),
                 waits=[t_r3], incs=[(sT, 1)])
            h["bc3"] = (sT, sT.bump())

        def p3_xbsplit(c):
            h = h3[c]
            t_xbl = None
            for kc in range(KD):
                scr = sg_sb if kc % 2 == 0 else tv_sb
                aw = [h["bc3"]]
                if t_xbl is None:
                    aw.append((sS, sS.v))
                step("vector", lambda e, kc=kc, scr=scr: e.scalar_tensor_tensor(
                    scr[:, :], q_s[:, kc, :], wffns[:, kc:kc + 1],
                    pb[3][:, 0:TC], OP.mult, OP.mult),
                     waits=aw + ([t_xbl] if t_xbl else []), incs=[(sV, 1)])
                t_tmp = (sV, sV.bump())
                t_xbl = act_split(scr[:, :], xbh[:, kc, :], xbl[:, kc, :], [t_tmp])
            h["xb"] = t_xbl

        def add_chunk3(c):
            g3 = {"sil": None, "pm": None, "wo": None, "pml": None, "sw2": None}

            def g3_consume(wh, wl, wtoks, i, c=c, g3=g3):
                m, r = i // 3, i % 3
                if r == 0:
                    mw = wtoks + [h3[c]["xb"]]
                    if g3["sil"] is not None:
                        mw.append(g3["sil"])
                    if m == 0 and c >= 1:
                        mw.append(h3[c - 1]["fin2"])
                    step("tensor", mm3(pb[0][:, 0:TC], wh, wl, xbh, xbl, KD),
                         waits=mw, incs=[(sT, 1)])
                    t_g = (sT, sT.bump())
                    sw2 = [t_g] + ([g3["pm"]] if g3["pm"] else [])
                    step("scalar", lambda e: e.activation(
                        sg_sb[:, :], pb[0][:, 0:TC], AF.Silu),
                         waits=sw2, incs=[(sS, 1)])
                    g3["sil"] = (sS, sS.bump())
                    return t_g
                if r == 1:
                    mw2 = wtoks + [h3[c]["xb"]]
                    if g3["pm"] is not None:
                        mw2.append(g3["pm"])
                    if m == 0 and c >= 1:
                        mw2.append(h3[c - 1]["fin2"])
                    step("tensor", mm3(pb[1][:, 0:TC], wh, wl, xbh, xbl, KD),
                         waits=mw2, incs=[(sT, 1)])
                    t_u = (sT, sT.bump())
                    step("vector", lambda e: e.tensor_tensor(
                        tmpf[:, :], sg_sb[:, :], pb[1][:, 0:TC], OP.mult),
                         waits=[g3["sil"], t_u] + ([g3["pml"]] if g3["pml"] else []),
                         incs=[(sV, 1)])
                    g3["pm"] = (sV, sV.bump())
                    g3["pml"] = act_split(tmpf[:, :], pmh[:, m, :], pml[:, m, :],
                                          [g3["pm"]])
                    return t_u

                def mmwo(e, wh=wh, wl=wl, m=m):
                    for mo in range(6):
                        e.matmul(pb[2 + mo][:, 0:TC], wh[:, mo, :], pmh[:, m, :],
                                 start=(m == 0), stop=False)
                        e.matmul(pb[2 + mo][:, 0:TC], wh[:, mo, :], pml[:, m, :],
                                 start=False, stop=False)
                        inst = e.matmul(pb[2 + mo][:, 0:TC], wl[:, mo, :],
                                        pmh[:, m, :], start=False,
                                        stop=(m == KF - 1))
                    return inst
                wo_w = wtoks + [g3["pml"]]
                if m == 0 and c >= 1:
                    wo_w.append(h3[c - 1]["fin1"])
                step("tensor", mmwo, waits=wo_w, incs=[(sT, 1)])
                g3["wo"] = (sT, sT.bump())
                return g3["wo"]

            for i in range(3 * KF):
                m, r = i // 3, i % 3
                if r == 0:
                    src, ns = wgateT[m], KD
                elif r == 1:
                    src, ns = wupT[m], KD
                else:
                    src = woutv[m, :, 0:768].rearrange("p (s q) -> p s q", q=128)
                    ns = 6
                SU(src, ns,
                   (lambda i: lambda wh, wl, wt: g3_consume(wh, wl, wt, i))(i))
                if c + 1 < NCH:
                    if i == 2:
                        HOOK(lambda c=c: p3_load_xsa(c + 1))
                    elif i == 6:
                        HOOK(lambda c=c: p3_scans(c + 1))
                    elif i == 40:
                        HOOK(lambda c=c: p3_load_xn(c + 1))
                    elif i == 44:
                        HOOK(lambda c=c: p3_part1(c + 1))

            def fin1(c=c, g3=g3):
                def finev1(e):
                    for mo in range(6):
                        inst = e.tensor_tensor(q_pa[:, mo, :], q_pa[:, mo, :],
                                               pb[2 + mo][:, 0:TC], OP.add)
                    return inst
                step("vector", finev1, waits=[g3["wo"], h3[c]["part2"]],
                     incs=[(sV, 1)])
                h3[c]["fin1"] = (sV, sV.bump())
                if c + 1 < NCH:
                    p3_stats(c + 1)
            HOOK(fin1)

            def sw2_consume(wh, wl, wtoks, m, c=c, g3=g3):
                mw3 = list(wtoks)
                if m == 0:
                    mw3 += [g3["pm"], g3["sil"], g3["pml"], h3[c]["fin1"]]

                def mm2(e, wh=wh, wl=wl, m=m):
                    for mo in range(2):
                        e.matmul(pb[mo][:, 0:TC], wh[:, mo, :], pmh[:, m, :],
                                 start=(m == 0), stop=False)
                        e.matmul(pb[mo][:, 0:TC], wh[:, mo, :], pml[:, m, :],
                                 start=False, stop=False)
                        inst = e.matmul(pb[mo][:, 0:TC], wl[:, mo, :],
                                        pmh[:, m, :], start=False,
                                        stop=(m == KF - 1))
                    return inst
                step("tensor", mm2, waits=mw3, incs=[(sT, 1)])
                g3["sw2"] = (sT, sT.bump())
                return g3["sw2"]

            for m in range(KF):
                SU(woutv[m, :, 768:1024].rearrange("p (s q) -> p s q", q=128), 2,
                   (lambda m: lambda wh, wl, wt: sw2_consume(wh, wl, wt, m))(m))
                if m == 2 and c + 1 < NCH:
                    HOOK(lambda c=c: p3_xbsplit(c + 1))

            def fin2(c=c, g3=g3):
                def finev2(e):
                    CI("vector", e.tensor_tensor(q_pa[:, 6, :], q_pa[:, 6, :],
                                                 pb[0][:, 0:TC], OP.add))
                    return e.tensor_tensor(q_pa[:, 7, :], q_pa[:, 7, :],
                                           pb[1][:, 0:TC], OP.add)
                step("vector", finev2, waits=[g3["sw2"]], incs=[(sV, 1)])
                h3[c]["fin2"] = (sV, sV.bump())
                sst = sSTp[c % 2]
                step("scalar", lambda e, a=c * TC, b=(c + 1) * TC, sst=sst:
                     e.dma_start(out=outv[:, :, a:b], in_=q_pa[:, :, :]),
                     waits=[h3[c]["fin1"], h3[c]["fin2"]], incs=[(sst, 16)])
                sst.bump(16)
                if c + 1 < NCH:
                    p3_load_pa(c + 1)
                    p3_part2(c + 1)
            HOOK(fin2)

        p3_load_xsa(0)
        p3_scans(0)
        p3_load_xn(0)
        p3_part1(0)
        p3_load_pa(0)
        p3_part2(0)
        p3_stats(0)
        p3_xbsplit(0)
        for c in range(NCH):
            add_chunk3(c)
        run_stream()

        step("gpsimd", None, waits=[(sSTp[0], sSTp[0].v), (sSTp[1], sSTp[1].v)])
        step("sync", None, waits=[(sSTp[0], sSTp[0].v), (sSTp[1], sSTp[1].v)])

        emit_block()
        p3.close()

    return nc


def kernel(L0_attn_x, L0_attn_rms_mix_w, L0_attn_rms_ffn_w, L0_attn_Wg, L0_attn_bg,
           L0_attn_Wv, L0_attn_bv, L0_attn_Wd, L0_attn_bd, L0_attn_W_gate,
           L0_attn_W_up, L0_attn_W_out, L0_ffn_w1, L0_ffn_b1, L0_ffn_w2, L0_ffn_b2):
    x = np.asarray(L0_attn_x, dtype=np.float32)

    def t(a):
        return np.ascontiguousarray(np.asarray(a, dtype=np.float32).T)

    def tw(a):
        w = np.asarray(a, dtype=np.float32)
        M, K = w.shape
        return np.ascontiguousarray(
            w.reshape(M // 128, 128, K // 128, 128).transpose(0, 3, 2, 1))

    def b2(a, k):
        return np.ascontiguousarray(np.asarray(a, dtype=np.float32).reshape(k, 128).T)

    shared = {
        "wgT": tw(L0_attn_Wg), "wvT": tw(L0_attn_Wv), "wdT": tw(L0_attn_Wd),
        "wgateT": tw(L0_attn_W_gate), "wupT": tw(L0_attn_W_up),
        "woutT": t(L0_attn_W_out),
        "w1T": tw(L0_ffn_w1), "w2T": tw(L0_ffn_w2),
        "bg2": b2(L0_attn_bg, KD), "bv2": b2(L0_attn_bv, KD), "bd2": b2(L0_attn_bd, KD),
        "b12": b2(L0_ffn_b1, KH), "b22": b2(L0_ffn_b2, KD),
        "wmix2": b2(L0_attn_rms_mix_w, KD), "wffn2": b2(L0_attn_rms_ffn_w, KD),
    }
    in_maps = []
    for core in range(8):
        b, h = core // 2, core % 2
        xc = np.ascontiguousarray(x[b, h * T:(h + 1) * T, :].T)
        d = dict(shared)
        d["xT"] = xc
        d["selp"] = np.full((128, 1), float(h), dtype=np.float32)
        in_maps.append(d)

    nc = build_nc()
    res = run_bass_kernel_spmd(nc, in_maps, core_ids=list(range(8)))
    out = np.empty((B, S, D), dtype=np.float32)
    for core in range(8):
        b, h = core // 2, core % 2
        out[b, h * T:(h + 1) * T, :] = res.results[core]["outT"].T
    return out


# version 17
# speedup vs baseline: 1.0607x; 1.0150x over previous
import contextlib

import numpy as np

import concourse.bass as bass
import concourse.mybir as mybir
from concourse.bass_utils import run_bass_kernel_spmd

F32 = mybir.dt.float32
BF16 = mybir.dt.bfloat16
FP8 = mybir.dt.float8e4
F16 = mybir.dt.float16
AF = mybir.ActivationFunctionType
OP = mybir.AluOpType

B, S, D, F, H = 4, 4096, 1024, 3072, 4096
T = 2048
KD, KF, KH = D // 128, F // 128, H // 128
TC, NCH = 512, 4
INV_D = 1.0 / D
PIPE_SLOTS = 8
NPIPE = 4
DEPTH = 3


class Sem:
    def __init__(self, nc, st, name):
        self.h = st.enter_context(nc.semaphore(name))
        self.v = 0

    def bump(self, n=1):
        self.v += n
        return self.v


def build_nc():
    nc = bass.Bass("TRN2", target_bir_lowering=False, debug=False)

    xT = nc.dram_tensor("xT", [D, T], F32, kind="ExternalInput").ap()
    wgT = nc.dram_tensor("wgT", [KD, 128, KD, 128], F32, kind="ExternalInput").ap()
    wvT = nc.dram_tensor("wvT", [KD, 128, KD, 128], F32, kind="ExternalInput").ap()
    wdT = nc.dram_tensor("wdT", [KD, 128, KD, 128], F32, kind="ExternalInput").ap()
    wgateT = nc.dram_tensor("wgateT", [KF, 128, KD, 128], F32, kind="ExternalInput").ap()
    wupT = nc.dram_tensor("wupT", [KF, 128, KD, 128], F32, kind="ExternalInput").ap()
    woutT = nc.dram_tensor("woutT", [F, D], F32, kind="ExternalInput").ap()
    w1T = nc.dram_tensor("w1T", [KH, 128, KD, 128], F32, kind="ExternalInput").ap()
    w2T = nc.dram_tensor("w2T", [KD, 128, KH, 128], F32, kind="ExternalInput").ap()
    w2q8 = nc.dram_tensor("w2q8", [KD, 128, KH, 128], FP8).ap()
    bg2 = nc.dram_tensor("bg2", [128, KD], F32, kind="ExternalInput").ap()
    bv2 = nc.dram_tensor("bv2", [128, KD], F32, kind="ExternalInput").ap()
    bd2 = nc.dram_tensor("bd2", [128, KD], F32, kind="ExternalInput").ap()
    b12 = nc.dram_tensor("b12", [128, KH], F32, kind="ExternalInput").ap()
    b22 = nc.dram_tensor("b22", [128, KD], F32, kind="ExternalInput").ap()
    wmix2 = nc.dram_tensor("wmix2", [128, KD], F32, kind="ExternalInput").ap()
    wffn2 = nc.dram_tensor("wffn2", [128, KD], F32, kind="ExternalInput").ap()
    selp = nc.dram_tensor("selp", [128, 1], F32, kind="ExternalInput").ap()

    outT = nc.dram_tensor("outT", [D, T], F32, kind="ExternalOutput").ap()

    xn_h = nc.dram_tensor("xn_h", [D, T], F32).ap()
    xs_h = nc.dram_tensor("xs_h", [D, T], F32).ap()
    a_h = nc.dram_tensor("a_h", [D, T], F32).ap()
    cc_in = nc.dram_tensor("cc_in", [128, KD], F32)
    cc_out = nc.dram_tensor("cc_out", [2, 128, KD], F32)

    def fm(ap_):
        return ap_.rearrange("(k p) t -> p k t", p=128)

    xTv, xnv, xsv, av, outv = fm(xT), fm(xn_h), fm(xs_h), fm(a_h), fm(outT)
    woutv = woutT.rearrange("(k p) m -> k p m", p=128)

    P = []

    def step(eng, body=None, waits=(), incs=()):
        waits = [(s.h, v) for (s, v) in waits]
        incs_ = [(s.h, n) for (s, n) in incs]

        def fn(e):
            for h, v in waits:
                e.wait_ge(h, v)
            inst = body(e) if body is not None else None
            if incs_:
                if inst is None:
                    assert body is None
                    inst = e.nop()
                for h, n in incs_:
                    inst.then_inc(h, n)

        P.append((eng, fn))

    def emit_block():
        with nc.Block() as blk:
            for eng_name in ("sync", "scalar", "vector", "tensor", "gpsimd"):
                def make(eng_name):
                    def emit(e):
                        for en, fn in P:
                            if en == eng_name:
                                fn(e)
                    return emit
                getattr(blk, eng_name)(make(eng_name))
        P.clear()

    with contextlib.ExitStack() as st:
        sb = lambda name, shape, dt: st.enter_context(nc.sbuf_tensor(name, shape, dt))

        ones = sb("ones", [128, 128], F32)
        bgs = sb("bgs", [128, KD], F32)
        bvs = sb("bvs", [128, KD], F32)
        bds = sb("bds", [128, KD], F32)
        b1s = sb("b1s", [128, KH], F32)
        b2s = sb("b2s", [128, KD], F32)
        bq16 = sb("bq16", [128, KD], F16)
        bq32 = sb("bq32", [128, KD], F32)
        wmixs = sb("wmixs", [128, KD], F32)
        wffns = sb("wffns", [128, KD], F32)
        selps = sb("selps", [128, 1], F32)
        carry = sb("carry", [128, KD], F32)
        carry2 = sb("carry2", [128, KD], F32)
        hprev = sb("hprev", [128, KD], F32)
        sg_sb = sb("sg_sb", [128, TC], F32)
        tv_sb = sb("tv_sb", [128, TC], F32)
        sl_sb = sb("sl_sb", [128, TC], F32)
        sq2_sb = sb("sq2_sb", [128, TC], F32)
        tmpf = sb("tmpf", [128, TC], F32)
        f16_sb = sb("f16_sb", [128, TC], F16)
        stat = sb("stat", [1, 7 * TC], F32)
        stg = [sb(f"stg{i}", [128, PIPE_SLOTS, 128], F32) for i in range(NPIPE)]
        wsh = [sb(f"wsh{i}", [128, PIPE_SLOTS, 128], BF16) for i in range(NPIPE)]
        wsl = [sb(f"wsl{i}", [128, PIPE_SLOTS, 128], BF16) for i in range(NPIPE)]

        pb = [st.enter_context(nc.psum_tensor(f"pb{i}", [128, 512], F32))
              for i in range(8)]

        sBIAS = Sem(nc, st, "sBIAS")
        sX2 = [Sem(nc, st, "sXa"), Sem(nc, st, "sXb")]
        sWp = [Sem(nc, st, f"sW{i}") for i in range(NPIPE)]
        sP3A = Sem(nc, st, "sP3A")
        sP3X = Sem(nc, st, "sP3X")
        sP3P = Sem(nc, st, "sP3P")
        sSTp = [Sem(nc, st, "sSTa"), Sem(nc, st, "sSTb")]
        sXN = Sem(nc, st, "sXN")
        sSP = Sem(nc, st, "sSP")
        sGD = Sem(nc, st, "sGD")
        sWQ = Sem(nc, st, "sWQ")
        sQ2 = [Sem(nc, st, "sQ2a"), Sem(nc, st, "sQ2b")]
        sT = Sem(nc, st, "sT")
        sV = Sem(nc, st, "sV")
        sS = Sem(nc, st, "sS")
        sG = Sem(nc, st, "sG")
        sCC = Sem(nc, st, "sCC")
        chain = {n: Sem(nc, st, f"ch_{n[:3]}") for n in ("vector", "scalar", "gpsimd")}

        def CI(en, inst):
            s = chain[en]
            s.v += 1
            inst.then_inc(s.h, 1)
            return inst

        def CW(en, e):
            s = chain[en]
            e.wait_ge(s.h, s.v)

        def row(i, n=TC):
            return stat[0:1, i * TC: i * TC + n]

        pipe_state = {i: {"free": []} for i in range(NPIPE)}
        pipe_par = [0]

        def pipe_load(src_ap, nslots):
            par = pipe_par[0]
            pipe_par[0] = (par + 1) % NPIPE
            ps = pipe_state[par]
            sw = sWp[par]
            stga, wha, wla = stg[par], wsh[par], wsl[par]
            step("sync", lambda e: e.dma_start(
                out=stga[:, 0:nslots, :], in_=src_ap).then_inc(sw.h, 16),
                 waits=list(ps["free"]))
            t_ld = (sw, sw.bump(16))
            step("scalar", lambda e: e.activation(
                wha[:, 0:nslots, :], stga[:, 0:nslots, :], AF.Copy),
                 waits=[t_ld], incs=[(sS, 1)])
            t_hi = (sS, sS.bump())
            step("vector", lambda e: e.tensor_tensor(
                wla[:, 0:nslots, :], stga[:, 0:nslots, :], wha[:, 0:nslots, :],
                OP.subtract), waits=[t_hi], incs=[(sV, 1)])
            t_lo = (sV, sV.bump())
            ps["free"] = [t_lo]
            return ("split", wha, wla, [t_hi, t_lo], par)

        def pipe_load_raw(src_ap, nslots):
            par = pipe_par[0]
            pipe_par[0] = (par + 1) % NPIPE
            ps = pipe_state[par]
            sw = sWp[par]
            stga = stg[par]
            step("sync", lambda e: e.dma_start(
                out=stga[:, 0:nslots, :], in_=src_ap).then_inc(sw.h, 16),
                 waits=list(ps["free"]))
            t_ld = (sw, sw.bump(16))
            ps["free"] = []
            return ("raw", stga, t_ld, None, par)

        stream = []
        hooks_after = {}

        def SU(src, nslots, consume):
            stream.append((False, src, nslots, consume))
            return len(stream) - 1

        def RU(src, nslots, consume):
            stream.append((True, src, nslots, consume))
            return len(stream) - 1

        def HOOK(fn):
            hooks_after.setdefault(len(stream) - 1, []).append(fn)

        def run_stream():
            n = len(stream)
            handles = {}

            def load(j):
                is_raw, src, ns, _ = stream[j]
                return pipe_load_raw(src, ns) if is_raw else pipe_load(src, ns)

            for j in range(min(DEPTH, n)):
                handles[j] = load(j)
            for i in range(n):
                kind, a1, a2, a3, par = handles.pop(i)
                consume = stream[i][3]
                if kind == "split":
                    tok = consume(a1, a2, a3)
                else:
                    tok = consume(a1, a2)
                pipe_state[par]["free"].append(tok)
                for fn in hooks_after.get(i, []):
                    fn()
                j = i + DEPTH
                if j < n:
                    handles[j] = load(j)
            stream.clear()
            hooks_after.clear()

        def mm3(ps_ap, wh, wl, xh, xl, nk):
            def body(e):
                for kc in range(nk):
                    e.matmul(ps_ap, wh[:, kc, :], xh[:, kc, :],
                             start=(kc == 0), stop=False)
                for kc in range(nk):
                    e.matmul(ps_ap, wh[:, kc, :], xl[:, kc, :],
                             start=False, stop=False)
                for kc in range(nk):
                    inst = e.matmul(ps_ap, wl[:, kc, :], xh[:, kc, :],
                                    start=False, stop=(kc == nk - 1))
                return inst
            return body

        def act_split(src_ap, hi_ap, lo_ap, waits):
            step("scalar", lambda e: e.activation(hi_ap, src_ap, AF.Copy),
                 waits=waits, incs=[(sS, 1)])
            t_hi = (sS, sS.bump())
            step("vector", lambda e: e.tensor_tensor(lo_ap, src_ap, hi_ap,
                                                     OP.subtract),
                 waits=[t_hi], incs=[(sV, 1)])
            return (sV, sV.bump())

        step("vector", lambda e: e.memset(ones[:, :], 1.0), incs=[(sV, 1)])
        t_ones = (sV, sV.bump())
        step("vector", lambda e: e.memset(carry[:, :], 0.0), incs=[(sV, 1)])
        t_car0 = (sV, sV.bump())
        step("vector", lambda e: e.memset(row(6), 1.0), incs=[(sV, 1)])
        t_onerow = (sV, sV.bump())

        for dst, src in ((bgs, bg2), (bvs, bv2), (bds, bd2), (b1s, b12),
                         (b2s, b22), (wmixs, wmix2), (wffns, wffn2), (selps, selp)):
            step("sync", lambda e, d=dst, s=src: e.dma_start(out=d[:, :], in_=s),
                 incs=[(sBIAS, 16)])
            sBIAS.bump(16)
        t_bias = (sBIAS, sBIAS.v)

        def mk_bq(e):
            CI("scalar", e.activation(bq16[:, :], b2s[:, :], AF.Copy))
            CW("scalar", e)
            return e.activation(bq32[:, :], bq16[:, :], AF.Copy)
        step("scalar", mk_bq, waits=[t_bias], incs=[(sS, 1)])
        t_bq = (sS, sS.bump())

        p1 = contextlib.ExitStack()
        sb1 = lambda name, shape, dt: p1.enter_context(nc.sbuf_tensor(name, shape, dt))
        xb0 = sb1("xb0", [128, KD, TC], F32)
        xb1 = sb1("xb1", [128, KD, TC], F32)
        xbufs = [xb0, xb1]
        xn_sb = sb1("xn_sb", [128, KD, TC], F32)
        xnh = sb1("xnh", [128, KD, TC], BF16)
        xnl = sb1("xnl", [128, KD, TC], BF16)
        xah = sb1("xah", [128, KD, TC], BF16)
        xal = sb1("xal", [128, KD, TC], BF16)
        xs_sb = sb1("xs_sb", [128, KD, TC], F32)
        a_sb = sb1("a_sb", [128, KD, TC], F32)
        hq_sb = sb1("hq_sb", [128, KH, TC], FP8)
        wq_sb = sb1("wq_sb", [128, KH, 128], FP8)
        wq2_sb = sb1("wq2_sb", [128, KH, 128], FP8)

        hs = {}

        def emit_load_x(c):
            h = hs.setdefault(c, {})
            sx = sX2[c % 2]
            xw = []
            if c >= 2:
                xw.append((sSTp[c % 2], sSTp[c % 2].v))
            step("sync", lambda e, a=c * TC, b=(c + 1) * TC, xb=xbufs[c % 2]:
                 e.dma_start(out=xb[:, :, :], in_=xTv[:, :, a:b]),
                 waits=xw, incs=[(sx, 16)])
            h["x"] = (sx, sx.bump(16))

        def emit_stats(c):
            h = hs[c]
            xb = xbufs[c % 2]
            pw = []
            if c >= 1:
                pw += [(sXN, sXN.v), hs[c - 1]["xnsp"], hs[c - 1]["xa"]]
            step("tensor", lambda e: [e.matmul(pb[4][0:1, 0:TC], ones[:, 0:1],
                                               xb[:, kc, :], start=(kc == 0),
                                               stop=(kc == KD - 1))
                                      for kc in range(KD)][-1],
                 waits=[h["x"], t_ones], incs=[(sT, 1)])
            t_sumx = (sT, sT.bump())
            t_sq = []
            for kc in range(KD):
                scr = xn_sb[:, kc % 2, :]
                w_ = [h["x"]] + pw if kc < 2 else [(sT, t_sq[kc - 2][1])]
                step("scalar", lambda e, kc=kc, scr=scr: e.activation(
                    scr, xb[:, kc, :], AF.Square), waits=w_, incs=[(sS, 1)])
                t_s = (sS, sS.bump())
                step("tensor", lambda e, kc=kc, scr=scr: e.matmul(
                    pb[5][0:1, 0:TC], ones[:, 0:1], scr,
                    start=(kc == 0), stop=(kc == KD - 1)),
                     waits=[t_s], incs=[(sT, 1)])
                t_sq.append((sT, sT.bump()))
            t_sumsq = t_sq[-1]

            ew = [t_sumx, t_sumsq]
            if c >= 1:
                ew += [hs[c - 1]["bc1"], hs[c - 1]["bc2"]]

            def evrows(e):
                CI("scalar", e.copy(row(0), pb[4][0:1, 0:TC]))
                return e.copy(row(1), pb[5][0:1, 0:TC])
            step("scalar", evrows, waits=ew, incs=[(sS, 1)])
            t_ev = (sS, sS.bump())

            def rows1(e):
                ops = [
                    lambda: e.tensor_scalar(row(0), row(0), INV_D, None, OP.mult),
                    lambda: e.tensor_scalar(row(1), row(1), INV_D, None, OP.mult),
                    lambda: e.tensor_tensor(row(2), row(0), row(0), OP.mult),
                    lambda: e.tensor_tensor(row(1), row(1), row(2), OP.subtract),
                    lambda: e.tensor_scalar(row(2), row(1), 1e-5, None, OP.add),
                    lambda: e.tensor_scalar(row(3), row(2), 1e-6, None, OP.mult),
                ]
                for op in ops:
                    CI("vector", op())
                    CW("vector", e)
                return e.tensor_tensor(row(3), row(1), row(3), OP.add)
            step("vector", rows1, waits=[t_ev, t_onerow], incs=[(sV, 1)])
            t_r1 = (sV, sV.bump())

            def sqrts(e):
                CI("scalar", e.activation(row(1), row(2), AF.Sqrt))
                return e.activation(row(4), row(3), AF.Sqrt)
            step("scalar", sqrts, waits=[t_r1], incs=[(sS, 1)])
            t_sqr = (sS, sS.bump())

            def rows2(e):
                ops = [
                    lambda: e.reciprocal(row(5), row(1)),
                    lambda: e.tensor_tensor(row(5), row(2), row(5), OP.mult),
                    lambda: e.tensor_tensor(row(5), row(1), row(5), OP.add),
                    lambda: e.tensor_scalar(row(5), row(5), 0.5, None, OP.mult),
                    lambda: e.reciprocal(row(1), row(5)),
                    lambda: e.reciprocal(row(5), row(4)),
                    lambda: e.tensor_tensor(row(5), row(3), row(5), OP.mult),
                    lambda: e.tensor_tensor(row(5), row(4), row(5), OP.add),
                    lambda: e.tensor_scalar(row(5), row(5), 0.5, None, OP.mult),
                    lambda: e.reciprocal(row(4), row(5)),
                    lambda: e.tensor_tensor(row(0), row(0), row(1), OP.mult),
                    lambda: e.tensor_tensor(row(5), row(2), row(1), OP.mult),
                ]
                for op in ops:
                    CI("vector", op())
                    CW("vector", e)
                return e.tensor_tensor(row(5), row(5), row(4), OP.mult)
            step("vector", rows2, waits=[t_sqr], incs=[(sV, 1)])
            h["rows"] = (sV, sV.bump())

        def emit_xn_a(c):
            h = hs[c]
            xb = xbufs[c % 2]
            bw = [h["rows"]]
            if c >= 1:
                bw += [hs[c - 1]["xa"]]
            step("tensor", lambda e: (e.matmul(pb[6][:, 0:TC], ones[0:1, :], row(1),
                                               start=True, stop=True),
                                      e.matmul(pb[7][:, 0:TC], ones[0:1, :], row(0),
                                               start=True, stop=True))[-1],
                 waits=bw, incs=[(sT, 1)])
            h["bc1"] = (sT, sT.bump())

            def mk_xn(e):
                for kc in range(KD):
                    inst = e.tensor_tensor(xn_sb[:, kc, :], xb[:, kc, :],
                                           pb[6][:, 0:TC], OP.mult)
                CI("vector", inst)
                CW("vector", e)
                for kc in range(KD):
                    inst = e.tensor_tensor(xn_sb[:, kc, :], xn_sb[:, kc, :],
                                           pb[7][:, 0:TC], OP.subtract)
                return inst
            step("vector", mk_xn, waits=[h["bc1"], h["x"]], incs=[(sV, 1)])
            h["xn"] = (sV, sV.bump())
            step("scalar", lambda e, a=c * TC, b=(c + 1) * TC: e.dma_start(
                out=xnv[:, :, a:b], in_=xn_sb[:, :, :]), waits=[h["xn"]],
                 incs=[(sXN, 16)])
            h["spill"] = (sXN, sXN.bump(16))

        def emit_xn_b(c):
            h = hs[c]
            step("tensor", lambda e: e.matmul(pb[6][:, 0:TC], ones[0:1, :], row(5),
                                              start=True, stop=True),
                 waits=[h["xn"]], incs=[(sT, 1)])
            h["bc2"] = (sT, sT.bump())

        def emit_xn_splits(c):
            h = hs[c]
            t_prev = (sT, sT.v)
            t_xnl = None
            for kc in range(KD):
                xw = [h["xn"]] + ([t_prev] if kc == 0 else [])
                t_xnl = act_split(xn_sb[:, kc, :], xnh[:, kc, :], xnl[:, kc, :], xw)
            h["xnsp"] = t_xnl

        def emit_xa_splits(c):
            h = hs[c]
            t_prev = (sT, sT.v)
            t_vec = (sV, sV.v)
            scrs = [sg_sb, tv_sb, sl_sb, sq2_sb]
            toks = {}
            for kc in range(KD):
                scr = scrs[kc % 4]
                aw = [h["bc2"]]
                if kc == 0:
                    aw += [t_prev, t_vec]
                if kc >= 4:
                    aw.append(toks[kc - 4])
                step("vector", lambda e, kc=kc, scr=scr: e.scalar_tensor_tensor(
                    scr[:, :], xn_sb[:, kc, :], wmixs[:, kc:kc + 1],
                    pb[6][:, 0:TC], OP.mult, OP.mult),
                     waits=aw, incs=[(sV, 1)])
                t_tmp = (sV, sV.bump())
                toks[kc] = act_split(scr[:, :], xah[:, kc, :], xal[:, kc, :], [t_tmp])
            h["xa"] = toks[KD - 1]

        gvd_cfg = ((bgs, AF.Sigmoid, sg_sb), (bvs, AF.Tanh, tv_sb),
                   (bds, AF.Sigmoid, sl_sb))

        def add_chunk1(c):
            gst = {"evac": {0: None, 1: None, 2: None}, "xs": None, "ls": None}

            def gvd_consume(wh, wl, wtoks, i, c=c, gst=gst):
                m, pi = i // 3, i % 3
                bias_t, func, dst = gvd_cfg[pi]
                mw = wtoks + [hs[c]["xa"]]
                if gst["evac"][pi] is not None:
                    mw.append(gst["evac"][pi])
                step("tensor", mm3(pb[pi][:, 0:TC], wh, wl, xah, xal, KD),
                     waits=mw, incs=[(sT, 1)])
                t_mm = (sT, sT.bump())
                if pi < 2:
                    ew = [t_mm] + ([gst["xs"]] if gst["xs"] else [])
                else:
                    ew = [t_mm] + ([gst["ls"]] if gst["ls"] else [])
                step("scalar", lambda e, m=m, pi=pi, bias_t=bias_t, func=func,
                     dst=dst: e.activation(dst[:, :], pb[pi][:, 0:TC], func,
                                           bias=bias_t[:, m:m + 1]),
                     waits=ew, incs=[(sS, 1)])
                gst["evac"][pi] = (sS, sS.bump())
                if pi == 2:
                    spw = [(sSP, sSP.v)] if (m == 0 and c >= 1) else []
                    step("vector", lambda e, m=m: e.tensor_tensor(
                        xs_sb[:, m, :], sg_sb[:, :], tv_sb[:, :], OP.mult),
                         waits=[gst["evac"][0], gst["evac"][1]] + spw,
                         incs=[(sV, 1)])
                    gst["xs"] = (sV, sV.bump())
                    step("vector", lambda e, m=m: e.tensor_scalar(
                        a_sb[:, m, :], sl_sb[:, :], 0.998, 0.001, OP.mult, OP.add),
                         waits=[gst["evac"][2]] + spw, incs=[(sV, 1)])
                    t_a = (sV, sV.bump())
                    cwait = [t_car0] if c == 0 else []

                    def lscan(e, m=m):
                        CW("vector", e)
                        CI("vector", e.tensor_tensor_scan(
                            sl_sb[:, :], a_sb[:, m, :], xs_sb[:, m, :],
                            carry[:, m:m + 1], OP.mult, OP.add))
                        CW("vector", e)
                        return e.tensor_copy(carry[:, m:m + 1], sl_sb[:, TC - 1:TC])
                    step("vector", lscan,
                         waits=[gst["xs"], t_a, gst["evac"][2]] + cwait,
                         incs=[(sV, 1)])
                    gst["ls"] = (sV, sV.bump())
                return t_mm

            for i in range(3 * KD):
                src = (wgT, wvT, wdT)[i % 3]
                m = i // 3
                SU(src[m], KD,
                   (lambda i: lambda wh, wl, wt: gvd_consume(wh, wl, wt, i))(i))
                if c + 1 < NCH:
                    if i == 0:
                        HOOK(lambda c=c: emit_load_x(c + 1))
                    elif i == 2:
                        HOOK(lambda c=c: emit_stats(c + 1))
                    elif i == 10:
                        HOOK(lambda c=c: emit_xn_a(c + 1))
                    elif i == 16:
                        HOOK(lambda c=c: emit_xn_b(c + 1))

            def end_gvd(c=c, gst=gst):
                t_gvd_all = (sV, sV.v)
                step("scalar", lambda e, a=c * TC, b=(c + 1) * TC: e.dma_start(
                    out=xsv[:, :, a:b], in_=xs_sb[:, :, :]), waits=[t_gvd_all],
                     incs=[(sSP, 16)])
                step("scalar", lambda e, a=c * TC, b=(c + 1) * TC: e.dma_start(
                    out=av[:, :, a:b], in_=a_sb[:, :, :]), waits=[t_gvd_all],
                     incs=[(sSP, 16)])
                sSP.bump(32)
                if c == NCH - 1:
                    step("gpsimd", lambda e: e.dma_start(out=cc_in.ap(),
                                                         in_=carry[:, :]),
                         waits=[gst["ls"]], incs=[(sGD, 16)])
                    t_ccin = (sGD, sGD.bump(16))
                    step("gpsimd", lambda e: e.collective_compute(
                        "AllGather", OP.bypass,
                        replica_groups=[[0, 1], [2, 3], [4, 5], [6, 7]],
                        ins=[cc_in.ap()], outs=[cc_out.ap()]),
                         waits=[t_ccin], incs=[(sCC, 1)])
                    t_cc = (sCC, sCC.bump())
                    step("gpsimd", lambda e: e.dma_start(
                        out=hprev[:, :], in_=cc_out.ap()[0, :, :]), waits=[t_cc],
                         incs=[(sGD, 16)])
                    t_hp = (sGD, sGD.bump(16))
                    step("vector", lambda e: e.tensor_scalar(
                        carry2[:, :], hprev[:, :], selps[:, 0:1], None, OP.mult),
                         waits=[t_hp], incs=[(sV, 1)])
                    hs["c2"] = (sV, sV.bump())
            HOOK(end_gvd)

            w1st = {"h": None}

            def w1_consume(wh, wl, wtoks, m, c=c, w1st=w1st):
                mw = wtoks + [hs[c]["xnsp"]]
                if w1st["h"] is not None:
                    mw.append(w1st["h"])
                step("tensor", mm3(pb[3][:, 0:TC], wh, wl, xnh, xnl, KD),
                     waits=mw, incs=[(sT, 1)])
                t_mm = (sT, sT.bump())
                step("scalar", lambda e, m=m: e.activation(
                    hq_sb[:, m, :], pb[3][:, 0:TC], AF.Relu, bias=b1s[:, m:m + 1]),
                     waits=[t_mm], incs=[(sS, 1)])
                w1st["h"] = (sS, sS.bump())
                return t_mm

            for m in range(KH):
                SU(w1T[m], KD,
                   (lambda m: lambda wh, wl, wt: w1_consume(wh, wl, wt, m))(m))
                if m == 2 and c + 1 < NCH:
                    HOOK(lambda c=c: emit_xa_splits(c + 1))
            if c + 1 < NCH:
                HOOK(lambda c=c: emit_xn_splits(c + 1))

            w2st = {"casts": [], "f": None, "part": None, "mm": None}

            def w2_tail(mo, t_mm, c=c, w2st=w2st):
                fw = [t_mm, t_bq]
                if w2st["part"] is not None:
                    fw.append(w2st["part"])
                step("scalar", lambda e, mo=mo: e.activation(
                    f16_sb[:, :], pb[3][:, 0:TC], AF.Identity,
                    bias=bq32[:, mo:mo + 1]), waits=fw, incs=[(sS, 1)])
                w2st["f"] = (sS, sS.bump())

                def mkpart(e, mo=mo, xb=xbufs[c % 2]):
                    CI("vector", e.tensor_copy(tmpf[:, :], f16_sb[:, :]))
                    CW("vector", e)
                    return e.tensor_tensor(xb[:, mo, :], xb[:, mo, :],
                                           tmpf[:, :], OP.add)
                step("vector", mkpart, waits=[w2st["f"]], incs=[(sV, 1)])
                w2st["part"] = (sV, sV.bump())

            def w2_consume(stga, t_ld, q, mo, c=c, w2st=w2st, w1st=w1st):
                vwt = [t_ld]
                if w2st["mm"] is not None:
                    vwt.append(w2st["mm"])
                if q == 0:
                    vwt.append((sWQ, sWQ.v))
                step("vector", lambda e, stga=stga, q=q: e.tensor_copy(
                    wq_sb[:, q * 8:(q + 1) * 8, :], stga[:, 0:8, :]),
                     waits=vwt, incs=[(sV, 1)])
                t_cast = (sV, sV.bump())
                w2st["casts"].append(t_cast)
                if q < 3:
                    return t_cast
                casts = list(w2st["casts"])
                w2st["casts"] = []
                step("scalar", lambda e, mo=mo: e.dma_start(
                    out=w2q8[mo], in_=wq_sb[:, :, :]), waits=casts,
                     incs=[(sWQ, 16)])
                sWQ.bump(16)
                mw = casts + [w1st["h"]]
                if w2st["f"] is not None:
                    mw.append(w2st["f"])

                def mmw2(e):
                    for kh in range(KH):
                        inst = e.matmul(pb[3][:, 0:TC], wq_sb[:, kh, :],
                                        hq_sb[:, kh, :], start=(kh == 0),
                                        stop=(kh == KH - 1))
                    return inst
                step("tensor", mmw2, waits=mw, incs=[(sT, 1)])
                w2st["mm"] = (sT, sT.bump())
                w2_tail(mo, w2st["mm"])
                return w2st["mm"]

            if c == 0:
                for mo in range(KD):
                    for q in range(4):
                        RU(w2T[mo][:, q * 8:(q + 1) * 8, :], 8,
                           (lambda q, mo: lambda stga, t_ld:
                            w2_consume(stga, t_ld, q, mo))(q, mo))

            def w2_cached(c=c, w2st=w2st, w1st=w1st):
                mms = {}
                for mo in range(KD):
                    wq = wq_sb if mo % 2 == 0 else wq2_sb
                    sq = sQ2[mo % 2]
                    lw = [(sWQ, sWQ.v)]
                    if mo >= 2:
                        lw.append(mms[mo - 2])
                    else:
                        lw.append(hs[c - 1]["w2mm"][mo - 2 + KD])
                    step("sync", lambda e, mo=mo, wq=wq, sq=sq: e.dma_start(
                        out=wq[:, :, :], in_=w2q8[mo]).then_inc(sq.h, 16),
                         waits=lw)
                    t_l = (sq, sq.bump(16))
                    mw = [t_l, w1st["h"]]
                    if w2st["f"] is not None:
                        mw.append(w2st["f"])

                    def mmw2(e, wq=wq):
                        for kh in range(KH):
                            inst = e.matmul(pb[3][:, 0:TC], wq[:, kh, :],
                                            hq_sb[:, kh, :], start=(kh == 0),
                                            stop=(kh == KH - 1))
                        return inst
                    step("tensor", mmw2, waits=mw, incs=[(sT, 1)])
                    w2st["mm"] = (sT, sT.bump())
                    mms[mo] = w2st["mm"]
                    w2_tail(mo, w2st["mm"])
                hs[c]["w2mm"] = mms

            def end_w2(c=c, w2st=w2st):
                if c == 0:
                    hs[0]["w2mm"] = {6: w2st["mm"], 7: w2st["mm"]}
                if c >= 1:
                    w2_cached()
                sst = sSTp[c % 2]
                step("scalar", lambda e, a=c * TC, b=(c + 1) * TC,
                     xb=xbufs[c % 2], sst=sst:
                     e.dma_start(out=outv[:, :, a:b], in_=xb[:, :, :]),
                     waits=[w2st["part"]], incs=[(sst, 16)])
                hs[c]["pstore"] = (sst, sst.bump(16))
            HOOK(end_w2)

        emit_load_x(0)
        emit_stats(0)
        emit_xn_a(0)
        emit_xn_b(0)
        emit_xa_splits(0)
        emit_xn_splits(0)
        for c in range(NCH):
            add_chunk1(c)
        run_stream()

        emit_block()
        p1.close()

        p3 = contextlib.ExitStack()
        sb3 = lambda name, shape, dt: p3.enter_context(nc.sbuf_tensor(name, shape, dt))
        q_xn = sb3("q_xn", [128, KD, TC], F32)
        q_xs = sb3("q_xs", [128, KD, TC], F32)
        q_a = sb3("q_a", [128, KD, TC], F32)
        q_pa = sb3("q_pa", [128, KD, TC], F32)
        q_s = sb3("q_s", [128, KD, TC], F32)
        xbh = sb3("xbh", [128, KD, TC], BF16)
        xbl = sb3("xbl", [128, KD, TC], BF16)
        pmh = sb3("pmh", [128, KF, TC], BF16)
        pml = sb3("pml", [128, KF, TC], BF16)

        h3 = {}

        def p3_boundary():
            return [(sXN, sXN.v), (sSP, sSP.v), (sT, sT.v), (sS, sS.v),
                    (sV, sV.v), (sSTp[0], sSTp[0].v), (sSTp[1], sSTp[1].v)]

        def p3_load_xsa(c):
            h = h3.setdefault(c, {})
            lw = p3_boundary() if c == 0 else [h3[c - 1]["scan"]]
            step("sync", lambda e, a=c * TC, b=(c + 1) * TC: e.dma_start(
                out=q_xs[:, :, :], in_=xsv[:, :, a:b]), waits=lw, incs=[(sP3A, 16)])
            sP3A.bump(16)
            step("sync", lambda e, a=c * TC, b=(c + 1) * TC: e.dma_start(
                out=q_a[:, :, :], in_=av[:, :, a:b]), incs=[(sP3A, 16)])
            sP3A.bump(16)
            h["lxa"] = (sP3A, sP3A.v)

        def p3_scans(c):
            h = h3[c]
            sw = [h["lxa"]]
            if c == 0:
                sw.append(hs["c2"])
            else:
                sw.append(h3[c - 1]["part2"])

            def scans(e):
                for m in range(KD):
                    CW("vector", e)
                    CI("vector", e.tensor_tensor_scan(
                        q_s[:, m, :], q_a[:, m, :], q_xs[:, m, :],
                        carry2[:, m:m + 1], OP.mult, OP.add))
                    CW("vector", e)
                    inst = e.tensor_copy(carry2[:, m:m + 1], q_s[:, m, TC - 1:TC])
                    if m < KD - 1:
                        CI("vector", inst)
                return inst
            step("vector", scans, waits=sw, incs=[(sV, 1)])
            h["scan"] = (sV, sV.bump())

        def p3_load_xn(c):
            h = h3[c]
            lw = p3_boundary() if c == 0 else [h3[c - 1]["part1"]]
            step("sync", lambda e, a=c * TC, b=(c + 1) * TC: e.dma_start(
                out=q_xn[:, :, :], in_=xnv[:, :, a:b]), waits=lw, incs=[(sP3X, 16)])
            h["lxn"] = (sP3X, sP3X.bump(16))

        def p3_part1(c):
            h = h3[c]
            step("vector", lambda e: e.tensor_tensor(
                q_s[:, :, :], q_xn[:, :, :], q_s[:, :, :], OP.add),
                 waits=[h["lxn"], h["scan"]], incs=[(sV, 1)])
            h["part1"] = (sV, sV.bump())

        def p3_load_pa(c):
            h = h3.setdefault(c, {})
            lw = [(sSTp[0], sSTp[0].v), (sSTp[1], sSTp[1].v)]
            step("sync", lambda e, a=c * TC, b=(c + 1) * TC: e.dma_start(
                out=q_pa[:, :, :], in_=outv[:, :, a:b]), waits=lw, incs=[(sP3P, 16)])
            h["lpa"] = (sP3P, sP3P.bump(16))

        def p3_part2(c):
            h = h3[c]
            step("vector", lambda e: e.tensor_tensor(
                q_pa[:, :, :], q_pa[:, :, :], q_s[:, :, :], OP.add),
                 waits=[h["lpa"], h["part1"]], incs=[(sV, 1)])
            h["part2"] = (sV, sV.bump())

        def p3_stats(c):
            h = h3[c]
            t_sq3 = []
            fw1 = [h3[c - 1]["fin1"]] if c >= 1 else []
            for kc in range(KD):
                scr = sl_sb if kc % 2 == 0 else sq2_sb
                w_ = [h["part1"]] + fw1 if kc < 2 else [(sT, t_sq3[kc - 2][1])]
                step("scalar", lambda e, kc=kc, scr=scr: e.activation(
                    scr[:, :], q_s[:, kc, :], AF.Square), waits=w_, incs=[(sS, 1)])
                t_s3 = (sS, sS.bump())
                step("tensor", lambda e, kc=kc, scr=scr: e.matmul(
                    pb[2][0:1, 0:TC], ones[:, 0:1], scr[:, :],
                    start=(kc == 0), stop=(kc == KD - 1)), waits=[t_s3],
                     incs=[(sT, 1)])
                t_sq3.append((sT, sT.bump()))
            step("scalar", lambda e: e.copy(row(0), pb[2][0:1, 0:TC]),
                 waits=[t_sq3[-1]], incs=[(sS, 1)])
            t_ev3 = (sS, sS.bump())
            step("vector", lambda e: e.tensor_scalar(
                row(1), row(0), INV_D, 1e-6, OP.mult, OP.add),
                 waits=[t_ev3], incs=[(sV, 1)])
            t_u3 = (sV, sV.bump())
            step("scalar", lambda e: e.activation(row(2), row(1), AF.Sqrt),
                 waits=[t_u3], incs=[(sS, 1)])
            t_y3 = (sS, sS.bump())

            def rows3(e):
                ops = [
                    lambda: e.reciprocal(row(3), row(2)),
                    lambda: e.tensor_tensor(row(3), row(1), row(3), OP.mult),
                    lambda: e.tensor_tensor(row(3), row(2), row(3), OP.add),
                    lambda: e.tensor_scalar(row(3), row(3), 0.5, None, OP.mult),
                ]
                for op in ops:
                    CI("vector", op())
                    CW("vector", e)
                return e.reciprocal(row(4), row(3))
            step("vector", rows3, waits=[t_y3], incs=[(sV, 1)])
            t_r3 = (sV, sV.bump())
            step("tensor", lambda e: e.matmul(pb[3][:, 0:TC], ones[0:1, :],
                                              row(4), start=True, stop=True),
                 waits=[t_r3], incs=[(sT, 1)])
            h["bc3"] = (sT, sT.bump())

        def p3_xbsplit(c):
            h = h3[c]
            scrs = [sg_sb, tv_sb, sl_sb, sq2_sb]
            toks = {}
            for kc in range(KD):
                scr = scrs[kc % 4]
                aw = [h["bc3"]]
                if kc == 0:
                    aw += [(sS, sS.v), (sV, sV.v)]
                if kc >= 4:
                    aw.append(toks[kc - 4])
                step("vector", lambda e, kc=kc, scr=scr: e.scalar_tensor_tensor(
                    scr[:, :], q_s[:, kc, :], wffns[:, kc:kc + 1],
                    pb[3][:, 0:TC], OP.mult, OP.mult),
                     waits=aw, incs=[(sV, 1)])
                t_tmp = (sV, sV.bump())
                toks[kc] = act_split(scr[:, :], xbh[:, kc, :], xbl[:, kc, :], [t_tmp])
            h["xb"] = toks[KD - 1]

        def add_chunk3(c):
            g3 = {"sil": None, "pm": None, "wo": None, "pml": None, "sw2": None}

            def g3_consume(wh, wl, wtoks, i, c=c, g3=g3):
                m, r = i // 3, i % 3
                if r == 0:
                    mw = wtoks + [h3[c]["xb"]]
                    if g3["sil"] is not None:
                        mw.append(g3["sil"])
                    if m == 0 and c >= 1:
                        mw.append(h3[c - 1]["fin2"])
                    step("tensor", mm3(pb[0][:, 0:TC], wh, wl, xbh, xbl, KD),
                         waits=mw, incs=[(sT, 1)])
                    t_g = (sT, sT.bump())
                    sw2 = [t_g] + ([g3["pm"]] if g3["pm"] else [])
                    step("scalar", lambda e: e.activation(
                        sg_sb[:, :], pb[0][:, 0:TC], AF.Silu),
                         waits=sw2, incs=[(sS, 1)])
                    g3["sil"] = (sS, sS.bump())
                    return t_g
                if r == 1:
                    mw2 = wtoks + [h3[c]["xb"]]
                    if g3["pm"] is not None:
                        mw2.append(g3["pm"])
                    if m == 0 and c >= 1:
                        mw2.append(h3[c - 1]["fin2"])
                    step("tensor", mm3(pb[1][:, 0:TC], wh, wl, xbh, xbl, KD),
                         waits=mw2, incs=[(sT, 1)])
                    t_u = (sT, sT.bump())
                    step("vector", lambda e: e.tensor_tensor(
                        tmpf[:, :], sg_sb[:, :], pb[1][:, 0:TC], OP.mult),
                         waits=[g3["sil"], t_u] + ([g3["pml"]] if g3["pml"] else []),
                         incs=[(sV, 1)])
                    g3["pm"] = (sV, sV.bump())
                    g3["pml"] = act_split(tmpf[:, :], pmh[:, m, :], pml[:, m, :],
                                          [g3["pm"]])
                    return t_u

                def mmwo(e, wh=wh, wl=wl, m=m):
                    for mo in range(6):
                        e.matmul(pb[2 + mo][:, 0:TC], wh[:, mo, :], pmh[:, m, :],
                                 start=(m == 0), stop=False)
                        e.matmul(pb[2 + mo][:, 0:TC], wh[:, mo, :], pml[:, m, :],
                                 start=False, stop=False)
                        inst = e.matmul(pb[2 + mo][:, 0:TC], wl[:, mo, :],
                                        pmh[:, m, :], start=False,
                                        stop=(m == KF - 1))
                    return inst
                wo_w = wtoks + [g3["pml"]]
                if m == 0 and c >= 1:
                    wo_w.append(h3[c - 1]["fin1"])
                step("tensor", mmwo, waits=wo_w, incs=[(sT, 1)])
                g3["wo"] = (sT, sT.bump())
                return g3["wo"]

            for i in range(3 * KF):
                m, r = i // 3, i % 3
                if r == 0:
                    src, ns = wgateT[m], KD
                elif r == 1:
                    src, ns = wupT[m], KD
                else:
                    src = woutv[m, :, 0:768].rearrange("p (s q) -> p s q", q=128)
                    ns = 6
                SU(src, ns,
                   (lambda i: lambda wh, wl, wt: g3_consume(wh, wl, wt, i))(i))
                if c + 1 < NCH:
                    if i == 2:
                        HOOK(lambda c=c: p3_load_xsa(c + 1))
                    elif i == 6:
                        HOOK(lambda c=c: p3_scans(c + 1))
                    elif i == 40:
                        HOOK(lambda c=c: p3_load_xn(c + 1))
                    elif i == 44:
                        HOOK(lambda c=c: p3_part1(c + 1))

            def fin1(c=c, g3=g3):
                def finev1(e):
                    for mo in range(6):
                        inst = e.tensor_tensor(q_pa[:, mo, :], q_pa[:, mo, :],
                                               pb[2 + mo][:, 0:TC], OP.add)
                    return inst
                step("vector", finev1, waits=[g3["wo"], h3[c]["part2"]],
                     incs=[(sV, 1)])
                h3[c]["fin1"] = (sV, sV.bump())
                if c + 1 < NCH:
                    p3_stats(c + 1)
            HOOK(fin1)

            def sw2_consume(wh, wl, wtoks, m, c=c, g3=g3):
                mw3 = list(wtoks)
                if m == 0:
                    mw3 += [g3["pm"], g3["sil"], g3["pml"], h3[c]["fin1"]]

                def mm2(e, wh=wh, wl=wl, m=m):
                    for mo in range(2):
                        e.matmul(pb[mo][:, 0:TC], wh[:, mo, :], pmh[:, m, :],
                                 start=(m == 0), stop=False)
                        e.matmul(pb[mo][:, 0:TC], wh[:, mo, :], pml[:, m, :],
                                 start=False, stop=False)
                        inst = e.matmul(pb[mo][:, 0:TC], wl[:, mo, :],
                                        pmh[:, m, :], start=False,
                                        stop=(m == KF - 1))
                    return inst
                step("tensor", mm2, waits=mw3, incs=[(sT, 1)])
                g3["sw2"] = (sT, sT.bump())
                return g3["sw2"]

            for m in range(KF):
                SU(woutv[m, :, 768:1024].rearrange("p (s q) -> p s q", q=128), 2,
                   (lambda m: lambda wh, wl, wt: sw2_consume(wh, wl, wt, m))(m))
                if m == 2 and c + 1 < NCH:
                    HOOK(lambda c=c: p3_xbsplit(c + 1))

            def fin2(c=c, g3=g3):
                def finev2(e):
                    CI("vector", e.tensor_tensor(q_pa[:, 6, :], q_pa[:, 6, :],
                                                 pb[0][:, 0:TC], OP.add))
                    return e.tensor_tensor(q_pa[:, 7, :], q_pa[:, 7, :],
                                           pb[1][:, 0:TC], OP.add)
                step("vector", finev2, waits=[g3["sw2"]], incs=[(sV, 1)])
                h3[c]["fin2"] = (sV, sV.bump())
                sst = sSTp[c % 2]
                step("scalar", lambda e, a=c * TC, b=(c + 1) * TC, sst=sst:
                     e.dma_start(out=outv[:, :, a:b], in_=q_pa[:, :, :]),
                     waits=[h3[c]["fin1"], h3[c]["fin2"]], incs=[(sst, 16)])
                sst.bump(16)
                if c + 1 < NCH:
                    p3_load_pa(c + 1)
                    p3_part2(c + 1)
            HOOK(fin2)

        p3_load_xsa(0)
        p3_scans(0)
        p3_load_xn(0)
        p3_part1(0)
        p3_load_pa(0)
        p3_part2(0)
        p3_stats(0)
        p3_xbsplit(0)
        for c in range(NCH):
            add_chunk3(c)
        run_stream()

        step("gpsimd", None, waits=[(sSTp[0], sSTp[0].v), (sSTp[1], sSTp[1].v)])
        step("sync", None, waits=[(sSTp[0], sSTp[0].v), (sSTp[1], sSTp[1].v)])

        emit_block()
        p3.close()

    return nc


def kernel(L0_attn_x, L0_attn_rms_mix_w, L0_attn_rms_ffn_w, L0_attn_Wg, L0_attn_bg,
           L0_attn_Wv, L0_attn_bv, L0_attn_Wd, L0_attn_bd, L0_attn_W_gate,
           L0_attn_W_up, L0_attn_W_out, L0_ffn_w1, L0_ffn_b1, L0_ffn_w2, L0_ffn_b2):
    x = np.asarray(L0_attn_x, dtype=np.float32)

    def t(a):
        return np.ascontiguousarray(np.asarray(a, dtype=np.float32).T)

    def tw(a):
        w = np.asarray(a, dtype=np.float32)
        M, K = w.shape
        return np.ascontiguousarray(
            w.reshape(M // 128, 128, K // 128, 128).transpose(0, 3, 2, 1))

    def b2(a, k):
        return np.ascontiguousarray(np.asarray(a, dtype=np.float32).reshape(k, 128).T)

    shared = {
        "wgT": tw(L0_attn_Wg), "wvT": tw(L0_attn_Wv), "wdT": tw(L0_attn_Wd),
        "wgateT": tw(L0_attn_W_gate), "wupT": tw(L0_attn_W_up),
        "woutT": t(L0_attn_W_out),
        "w1T": tw(L0_ffn_w1), "w2T": tw(L0_ffn_w2),
        "bg2": b2(L0_attn_bg, KD), "bv2": b2(L0_attn_bv, KD), "bd2": b2(L0_attn_bd, KD),
        "b12": b2(L0_ffn_b1, KH), "b22": b2(L0_ffn_b2, KD),
        "wmix2": b2(L0_attn_rms_mix_w, KD), "wffn2": b2(L0_attn_rms_ffn_w, KD),
    }
    in_maps = []
    for core in range(8):
        b, h = core // 2, core % 2
        xc = np.ascontiguousarray(x[b, h * T:(h + 1) * T, :].T)
        d = dict(shared)
        d["xT"] = xc
        d["selp"] = np.full((128, 1), float(h), dtype=np.float32)
        in_maps.append(d)

    nc = build_nc()
    res = run_bass_kernel_spmd(nc, in_maps, core_ids=list(range(8)))
    out = np.empty((B, S, D), dtype=np.float32)
    for core in range(8):
        b, h = core // 2, core % 2
        out[b, h * T:(h + 1) * T, :] = res.results[core]["outT"].T
    return out


# version 18
# speedup vs baseline: 1.0796x; 1.0178x over previous
import contextlib

import numpy as np

import concourse.bass as bass
import concourse.mybir as mybir
from concourse.bass_utils import run_bass_kernel_spmd

F32 = mybir.dt.float32
BF16 = mybir.dt.bfloat16
FP8 = mybir.dt.float8e4
F16 = mybir.dt.float16
AF = mybir.ActivationFunctionType
OP = mybir.AluOpType

B, S, D, F, H = 4, 4096, 1024, 3072, 4096
T = 2048
KD, KF, KH = D // 128, F // 128, H // 128
TC, NCH = 512, 4
INV_D = 1.0 / D
PIPE_SLOTS = 8
NPIPE = 4
DEPTH = 3


class Sem:
    def __init__(self, nc, st, name):
        self.h = st.enter_context(nc.semaphore(name))
        self.v = 0

    def bump(self, n=1):
        self.v += n
        return self.v


def build_nc():
    nc = bass.Bass("TRN2", target_bir_lowering=False, debug=False)

    xT = nc.dram_tensor("xT", [D, T], F32, kind="ExternalInput").ap()
    wgT = nc.dram_tensor("wgT", [KD, 128, KD, 128], F32, kind="ExternalInput").ap()
    wvT = nc.dram_tensor("wvT", [KD, 128, KD, 128], F32, kind="ExternalInput").ap()
    wdT = nc.dram_tensor("wdT", [KD, 128, KD, 128], F32, kind="ExternalInput").ap()
    wgateT = nc.dram_tensor("wgateT", [KF, 128, KD, 128], F32, kind="ExternalInput").ap()
    wupT = nc.dram_tensor("wupT", [KF, 128, KD, 128], F32, kind="ExternalInput").ap()
    woutT = nc.dram_tensor("woutT", [F, D], F32, kind="ExternalInput").ap()
    w1T = nc.dram_tensor("w1T", [KH, 128, KD, 128], F32, kind="ExternalInput").ap()
    w2T = nc.dram_tensor("w2T", [KD, 128, KH, 128], F32, kind="ExternalInput").ap()
    w2q8 = nc.dram_tensor("w2q8", [KD, 128, KH, 128], FP8).ap()
    bg2 = nc.dram_tensor("bg2", [128, KD], F32, kind="ExternalInput").ap()
    bv2 = nc.dram_tensor("bv2", [128, KD], F32, kind="ExternalInput").ap()
    bd2 = nc.dram_tensor("bd2", [128, KD], F32, kind="ExternalInput").ap()
    b12 = nc.dram_tensor("b12", [128, KH], F32, kind="ExternalInput").ap()
    b22 = nc.dram_tensor("b22", [128, KD], F32, kind="ExternalInput").ap()
    wmix2 = nc.dram_tensor("wmix2", [128, KD], F32, kind="ExternalInput").ap()
    wffn2 = nc.dram_tensor("wffn2", [128, KD], F32, kind="ExternalInput").ap()
    selp = nc.dram_tensor("selp", [128, 1], F32, kind="ExternalInput").ap()

    outT = nc.dram_tensor("outT", [D, T], F32, kind="ExternalOutput").ap()

    xn_h = nc.dram_tensor("xn_h", [D, T], F32).ap()
    xs_h = nc.dram_tensor("xs_h", [D, T], F32).ap()
    a_h = nc.dram_tensor("a_h", [D, T], F32).ap()
    cc_in = nc.dram_tensor("cc_in", [128, KD], F32)
    cc_out = nc.dram_tensor("cc_out", [2, 128, KD], F32)

    def fm(ap_):
        return ap_.rearrange("(k p) t -> p k t", p=128)

    xTv, xnv, xsv, av, outv = fm(xT), fm(xn_h), fm(xs_h), fm(a_h), fm(outT)
    woutv = woutT.rearrange("(k p) m -> k p m", p=128)

    P = []

    def step(eng, body=None, waits=(), incs=()):
        waits = [(s.h, v) for (s, v) in waits]
        incs_ = [(s.h, n) for (s, n) in incs]

        def fn(e):
            for h, v in waits:
                e.wait_ge(h, v)
            inst = body(e) if body is not None else None
            if incs_:
                if inst is None:
                    assert body is None
                    inst = e.nop()
                for h, n in incs_:
                    inst.then_inc(h, n)

        P.append((eng, fn))

    def emit_block():
        with nc.Block() as blk:
            for eng_name in ("sync", "scalar", "vector", "tensor", "gpsimd"):
                def make(eng_name):
                    def emit(e):
                        for en, fn in P:
                            if en == eng_name:
                                fn(e)
                    return emit
                getattr(blk, eng_name)(make(eng_name))
        P.clear()

    with contextlib.ExitStack() as st:
        sb = lambda name, shape, dt: st.enter_context(nc.sbuf_tensor(name, shape, dt))

        ones = sb("ones", [128, 128], F32)
        bgs = sb("bgs", [128, KD], F32)
        bvs = sb("bvs", [128, KD], F32)
        bds = sb("bds", [128, KD], F32)
        b1s = sb("b1s", [128, KH], F32)
        b2s = sb("b2s", [128, KD], F32)
        bq16 = sb("bq16", [128, KD], F16)
        bq32 = sb("bq32", [128, KD], F32)
        wmixs = sb("wmixs", [128, KD], F32)
        wffns = sb("wffns", [128, KD], F32)
        selps = sb("selps", [128, 1], F32)
        carry = sb("carry", [128, KD], F32)
        carry2 = sb("carry2", [128, KD], F32)
        hprev = sb("hprev", [128, KD], F32)
        sg_sb = sb("sg_sb", [128, TC], F32)
        tv_sb = sb("tv_sb", [128, TC], F32)
        sl_sb = sb("sl_sb", [128, TC], F32)
        sq2_sb = sb("sq2_sb", [128, TC], F32)
        tmpf = sb("tmpf", [128, TC], F32)
        f16_sb = sb("f16_sb", [128, TC], F16)
        stat = sb("stat", [1, 7 * TC], F32)
        stg = [sb(f"stg{i}", [128, PIPE_SLOTS, 128], F32) for i in range(NPIPE)]
        wsh = [sb(f"wsh{i}", [128, PIPE_SLOTS, 128], BF16) for i in range(NPIPE)]
        wsl = [sb(f"wsl{i}", [128, PIPE_SLOTS, 128], BF16) for i in range(NPIPE)]

        pb = [st.enter_context(nc.psum_tensor(f"pb{i}", [128, 512], F32))
              for i in range(8)]

        sBIAS = Sem(nc, st, "sBIAS")
        sX2 = [Sem(nc, st, "sXa"), Sem(nc, st, "sXb")]
        sWp = [Sem(nc, st, f"sW{i}") for i in range(NPIPE)]
        sP3A = Sem(nc, st, "sP3A")
        sP3X = Sem(nc, st, "sP3X")
        sP3P = Sem(nc, st, "sP3P")
        sSTp = [Sem(nc, st, "sSTa"), Sem(nc, st, "sSTb")]
        sXN = Sem(nc, st, "sXN")
        sSP = Sem(nc, st, "sSP")
        sGD = Sem(nc, st, "sGD")
        sWQ = Sem(nc, st, "sWQ")
        sQ2 = [Sem(nc, st, "sQ2a"), Sem(nc, st, "sQ2b")]
        sT = Sem(nc, st, "sT")
        sV = Sem(nc, st, "sV")
        sS = Sem(nc, st, "sS")
        sG = Sem(nc, st, "sG")
        sCC = Sem(nc, st, "sCC")
        chain = {n: Sem(nc, st, f"ch_{n[:3]}") for n in ("vector", "scalar", "gpsimd")}

        def CI(en, inst):
            s = chain[en]
            s.v += 1
            inst.then_inc(s.h, 1)
            return inst

        def CW(en, e):
            s = chain[en]
            e.wait_ge(s.h, s.v)

        def row(i, n=TC):
            return stat[0:1, i * TC: i * TC + n]

        pipe_state = {i: {"free": []} for i in range(NPIPE)}
        pipe_par = [0]

        def pipe_load(src_ap, nslots):
            par = pipe_par[0]
            pipe_par[0] = (par + 1) % NPIPE
            ps = pipe_state[par]
            sw = sWp[par]
            stga, wha, wla = stg[par], wsh[par], wsl[par]
            step("sync", lambda e: e.dma_start(
                out=stga[:, 0:nslots, :], in_=src_ap).then_inc(sw.h, 16),
                 waits=list(ps["free"]))
            t_ld = (sw, sw.bump(16))
            step("scalar", lambda e: e.activation(
                wha[:, 0:nslots, :], stga[:, 0:nslots, :], AF.Copy),
                 waits=[t_ld], incs=[(sS, 1)])
            t_hi = (sS, sS.bump())
            step("vector", lambda e: e.tensor_tensor(
                wla[:, 0:nslots, :], stga[:, 0:nslots, :], wha[:, 0:nslots, :],
                OP.subtract), waits=[t_hi], incs=[(sV, 1)])
            t_lo = (sV, sV.bump())
            ps["free"] = [t_lo]
            return ("split", wha, wla, [t_hi, t_lo], par)

        def pipe_load_raw(src_ap, nslots):
            par = pipe_par[0]
            pipe_par[0] = (par + 1) % NPIPE
            ps = pipe_state[par]
            sw = sWp[par]
            stga = stg[par]
            step("sync", lambda e: e.dma_start(
                out=stga[:, 0:nslots, :], in_=src_ap).then_inc(sw.h, 16),
                 waits=list(ps["free"]))
            t_ld = (sw, sw.bump(16))
            ps["free"] = []
            return ("raw", stga, t_ld, None, par)

        stream = []
        hooks_after = {}

        def SU(src, nslots, consume):
            stream.append((False, src, nslots, consume))
            return len(stream) - 1

        def RU(src, nslots, consume):
            stream.append((True, src, nslots, consume))
            return len(stream) - 1

        def HOOK(fn):
            hooks_after.setdefault(len(stream) - 1, []).append(fn)

        def run_stream():
            n = len(stream)
            handles = {}

            def load(j):
                is_raw, src, ns, _ = stream[j]
                return pipe_load_raw(src, ns) if is_raw else pipe_load(src, ns)

            for j in range(min(DEPTH, n)):
                handles[j] = load(j)
            for i in range(n):
                kind, a1, a2, a3, par = handles.pop(i)
                consume = stream[i][3]
                if kind == "split":
                    tok = consume(a1, a2, a3)
                else:
                    tok = consume(a1, a2)
                pipe_state[par]["free"].append(tok)
                for fn in hooks_after.get(i, []):
                    fn()
                j = i + DEPTH
                if j < n:
                    handles[j] = load(j)
            stream.clear()
            hooks_after.clear()

        def mm3(ps_ap, wh, wl, xh, xl, nk):
            def body(e):
                for kc in range(nk):
                    e.matmul(ps_ap, wh[:, kc, :], xh[:, kc, :],
                             start=(kc == 0), stop=False)
                for kc in range(nk):
                    e.matmul(ps_ap, wh[:, kc, :], xl[:, kc, :],
                             start=False, stop=False)
                for kc in range(nk):
                    inst = e.matmul(ps_ap, wl[:, kc, :], xh[:, kc, :],
                                    start=False, stop=(kc == nk - 1))
                return inst
            return body

        def act_split(src_ap, hi_ap, lo_ap, waits):
            step("scalar", lambda e: e.activation(hi_ap, src_ap, AF.Copy),
                 waits=waits, incs=[(sS, 1)])
            t_hi = (sS, sS.bump())
            step("vector", lambda e: e.tensor_tensor(lo_ap, src_ap, hi_ap,
                                                     OP.subtract),
                 waits=[t_hi], incs=[(sV, 1)])
            return (sV, sV.bump())

        step("vector", lambda e: e.memset(ones[:, :], 1.0), incs=[(sV, 1)])
        t_ones = (sV, sV.bump())
        step("vector", lambda e: e.memset(carry[:, :], 0.0), incs=[(sV, 1)])
        t_car0 = (sV, sV.bump())
        step("vector", lambda e: e.memset(row(6), 1.0), incs=[(sV, 1)])
        t_onerow = (sV, sV.bump())

        for dst, src in ((bgs, bg2), (bvs, bv2), (bds, bd2), (b1s, b12),
                         (b2s, b22), (wmixs, wmix2), (wffns, wffn2), (selps, selp)):
            step("sync", lambda e, d=dst, s=src: e.dma_start(out=d[:, :], in_=s),
                 incs=[(sBIAS, 16)])
            sBIAS.bump(16)
        t_bias = (sBIAS, sBIAS.v)

        def mk_bq(e):
            CI("scalar", e.activation(bq16[:, :], b2s[:, :], AF.Copy))
            CW("scalar", e)
            return e.activation(bq32[:, :], bq16[:, :], AF.Copy)
        step("scalar", mk_bq, waits=[t_bias], incs=[(sS, 1)])
        t_bq = (sS, sS.bump())

        p1 = contextlib.ExitStack()
        sb1 = lambda name, shape, dt: p1.enter_context(nc.sbuf_tensor(name, shape, dt))
        xb0 = sb1("xb0", [128, KD, TC], F32)
        xb1 = sb1("xb1", [128, KD, TC], F32)
        xbufs = [xb0, xb1]
        xn_sb = sb1("xn_sb", [128, KD, TC], F32)
        xnh = sb1("xnh", [128, KD, TC], BF16)
        xnl = sb1("xnl", [128, KD, TC], BF16)
        xah = sb1("xah", [128, KD, TC], BF16)
        xal = sb1("xal", [128, KD, TC], BF16)
        xs_sb = sb1("xs_sb", [128, KD, TC], F32)
        a_sb = sb1("a_sb", [128, KD, TC], F32)
        hq_sb = sb1("hq_sb", [128, KH, TC], FP8)
        wq_sb = sb1("wq_sb", [128, KH, 128], FP8)
        wq2_sb = sb1("wq2_sb", [128, KH, 128], FP8)

        hs = {}

        def emit_load_x(c):
            h = hs.setdefault(c, {})
            sx = sX2[c % 2]
            xw = []
            if c >= 2:
                xw.append((sSTp[c % 2], sSTp[c % 2].v))
            step("sync", lambda e, a=c * TC, b=(c + 1) * TC, xb=xbufs[c % 2]:
                 e.dma_start(out=xb[:, :, :], in_=xTv[:, :, a:b]),
                 waits=xw, incs=[(sx, 16)])
            h["x"] = (sx, sx.bump(16))

        def emit_stats(c):
            h = hs[c]
            xb = xbufs[c % 2]
            pw = []
            if c >= 1:
                pw += [(sXN, sXN.v), hs[c - 1]["xnsp"], hs[c - 1]["xa"]]
            step("tensor", lambda e: [e.matmul(pb[4][0:1, 0:TC], ones[:, 0:1],
                                               xb[:, kc, :], start=(kc == 0),
                                               stop=(kc == KD - 1))
                                      for kc in range(KD)][-1],
                 waits=[h["x"], t_ones], incs=[(sT, 1)])
            t_sumx = (sT, sT.bump())
            t_sq = []
            for kc in range(KD):
                scr = xn_sb[:, kc % 2, :]
                w_ = [h["x"]] + pw if kc < 2 else [(sT, t_sq[kc - 2][1])]
                step("scalar", lambda e, kc=kc, scr=scr: e.activation(
                    scr, xb[:, kc, :], AF.Square), waits=w_, incs=[(sS, 1)])
                t_s = (sS, sS.bump())
                step("tensor", lambda e, kc=kc, scr=scr: e.matmul(
                    pb[5][0:1, 0:TC], ones[:, 0:1], scr,
                    start=(kc == 0), stop=(kc == KD - 1)),
                     waits=[t_s], incs=[(sT, 1)])
                t_sq.append((sT, sT.bump()))
            t_sumsq = t_sq[-1]

            ew = [t_sumx, t_sumsq]
            if c >= 1:
                ew += [hs[c - 1]["bc1"], hs[c - 1]["bc2"]]

            def evrows(e):
                CI("scalar", e.copy(row(0), pb[4][0:1, 0:TC]))
                return e.copy(row(1), pb[5][0:1, 0:TC])
            step("scalar", evrows, waits=ew, incs=[(sS, 1)])
            h["_ev"] = (sS, sS.bump())

        def emit_rows1(c):
            h = hs[c]

            def rows1(e):
                ops = [
                    lambda: e.tensor_scalar(row(0), row(0), INV_D, None, OP.mult),
                    lambda: e.tensor_scalar(row(1), row(1), INV_D, None, OP.mult),
                    lambda: e.tensor_tensor(row(2), row(0), row(0), OP.mult),
                    lambda: e.tensor_tensor(row(1), row(1), row(2), OP.subtract),
                    lambda: e.tensor_scalar(row(2), row(1), 1e-5, None, OP.add),
                    lambda: e.tensor_scalar(row(3), row(2), 1e-6, None, OP.mult),
                ]
                for op in ops:
                    CI("vector", op())
                    CW("vector", e)
                return e.tensor_tensor(row(3), row(1), row(3), OP.add)
            step("vector", rows1, waits=[h["_ev"], t_onerow], incs=[(sV, 1)])
            t_r1 = (sV, sV.bump())

            def sqrts(e):
                CI("scalar", e.activation(row(1), row(2), AF.Sqrt))
                return e.activation(row(4), row(3), AF.Sqrt)
            step("scalar", sqrts, waits=[t_r1], incs=[(sS, 1)])
            h["_sqr"] = (sS, sS.bump())

        def emit_rows2a(c):
            h = hs[c]

            def rows2a(e):
                ops = [
                    lambda: e.reciprocal(row(5), row(1)),
                    lambda: e.tensor_tensor(row(1), row(5), row(5), OP.mult),
                    lambda: e.tensor_tensor(row(1), row(2), row(1), OP.mult),
                    lambda: e.tensor_scalar(row(1), row(1), -0.5, 1.5, OP.mult,
                                            OP.add),
                ]
                for op in ops:
                    CI("vector", op())
                    CW("vector", e)
                return e.tensor_tensor(row(1), row(5), row(1), OP.mult)
            step("vector", rows2a, waits=[h["_sqr"]], incs=[(sV, 1)])
            h["_r2a"] = (sV, sV.bump())

        def emit_rows2b(c):
            h = hs[c]

            def rows2b(e):
                ops = [
                    lambda: e.reciprocal(row(5), row(4)),
                    lambda: e.tensor_tensor(row(4), row(5), row(5), OP.mult),
                    lambda: e.tensor_tensor(row(4), row(3), row(4), OP.mult),
                    lambda: e.tensor_scalar(row(4), row(4), -0.5, 1.5, OP.mult,
                                            OP.add),
                    lambda: e.tensor_tensor(row(4), row(5), row(4), OP.mult),
                    lambda: e.tensor_tensor(row(0), row(0), row(1), OP.mult),
                    lambda: e.tensor_tensor(row(5), row(2), row(1), OP.mult),
                ]
                for op in ops:
                    CI("vector", op())
                    CW("vector", e)
                return e.tensor_tensor(row(5), row(5), row(4), OP.mult)
            step("vector", rows2b, waits=[h["_r2a"]], incs=[(sV, 1)])
            h["rows"] = (sV, sV.bump())

        def emit_xn_a(c):
            h = hs[c]
            xb = xbufs[c % 2]
            bw = [h["rows"]]
            if c >= 1:
                bw += [hs[c - 1]["xa"]]
            step("tensor", lambda e: (e.matmul(pb[6][:, 0:TC], ones[0:1, :], row(1),
                                               start=True, stop=True),
                                      e.matmul(pb[7][:, 0:TC], ones[0:1, :], row(0),
                                               start=True, stop=True))[-1],
                 waits=bw, incs=[(sT, 1)])
            h["bc1"] = (sT, sT.bump())

            def mk_xn(e):
                for kc in range(KD):
                    inst = e.tensor_tensor(xn_sb[:, kc, :], xb[:, kc, :],
                                           pb[6][:, 0:TC], OP.mult)
                CI("vector", inst)
                CW("vector", e)
                for kc in range(KD):
                    inst = e.tensor_tensor(xn_sb[:, kc, :], xn_sb[:, kc, :],
                                           pb[7][:, 0:TC], OP.subtract)
                return inst
            step("vector", mk_xn, waits=[h["bc1"], h["x"]], incs=[(sV, 1)])
            h["xn"] = (sV, sV.bump())
            step("scalar", lambda e, a=c * TC, b=(c + 1) * TC: e.dma_start(
                out=xnv[:, :, a:b], in_=xn_sb[:, :, :]), waits=[h["xn"]],
                 incs=[(sXN, 16)])
            h["spill"] = (sXN, sXN.bump(16))

        def emit_xn_b(c):
            h = hs[c]
            step("tensor", lambda e: e.matmul(pb[6][:, 0:TC], ones[0:1, :], row(5),
                                              start=True, stop=True),
                 waits=[h["xn"]], incs=[(sT, 1)])
            h["bc2"] = (sT, sT.bump())

        def emit_xn_splits(c):
            h = hs[c]
            t_prev = (sT, sT.v)
            t_xnl = None
            for kc in range(KD):
                xw = [h["xn"]] + ([t_prev] if kc == 0 else [])
                t_xnl = act_split(xn_sb[:, kc, :], xnh[:, kc, :], xnl[:, kc, :], xw)
            h["xnsp"] = t_xnl

        def emit_xa_splits(c):
            h = hs[c]
            t_prev = (sT, sT.v)
            t_vec = (sV, sV.v)
            scrs = [sg_sb, tv_sb, sl_sb, sq2_sb]
            toks = {}
            for kc in range(KD):
                scr = scrs[kc % 4]
                aw = [h["bc2"]]
                if kc == 0:
                    aw += [t_prev, t_vec]
                if kc >= 4:
                    aw.append(toks[kc - 4])
                step("vector", lambda e, kc=kc, scr=scr: e.scalar_tensor_tensor(
                    scr[:, :], xn_sb[:, kc, :], wmixs[:, kc:kc + 1],
                    pb[6][:, 0:TC], OP.mult, OP.mult),
                     waits=aw, incs=[(sV, 1)])
                t_tmp = (sV, sV.bump())
                toks[kc] = act_split(scr[:, :], xah[:, kc, :], xal[:, kc, :], [t_tmp])
            h["xa"] = toks[KD - 1]

        gvd_cfg = ((bgs, AF.Sigmoid, sg_sb), (bvs, AF.Tanh, tv_sb),
                   (bds, AF.Sigmoid, sl_sb))

        def add_chunk1(c):
            gst = {"evac": {0: None, 1: None, 2: None}, "xs": None, "ls": None}

            def gvd_consume(wh, wl, wtoks, i, c=c, gst=gst):
                m, pi = i // 3, i % 3
                bias_t, func, dst = gvd_cfg[pi]
                mw = wtoks + [hs[c]["xa"]]
                if gst["evac"][pi] is not None:
                    mw.append(gst["evac"][pi])
                step("tensor", mm3(pb[pi][:, 0:TC], wh, wl, xah, xal, KD),
                     waits=mw, incs=[(sT, 1)])
                t_mm = (sT, sT.bump())
                if pi < 2:
                    ew = [t_mm] + ([gst["xs"]] if gst["xs"] else [])
                else:
                    ew = [t_mm] + ([gst["ls"]] if gst["ls"] else [])
                step("scalar", lambda e, m=m, pi=pi, bias_t=bias_t, func=func,
                     dst=dst: e.activation(dst[:, :], pb[pi][:, 0:TC], func,
                                           bias=bias_t[:, m:m + 1]),
                     waits=ew, incs=[(sS, 1)])
                gst["evac"][pi] = (sS, sS.bump())
                if pi == 2:
                    spw = [(sSP, sSP.v)] if (m == 0 and c >= 1) else []
                    step("vector", lambda e, m=m: e.tensor_tensor(
                        xs_sb[:, m, :], sg_sb[:, :], tv_sb[:, :], OP.mult),
                         waits=[gst["evac"][0], gst["evac"][1]] + spw,
                         incs=[(sV, 1)])
                    gst["xs"] = (sV, sV.bump())
                    step("vector", lambda e, m=m: e.tensor_scalar(
                        a_sb[:, m, :], sl_sb[:, :], 0.998, 0.001, OP.mult, OP.add),
                         waits=[gst["evac"][2]] + spw, incs=[(sV, 1)])
                    t_a = (sV, sV.bump())
                    cwait = [t_car0] if c == 0 else []

                    def lscan(e, m=m):
                        CW("vector", e)
                        CI("vector", e.tensor_tensor_scan(
                            sl_sb[:, :], a_sb[:, m, :], xs_sb[:, m, :],
                            carry[:, m:m + 1], OP.mult, OP.add))
                        CW("vector", e)
                        return e.tensor_copy(carry[:, m:m + 1], sl_sb[:, TC - 1:TC])
                    step("vector", lscan,
                         waits=[gst["xs"], t_a, gst["evac"][2]] + cwait,
                         incs=[(sV, 1)])
                    gst["ls"] = (sV, sV.bump())
                return t_mm

            for i in range(3 * KD):
                src = (wgT, wvT, wdT)[i % 3]
                m = i // 3
                SU(src[m], KD,
                   (lambda i: lambda wh, wl, wt: gvd_consume(wh, wl, wt, i))(i))
                if c + 1 < NCH:
                    if i == 0:
                        HOOK(lambda c=c: emit_load_x(c + 1))
                    elif i == 2:
                        HOOK(lambda c=c: emit_stats(c + 1))
                    elif i == 5:
                        HOOK(lambda c=c: emit_rows1(c + 1))
                    elif i == 7:
                        HOOK(lambda c=c: emit_rows2a(c + 1))
                    elif i == 9:
                        HOOK(lambda c=c: emit_rows2b(c + 1))
                    elif i == 12:
                        HOOK(lambda c=c: emit_xn_a(c + 1))
                    elif i == 16:
                        HOOK(lambda c=c: emit_xn_b(c + 1))

            def end_gvd(c=c, gst=gst):
                t_gvd_all = (sV, sV.v)
                step("scalar", lambda e, a=c * TC, b=(c + 1) * TC: e.dma_start(
                    out=xsv[:, :, a:b], in_=xs_sb[:, :, :]), waits=[t_gvd_all],
                     incs=[(sSP, 16)])
                step("scalar", lambda e, a=c * TC, b=(c + 1) * TC: e.dma_start(
                    out=av[:, :, a:b], in_=a_sb[:, :, :]), waits=[t_gvd_all],
                     incs=[(sSP, 16)])
                sSP.bump(32)
                if c == NCH - 1:
                    step("gpsimd", lambda e: e.dma_start(out=cc_in.ap(),
                                                         in_=carry[:, :]),
                         waits=[gst["ls"]], incs=[(sGD, 16)])
                    t_ccin = (sGD, sGD.bump(16))
                    step("gpsimd", lambda e: e.collective_compute(
                        "AllGather", OP.bypass,
                        replica_groups=[[0, 1], [2, 3], [4, 5], [6, 7]],
                        ins=[cc_in.ap()], outs=[cc_out.ap()]),
                         waits=[t_ccin], incs=[(sCC, 1)])
                    t_cc = (sCC, sCC.bump())
                    step("gpsimd", lambda e: e.dma_start(
                        out=hprev[:, :], in_=cc_out.ap()[0, :, :]), waits=[t_cc],
                         incs=[(sGD, 16)])
                    t_hp = (sGD, sGD.bump(16))
                    step("vector", lambda e: e.tensor_scalar(
                        carry2[:, :], hprev[:, :], selps[:, 0:1], None, OP.mult),
                         waits=[t_hp], incs=[(sV, 1)])
                    hs["c2"] = (sV, sV.bump())
            HOOK(end_gvd)

            w1st = {"h": None}

            def w1_consume(wh, wl, wtoks, m, c=c, w1st=w1st):
                mw = wtoks + [hs[c]["xnsp"]]
                if w1st["h"] is not None:
                    mw.append(w1st["h"])
                step("tensor", mm3(pb[3][:, 0:TC], wh, wl, xnh, xnl, KD),
                     waits=mw, incs=[(sT, 1)])
                t_mm = (sT, sT.bump())
                step("scalar", lambda e, m=m: e.activation(
                    hq_sb[:, m, :], pb[3][:, 0:TC], AF.Relu, bias=b1s[:, m:m + 1]),
                     waits=[t_mm], incs=[(sS, 1)])
                w1st["h"] = (sS, sS.bump())
                return t_mm

            for m in range(KH):
                SU(w1T[m], KD,
                   (lambda m: lambda wh, wl, wt: w1_consume(wh, wl, wt, m))(m))
                if m == 2 and c + 1 < NCH:
                    HOOK(lambda c=c: emit_xa_splits(c + 1))
            if c + 1 < NCH:
                HOOK(lambda c=c: emit_xn_splits(c + 1))

            w2st = {"casts": [], "f": None, "part": None, "mm": None}

            def w2_tail(mo, t_mm, c=c, w2st=w2st):
                fw = [t_mm, t_bq]
                if w2st["part"] is not None:
                    fw.append(w2st["part"])
                step("scalar", lambda e, mo=mo: e.activation(
                    f16_sb[:, :], pb[3][:, 0:TC], AF.Identity,
                    bias=bq32[:, mo:mo + 1]), waits=fw, incs=[(sS, 1)])
                w2st["f"] = (sS, sS.bump())

                def mkpart(e, mo=mo, xb=xbufs[c % 2]):
                    CI("vector", e.tensor_copy(tmpf[:, :], f16_sb[:, :]))
                    CW("vector", e)
                    return e.tensor_tensor(xb[:, mo, :], xb[:, mo, :],
                                           tmpf[:, :], OP.add)
                step("vector", mkpart, waits=[w2st["f"]], incs=[(sV, 1)])
                w2st["part"] = (sV, sV.bump())

            def w2_consume(stga, t_ld, q, mo, c=c, w2st=w2st, w1st=w1st):
                vwt = [t_ld]
                if w2st["mm"] is not None:
                    vwt.append(w2st["mm"])
                if q == 0:
                    vwt.append((sWQ, sWQ.v))
                step("vector", lambda e, stga=stga, q=q: e.tensor_copy(
                    wq_sb[:, q * 8:(q + 1) * 8, :], stga[:, 0:8, :]),
                     waits=vwt, incs=[(sV, 1)])
                t_cast = (sV, sV.bump())
                w2st["casts"].append(t_cast)
                if q < 3:
                    return t_cast
                casts = list(w2st["casts"])
                w2st["casts"] = []
                step("scalar", lambda e, mo=mo: e.dma_start(
                    out=w2q8[mo], in_=wq_sb[:, :, :]), waits=casts,
                     incs=[(sWQ, 16)])
                sWQ.bump(16)
                mw = casts + [w1st["h"]]
                if w2st["f"] is not None:
                    mw.append(w2st["f"])

                def mmw2(e):
                    for kh in range(KH):
                        inst = e.matmul(pb[3][:, 0:TC], wq_sb[:, kh, :],
                                        hq_sb[:, kh, :], start=(kh == 0),
                                        stop=(kh == KH - 1))
                    return inst
                step("tensor", mmw2, waits=mw, incs=[(sT, 1)])
                w2st["mm"] = (sT, sT.bump())
                w2_tail(mo, w2st["mm"])
                return w2st["mm"]

            if c == 0:
                for mo in range(KD):
                    for q in range(4):
                        RU(w2T[mo][:, q * 8:(q + 1) * 8, :], 8,
                           (lambda q, mo: lambda stga, t_ld:
                            w2_consume(stga, t_ld, q, mo))(q, mo))

            def w2_cached(c=c, w2st=w2st, w1st=w1st):
                mms = {}
                for mo in range(KD):
                    wq = wq_sb if mo % 2 == 0 else wq2_sb
                    sq = sQ2[mo % 2]
                    lw = [(sWQ, sWQ.v)]
                    if mo >= 2:
                        lw.append(mms[mo - 2])
                    else:
                        lw.append(hs[c - 1]["w2mm"][mo - 2 + KD])
                    step("sync", lambda e, mo=mo, wq=wq, sq=sq: e.dma_start(
                        out=wq[:, :, :], in_=w2q8[mo]).then_inc(sq.h, 16),
                         waits=lw)
                    t_l = (sq, sq.bump(16))
                    mw = [t_l, w1st["h"]]
                    if w2st["f"] is not None:
                        mw.append(w2st["f"])

                    def mmw2(e, wq=wq):
                        for kh in range(KH):
                            inst = e.matmul(pb[3][:, 0:TC], wq[:, kh, :],
                                            hq_sb[:, kh, :], start=(kh == 0),
                                            stop=(kh == KH - 1))
                        return inst
                    step("tensor", mmw2, waits=mw, incs=[(sT, 1)])
                    w2st["mm"] = (sT, sT.bump())
                    mms[mo] = w2st["mm"]
                    w2_tail(mo, w2st["mm"])
                hs[c]["w2mm"] = mms

            def end_w2(c=c, w2st=w2st):
                if c == 0:
                    hs[0]["w2mm"] = {6: w2st["mm"], 7: w2st["mm"]}
                if c >= 1:
                    w2_cached()
                sst = sSTp[c % 2]
                step("scalar", lambda e, a=c * TC, b=(c + 1) * TC,
                     xb=xbufs[c % 2], sst=sst:
                     e.dma_start(out=outv[:, :, a:b], in_=xb[:, :, :]),
                     waits=[w2st["part"]], incs=[(sst, 16)])
                hs[c]["pstore"] = (sst, sst.bump(16))
            HOOK(end_w2)

        emit_load_x(0)
        emit_stats(0)
        emit_rows1(0)
        emit_rows2a(0)
        emit_rows2b(0)
        emit_xn_a(0)
        emit_xn_b(0)
        emit_xa_splits(0)
        emit_xn_splits(0)
        for c in range(NCH):
            add_chunk1(c)
        run_stream()

        emit_block()
        p1.close()

        p3 = contextlib.ExitStack()
        sb3 = lambda name, shape, dt: p3.enter_context(nc.sbuf_tensor(name, shape, dt))
        q_xn = sb3("q_xn", [128, KD, TC], F32)
        q_xs = sb3("q_xs", [128, KD, TC], F32)
        q_a = sb3("q_a", [128, KD, TC], F32)
        q_pa = sb3("q_pa", [128, KD, TC], F32)
        q_s = sb3("q_s", [128, KD, TC], F32)
        xbh = sb3("xbh", [128, KD, TC], BF16)
        xbl = sb3("xbl", [128, KD, TC], BF16)
        pmh = sb3("pmh", [128, KF, TC], BF16)
        pml = sb3("pml", [128, KF, TC], BF16)

        h3 = {}

        def p3_boundary():
            return [(sXN, sXN.v), (sSP, sSP.v), (sT, sT.v), (sS, sS.v),
                    (sV, sV.v), (sSTp[0], sSTp[0].v), (sSTp[1], sSTp[1].v)]

        def p3_load_xsa(c):
            h = h3.setdefault(c, {})
            lw = p3_boundary() if c == 0 else [h3[c - 1]["scan"]]
            step("sync", lambda e, a=c * TC, b=(c + 1) * TC: e.dma_start(
                out=q_xs[:, :, :], in_=xsv[:, :, a:b]), waits=lw, incs=[(sP3A, 16)])
            sP3A.bump(16)
            step("sync", lambda e, a=c * TC, b=(c + 1) * TC: e.dma_start(
                out=q_a[:, :, :], in_=av[:, :, a:b]), incs=[(sP3A, 16)])
            sP3A.bump(16)
            h["lxa"] = (sP3A, sP3A.v)

        def p3_scans(c):
            h = h3[c]
            sw = [h["lxa"]]
            if c == 0:
                sw.append(hs["c2"])
            else:
                sw.append(h3[c - 1]["part2"])

            def scans(e):
                for m in range(KD):
                    CW("vector", e)
                    CI("vector", e.tensor_tensor_scan(
                        q_s[:, m, :], q_a[:, m, :], q_xs[:, m, :],
                        carry2[:, m:m + 1], OP.mult, OP.add))
                    CW("vector", e)
                    inst = e.tensor_copy(carry2[:, m:m + 1], q_s[:, m, TC - 1:TC])
                    if m < KD - 1:
                        CI("vector", inst)
                return inst
            step("vector", scans, waits=sw, incs=[(sV, 1)])
            h["scan"] = (sV, sV.bump())

        def p3_load_xn(c):
            h = h3[c]
            lw = p3_boundary() if c == 0 else [h3[c - 1]["part1"]]
            step("sync", lambda e, a=c * TC, b=(c + 1) * TC: e.dma_start(
                out=q_xn[:, :, :], in_=xnv[:, :, a:b]), waits=lw, incs=[(sP3X, 16)])
            h["lxn"] = (sP3X, sP3X.bump(16))

        def p3_part1(c):
            h = h3[c]
            step("vector", lambda e: e.tensor_tensor(
                q_s[:, :, :], q_xn[:, :, :], q_s[:, :, :], OP.add),
                 waits=[h["lxn"], h["scan"]], incs=[(sV, 1)])
            h["part1"] = (sV, sV.bump())

        def p3_load_pa(c):
            h = h3.setdefault(c, {})
            lw = [(sSTp[0], sSTp[0].v), (sSTp[1], sSTp[1].v)]
            step("sync", lambda e, a=c * TC, b=(c + 1) * TC: e.dma_start(
                out=q_pa[:, :, :], in_=outv[:, :, a:b]), waits=lw, incs=[(sP3P, 16)])
            h["lpa"] = (sP3P, sP3P.bump(16))

        def p3_part2(c):
            h = h3[c]
            step("vector", lambda e: e.tensor_tensor(
                q_pa[:, :, :], q_pa[:, :, :], q_s[:, :, :], OP.add),
                 waits=[h["lpa"], h["part1"]], incs=[(sV, 1)])
            h["part2"] = (sV, sV.bump())

        def p3_stats(c):
            h = h3[c]
            t_sq3 = []
            fw1 = [h3[c - 1]["fin1"]] if c >= 1 else []
            for kc in range(KD):
                scr = sl_sb if kc % 2 == 0 else sq2_sb
                w_ = [h["part1"]] + fw1 if kc < 2 else [(sT, t_sq3[kc - 2][1])]
                step("scalar", lambda e, kc=kc, scr=scr: e.activation(
                    scr[:, :], q_s[:, kc, :], AF.Square), waits=w_, incs=[(sS, 1)])
                t_s3 = (sS, sS.bump())
                step("tensor", lambda e, kc=kc, scr=scr: e.matmul(
                    pb[2][0:1, 0:TC], ones[:, 0:1], scr[:, :],
                    start=(kc == 0), stop=(kc == KD - 1)), waits=[t_s3],
                     incs=[(sT, 1)])
                t_sq3.append((sT, sT.bump()))
            step("scalar", lambda e: e.copy(row(0), pb[2][0:1, 0:TC]),
                 waits=[t_sq3[-1]], incs=[(sS, 1)])
            h["_ev3"] = (sS, sS.bump())

        def p3_rows3a(c):
            h = h3[c]
            step("vector", lambda e: e.tensor_scalar(
                row(1), row(0), INV_D, 1e-6, OP.mult, OP.add),
                 waits=[h["_ev3"]], incs=[(sV, 1)])
            t_u3 = (sV, sV.bump())
            step("scalar", lambda e: e.activation(row(2), row(1), AF.Sqrt),
                 waits=[t_u3], incs=[(sS, 1)])
            h["_y3"] = (sS, sS.bump())

        def p3_rows3b(c):
            h = h3[c]

            def rows3(e):
                ops = [
                    lambda: e.reciprocal(row(3), row(2)),
                    lambda: e.tensor_tensor(row(2), row(3), row(3), OP.mult),
                    lambda: e.tensor_tensor(row(2), row(1), row(2), OP.mult),
                    lambda: e.tensor_scalar(row(2), row(2), -0.5, 1.5, OP.mult,
                                            OP.add),
                ]
                for op in ops:
                    CI("vector", op())
                    CW("vector", e)
                return e.tensor_tensor(row(4), row(3), row(2), OP.mult)
            step("vector", rows3, waits=[h["_y3"]], incs=[(sV, 1)])
            t_r3 = (sV, sV.bump())
            step("tensor", lambda e: e.matmul(pb[3][:, 0:TC], ones[0:1, :],
                                              row(4), start=True, stop=True),
                 waits=[t_r3], incs=[(sT, 1)])
            h["bc3"] = (sT, sT.bump())

        def p3_xbsplit(c):
            h = h3[c]
            scrs = [sg_sb, tv_sb, sl_sb, sq2_sb]
            toks = {}
            for kc in range(KD):
                scr = scrs[kc % 4]
                aw = [h["bc3"]]
                if kc == 0:
                    aw += [(sS, sS.v), (sV, sV.v)]
                if kc >= 4:
                    aw.append(toks[kc - 4])
                step("vector", lambda e, kc=kc, scr=scr: e.scalar_tensor_tensor(
                    scr[:, :], q_s[:, kc, :], wffns[:, kc:kc + 1],
                    pb[3][:, 0:TC], OP.mult, OP.mult),
                     waits=aw, incs=[(sV, 1)])
                t_tmp = (sV, sV.bump())
                toks[kc] = act_split(scr[:, :], xbh[:, kc, :], xbl[:, kc, :], [t_tmp])
            h["xb"] = toks[KD - 1]

        def add_chunk3(c):
            g3 = {"sil": None, "pm": None, "wo": None, "pml": None, "sw2": None}

            def g3_consume(wh, wl, wtoks, i, c=c, g3=g3):
                m, r = i // 3, i % 3
                if r == 0:
                    mw = wtoks + [h3[c]["xb"]]
                    if g3["sil"] is not None:
                        mw.append(g3["sil"])
                    if m == 0 and c >= 1:
                        mw.append(h3[c - 1]["fin2"])
                    step("tensor", mm3(pb[0][:, 0:TC], wh, wl, xbh, xbl, KD),
                         waits=mw, incs=[(sT, 1)])
                    t_g = (sT, sT.bump())
                    sw2 = [t_g] + ([g3["pm"]] if g3["pm"] else [])
                    step("scalar", lambda e: e.activation(
                        sg_sb[:, :], pb[0][:, 0:TC], AF.Silu),
                         waits=sw2, incs=[(sS, 1)])
                    g3["sil"] = (sS, sS.bump())
                    return t_g
                if r == 1:
                    mw2 = wtoks + [h3[c]["xb"]]
                    if g3["pm"] is not None:
                        mw2.append(g3["pm"])
                    if m == 0 and c >= 1:
                        mw2.append(h3[c - 1]["fin2"])
                    step("tensor", mm3(pb[1][:, 0:TC], wh, wl, xbh, xbl, KD),
                         waits=mw2, incs=[(sT, 1)])
                    t_u = (sT, sT.bump())
                    step("vector", lambda e: e.tensor_tensor(
                        tmpf[:, :], sg_sb[:, :], pb[1][:, 0:TC], OP.mult),
                         waits=[g3["sil"], t_u] + ([g3["pml"]] if g3["pml"] else []),
                         incs=[(sV, 1)])
                    g3["pm"] = (sV, sV.bump())
                    g3["pml"] = act_split(tmpf[:, :], pmh[:, m, :], pml[:, m, :],
                                          [g3["pm"]])
                    return t_u

                def mmwo(e, wh=wh, wl=wl, m=m):
                    for mo in range(6):
                        e.matmul(pb[2 + mo][:, 0:TC], wh[:, mo, :], pmh[:, m, :],
                                 start=(m == 0), stop=False)
                        e.matmul(pb[2 + mo][:, 0:TC], wh[:, mo, :], pml[:, m, :],
                                 start=False, stop=False)
                        inst = e.matmul(pb[2 + mo][:, 0:TC], wl[:, mo, :],
                                        pmh[:, m, :], start=False,
                                        stop=(m == KF - 1))
                    return inst
                wo_w = wtoks + [g3["pml"]]
                if m == 0 and c >= 1:
                    wo_w.append(h3[c - 1]["fin1"])
                step("tensor", mmwo, waits=wo_w, incs=[(sT, 1)])
                g3["wo"] = (sT, sT.bump())
                return g3["wo"]

            for i in range(3 * KF):
                m, r = i // 3, i % 3
                if r == 0:
                    src, ns = wgateT[m], KD
                elif r == 1:
                    src, ns = wupT[m], KD
                else:
                    src = woutv[m, :, 0:768].rearrange("p (s q) -> p s q", q=128)
                    ns = 6
                SU(src, ns,
                   (lambda i: lambda wh, wl, wt: g3_consume(wh, wl, wt, i))(i))
                if c + 1 < NCH:
                    if i == 2:
                        HOOK(lambda c=c: p3_load_xsa(c + 1))
                    elif i == 6:
                        HOOK(lambda c=c: p3_scans(c + 1))
                    elif i == 40:
                        HOOK(lambda c=c: p3_load_xn(c + 1))
                    elif i == 44:
                        HOOK(lambda c=c: p3_part1(c + 1))

            def fin1(c=c, g3=g3):
                def finev1(e):
                    for mo in range(6):
                        inst = e.tensor_tensor(q_pa[:, mo, :], q_pa[:, mo, :],
                                               pb[2 + mo][:, 0:TC], OP.add)
                    return inst
                step("vector", finev1, waits=[g3["wo"], h3[c]["part2"]],
                     incs=[(sV, 1)])
                h3[c]["fin1"] = (sV, sV.bump())
                if c + 1 < NCH:
                    p3_stats(c + 1)
            HOOK(fin1)

            def sw2_consume(wh, wl, wtoks, m, c=c, g3=g3):
                mw3 = list(wtoks)
                if m == 0:
                    mw3 += [g3["pm"], g3["sil"], g3["pml"], h3[c]["fin1"]]

                def mm2(e, wh=wh, wl=wl, m=m):
                    for mo in range(2):
                        e.matmul(pb[mo][:, 0:TC], wh[:, mo, :], pmh[:, m, :],
                                 start=(m == 0), stop=False)
                        e.matmul(pb[mo][:, 0:TC], wh[:, mo, :], pml[:, m, :],
                                 start=False, stop=False)
                        inst = e.matmul(pb[mo][:, 0:TC], wl[:, mo, :],
                                        pmh[:, m, :], start=False,
                                        stop=(m == KF - 1))
                    return inst
                step("tensor", mm2, waits=mw3, incs=[(sT, 1)])
                g3["sw2"] = (sT, sT.bump())
                return g3["sw2"]

            for m in range(KF):
                SU(woutv[m, :, 768:1024].rearrange("p (s q) -> p s q", q=128), 2,
                   (lambda m: lambda wh, wl, wt: sw2_consume(wh, wl, wt, m))(m))
                if c + 1 < NCH:
                    if m == 1:
                        HOOK(lambda c=c: p3_rows3a(c + 1))
                    elif m == 4:
                        HOOK(lambda c=c: p3_rows3b(c + 1))
                    elif m == 8:
                        HOOK(lambda c=c: p3_xbsplit(c + 1))

            def fin2(c=c, g3=g3):
                def finev2(e):
                    CI("vector", e.tensor_tensor(q_pa[:, 6, :], q_pa[:, 6, :],
                                                 pb[0][:, 0:TC], OP.add))
                    return e.tensor_tensor(q_pa[:, 7, :], q_pa[:, 7, :],
                                           pb[1][:, 0:TC], OP.add)
                step("vector", finev2, waits=[g3["sw2"]], incs=[(sV, 1)])
                h3[c]["fin2"] = (sV, sV.bump())
                sst = sSTp[c % 2]
                step("scalar", lambda e, a=c * TC, b=(c + 1) * TC, sst=sst:
                     e.dma_start(out=outv[:, :, a:b], in_=q_pa[:, :, :]),
                     waits=[h3[c]["fin1"], h3[c]["fin2"]], incs=[(sst, 16)])
                sst.bump(16)
                if c + 1 < NCH:
                    p3_load_pa(c + 1)
                    p3_part2(c + 1)
            HOOK(fin2)

        p3_load_xsa(0)
        p3_scans(0)
        p3_load_xn(0)
        p3_part1(0)
        p3_load_pa(0)
        p3_part2(0)
        p3_stats(0)
        p3_rows3a(0)
        p3_rows3b(0)
        p3_xbsplit(0)
        for c in range(NCH):
            add_chunk3(c)
        run_stream()

        step("gpsimd", None, waits=[(sSTp[0], sSTp[0].v), (sSTp[1], sSTp[1].v)])
        step("sync", None, waits=[(sSTp[0], sSTp[0].v), (sSTp[1], sSTp[1].v)])

        emit_block()
        p3.close()

    return nc


def kernel(L0_attn_x, L0_attn_rms_mix_w, L0_attn_rms_ffn_w, L0_attn_Wg, L0_attn_bg,
           L0_attn_Wv, L0_attn_bv, L0_attn_Wd, L0_attn_bd, L0_attn_W_gate,
           L0_attn_W_up, L0_attn_W_out, L0_ffn_w1, L0_ffn_b1, L0_ffn_w2, L0_ffn_b2):
    x = np.asarray(L0_attn_x, dtype=np.float32)

    def t(a):
        return np.ascontiguousarray(np.asarray(a, dtype=np.float32).T)

    def tw(a):
        w = np.asarray(a, dtype=np.float32)
        M, K = w.shape
        return np.ascontiguousarray(
            w.reshape(M // 128, 128, K // 128, 128).transpose(0, 3, 2, 1))

    def b2(a, k):
        return np.ascontiguousarray(np.asarray(a, dtype=np.float32).reshape(k, 128).T)

    shared = {
        "wgT": tw(L0_attn_Wg), "wvT": tw(L0_attn_Wv), "wdT": tw(L0_attn_Wd),
        "wgateT": tw(L0_attn_W_gate), "wupT": tw(L0_attn_W_up),
        "woutT": t(L0_attn_W_out),
        "w1T": tw(L0_ffn_w1), "w2T": tw(L0_ffn_w2),
        "bg2": b2(L0_attn_bg, KD), "bv2": b2(L0_attn_bv, KD), "bd2": b2(L0_attn_bd, KD),
        "b12": b2(L0_ffn_b1, KH), "b22": b2(L0_ffn_b2, KD),
        "wmix2": b2(L0_attn_rms_mix_w, KD), "wffn2": b2(L0_attn_rms_ffn_w, KD),
    }
    in_maps = []
    for core in range(8):
        b, h = core // 2, core % 2
        xc = np.ascontiguousarray(x[b, h * T:(h + 1) * T, :].T)
        d = dict(shared)
        d["xT"] = xc
        d["selp"] = np.full((128, 1), float(h), dtype=np.float32)
        in_maps.append(d)

    nc = build_nc()
    res = run_bass_kernel_spmd(nc, in_maps, core_ids=list(range(8)))
    out = np.empty((B, S, D), dtype=np.float32)
    for core in range(8):
        b, h = core // 2, core % 2
        out[b, h * T:(h + 1) * T, :] = res.results[core]["outT"].T
    return out


# version 19
# speedup vs baseline: 1.1069x; 1.0253x over previous
import contextlib

import numpy as np

import concourse.bass as bass
import concourse.mybir as mybir
from concourse.bass_utils import run_bass_kernel_spmd

F32 = mybir.dt.float32
DR = mybir.MatmulPerfMode.DoubleRow
BF16 = mybir.dt.bfloat16
FP8 = mybir.dt.float8e4
F16 = mybir.dt.float16
AF = mybir.ActivationFunctionType
OP = mybir.AluOpType

B, S, D, F, H = 4, 4096, 1024, 3072, 4096
T = 2048
KD, KF, KH = D // 128, F // 128, H // 128
TC, NCH = 512, 4
INV_D = 1.0 / D
PIPE_SLOTS = 8
NPIPE = 4
DEPTH = 3


class Sem:
    def __init__(self, nc, st, name):
        self.h = st.enter_context(nc.semaphore(name))
        self.v = 0

    def bump(self, n=1):
        self.v += n
        return self.v


def build_nc():
    nc = bass.Bass("TRN2", target_bir_lowering=False, debug=False)

    xT = nc.dram_tensor("xT", [D, T], F32, kind="ExternalInput").ap()
    wgT = nc.dram_tensor("wgT", [KD, 128, KD, 128], F32, kind="ExternalInput").ap()
    wvT = nc.dram_tensor("wvT", [KD, 128, KD, 128], F32, kind="ExternalInput").ap()
    wdT = nc.dram_tensor("wdT", [KD, 128, KD, 128], F32, kind="ExternalInput").ap()
    wgateT = nc.dram_tensor("wgateT", [KF, 128, KD, 128], F32, kind="ExternalInput").ap()
    wupT = nc.dram_tensor("wupT", [KF, 128, KD, 128], F32, kind="ExternalInput").ap()
    woutT = nc.dram_tensor("woutT", [F, D], F32, kind="ExternalInput").ap()
    w1T = nc.dram_tensor("w1T", [KH, 128, KD, 128], F32, kind="ExternalInput").ap()
    w2T = nc.dram_tensor("w2T", [KD, 128, KH, 128], F32, kind="ExternalInput").ap()
    w2q8 = nc.dram_tensor("w2q8", [KD, 128, KH, 128], FP8).ap()
    bg2 = nc.dram_tensor("bg2", [128, KD], F32, kind="ExternalInput").ap()
    bv2 = nc.dram_tensor("bv2", [128, KD], F32, kind="ExternalInput").ap()
    bd2 = nc.dram_tensor("bd2", [128, KD], F32, kind="ExternalInput").ap()
    b12 = nc.dram_tensor("b12", [128, KH], F32, kind="ExternalInput").ap()
    b22 = nc.dram_tensor("b22", [128, KD], F32, kind="ExternalInput").ap()
    wmix2 = nc.dram_tensor("wmix2", [128, KD], F32, kind="ExternalInput").ap()
    wffn2 = nc.dram_tensor("wffn2", [128, KD], F32, kind="ExternalInput").ap()
    selp = nc.dram_tensor("selp", [128, 1], F32, kind="ExternalInput").ap()

    outT = nc.dram_tensor("outT", [D, T], F32, kind="ExternalOutput").ap()

    xn_h = nc.dram_tensor("xn_h", [D, T], F32).ap()
    xs_h = nc.dram_tensor("xs_h", [D, T], F32).ap()
    a_h = nc.dram_tensor("a_h", [D, T], F32).ap()
    cc_in = nc.dram_tensor("cc_in", [128, KD], F32)
    cc_out = nc.dram_tensor("cc_out", [2, 128, KD], F32)

    def fm(ap_):
        return ap_.rearrange("(k p) t -> p k t", p=128)

    xTv, xnv, xsv, av, outv = fm(xT), fm(xn_h), fm(xs_h), fm(a_h), fm(outT)
    woutv = woutT.rearrange("(k p) m -> k p m", p=128)

    P = []

    def step(eng, body=None, waits=(), incs=()):
        waits = [(s.h, v) for (s, v) in waits]
        incs_ = [(s.h, n) for (s, n) in incs]

        def fn(e):
            for h, v in waits:
                e.wait_ge(h, v)
            inst = body(e) if body is not None else None
            if incs_:
                if inst is None:
                    assert body is None
                    inst = e.nop()
                for h, n in incs_:
                    inst.then_inc(h, n)

        P.append((eng, fn))

    def emit_block():
        with nc.Block() as blk:
            for eng_name in ("sync", "scalar", "vector", "tensor", "gpsimd"):
                def make(eng_name):
                    def emit(e):
                        for en, fn in P:
                            if en == eng_name:
                                fn(e)
                    return emit
                getattr(blk, eng_name)(make(eng_name))
        P.clear()

    with contextlib.ExitStack() as st:
        sb = lambda name, shape, dt: st.enter_context(nc.sbuf_tensor(name, shape, dt))

        ones = sb("ones", [128, 128], F32)
        bgs = sb("bgs", [128, KD], F32)
        bvs = sb("bvs", [128, KD], F32)
        bds = sb("bds", [128, KD], F32)
        b1s = sb("b1s", [128, KH], F32)
        b2s = sb("b2s", [128, KD], F32)
        bq16 = sb("bq16", [128, KD], F16)
        bq32 = sb("bq32", [128, KD], F32)
        wmixs = sb("wmixs", [128, KD], F32)
        wffns = sb("wffns", [128, KD], F32)
        selps = sb("selps", [128, 1], F32)
        carry = sb("carry", [128, KD], F32)
        carry2 = sb("carry2", [128, KD], F32)
        hprev = sb("hprev", [128, KD], F32)
        sg_sb = sb("sg_sb", [128, TC], F32)
        tv_sb = sb("tv_sb", [128, TC], F32)
        sl_sb = sb("sl_sb", [128, TC], F32)
        sq2_sb = sb("sq2_sb", [128, TC], F32)
        tmpf = sb("tmpf", [128, TC], F32)
        f16_sb = sb("f16_sb", [128, TC], F16)
        stat = sb("stat", [1, 7 * TC], F32)
        stg = [sb(f"stg{i}", [128, PIPE_SLOTS, 128], F32) for i in range(NPIPE)]
        wsh = [sb(f"wsh{i}", [128, PIPE_SLOTS, 128], BF16) for i in range(NPIPE)]
        wsl = [sb(f"wsl{i}", [128, PIPE_SLOTS, 128], BF16) for i in range(NPIPE)]

        pb = [st.enter_context(nc.psum_tensor(f"pb{i}", [128, 512], F32))
              for i in range(8)]

        sBIAS = Sem(nc, st, "sBIAS")
        sX2 = [Sem(nc, st, "sXa"), Sem(nc, st, "sXb")]
        sWp = [Sem(nc, st, f"sW{i}") for i in range(NPIPE)]
        sP3A = Sem(nc, st, "sP3A")
        sP3X = Sem(nc, st, "sP3X")
        sP3P = Sem(nc, st, "sP3P")
        sSTp = [Sem(nc, st, "sSTa"), Sem(nc, st, "sSTb")]
        sXN = Sem(nc, st, "sXN")
        sSP = Sem(nc, st, "sSP")
        sGD = Sem(nc, st, "sGD")
        sWQ = Sem(nc, st, "sWQ")
        sQ2 = [Sem(nc, st, "sQ2a"), Sem(nc, st, "sQ2b")]
        sT = Sem(nc, st, "sT")
        sV = Sem(nc, st, "sV")
        sS = Sem(nc, st, "sS")
        sG = Sem(nc, st, "sG")
        sCC = Sem(nc, st, "sCC")
        chain = {n: Sem(nc, st, f"ch_{n[:3]}") for n in ("vector", "scalar", "gpsimd")}

        def CI(en, inst):
            s = chain[en]
            s.v += 1
            inst.then_inc(s.h, 1)
            return inst

        def CW(en, e):
            s = chain[en]
            e.wait_ge(s.h, s.v)

        def row(i, n=TC):
            return stat[0:1, i * TC: i * TC + n]

        pipe_state = {i: {"free": []} for i in range(NPIPE)}
        pipe_par = [0]

        def pipe_load(src_ap, nslots):
            par = pipe_par[0]
            pipe_par[0] = (par + 1) % NPIPE
            ps = pipe_state[par]
            sw = sWp[par]
            stga, wha, wla = stg[par], wsh[par], wsl[par]
            step("sync", lambda e: e.dma_start(
                out=stga[:, 0:nslots, :], in_=src_ap).then_inc(sw.h, 16),
                 waits=list(ps["free"]))
            t_ld = (sw, sw.bump(16))
            step("scalar", lambda e: e.activation(
                wha[:, 0:nslots, :], stga[:, 0:nslots, :], AF.Copy),
                 waits=[t_ld], incs=[(sS, 1)])
            t_hi = (sS, sS.bump())
            step("vector", lambda e: e.tensor_tensor(
                wla[:, 0:nslots, :], stga[:, 0:nslots, :], wha[:, 0:nslots, :],
                OP.subtract), waits=[t_hi], incs=[(sV, 1)])
            t_lo = (sV, sV.bump())
            ps["free"] = [t_lo]
            return ("split", wha, wla, [t_hi, t_lo], par)

        def pipe_load_raw(src_ap, nslots):
            par = pipe_par[0]
            pipe_par[0] = (par + 1) % NPIPE
            ps = pipe_state[par]
            sw = sWp[par]
            stga = stg[par]
            step("sync", lambda e: e.dma_start(
                out=stga[:, 0:nslots, :], in_=src_ap).then_inc(sw.h, 16),
                 waits=list(ps["free"]))
            t_ld = (sw, sw.bump(16))
            ps["free"] = []
            return ("raw", stga, t_ld, None, par)

        stream = []
        hooks_after = {}

        def SU(src, nslots, consume):
            stream.append((False, src, nslots, consume))
            return len(stream) - 1

        def RU(src, nslots, consume):
            stream.append((True, src, nslots, consume))
            return len(stream) - 1

        def HOOK(fn):
            hooks_after.setdefault(len(stream) - 1, []).append(fn)

        def run_stream():
            n = len(stream)
            handles = {}

            def load(j):
                is_raw, src, ns, _ = stream[j]
                return pipe_load_raw(src, ns) if is_raw else pipe_load(src, ns)

            for j in range(min(DEPTH, n)):
                handles[j] = load(j)
            for i in range(n):
                kind, a1, a2, a3, par = handles.pop(i)
                consume = stream[i][3]
                if kind == "split":
                    tok = consume(a1, a2, a3)
                else:
                    tok = consume(a1, a2)
                pipe_state[par]["free"].append(tok)
                for fn in hooks_after.get(i, []):
                    fn()
                j = i + DEPTH
                if j < n:
                    handles[j] = load(j)
            stream.clear()
            hooks_after.clear()

        def mm3(ps_ap, wh, wl, xh, xl, nk):
            def body(e):
                for kc in range(nk):
                    e.matmul(ps_ap, wh[:, kc, :], xh[:, kc, :],
                             start=(kc == 0), stop=False)
                for kc in range(nk):
                    e.matmul(ps_ap, wh[:, kc, :], xl[:, kc, :],
                             start=False, stop=False)
                for kc in range(nk):
                    inst = e.matmul(ps_ap, wl[:, kc, :], xh[:, kc, :],
                                    start=False, stop=(kc == nk - 1))
                return inst
            return body

        def act_split(src_ap, hi_ap, lo_ap, waits):
            step("scalar", lambda e: e.activation(hi_ap, src_ap, AF.Copy),
                 waits=waits, incs=[(sS, 1)])
            t_hi = (sS, sS.bump())
            step("vector", lambda e: e.tensor_tensor(lo_ap, src_ap, hi_ap,
                                                     OP.subtract),
                 waits=[t_hi], incs=[(sV, 1)])
            return (sV, sV.bump())

        step("vector", lambda e: e.memset(ones[:, :], 1.0), incs=[(sV, 1)])
        t_ones = (sV, sV.bump())
        step("vector", lambda e: e.memset(carry[:, :], 0.0), incs=[(sV, 1)])
        t_car0 = (sV, sV.bump())
        step("vector", lambda e: e.memset(row(6), 1.0), incs=[(sV, 1)])
        t_onerow = (sV, sV.bump())

        for dst, src in ((bgs, bg2), (bvs, bv2), (bds, bd2), (b1s, b12),
                         (b2s, b22), (wmixs, wmix2), (wffns, wffn2), (selps, selp)):
            step("sync", lambda e, d=dst, s=src: e.dma_start(out=d[:, :], in_=s),
                 incs=[(sBIAS, 16)])
            sBIAS.bump(16)
        t_bias = (sBIAS, sBIAS.v)

        def mk_bq(e):
            CI("scalar", e.activation(bq16[:, :], b2s[:, :], AF.Copy))
            CW("scalar", e)
            return e.activation(bq32[:, :], bq16[:, :], AF.Copy)
        step("scalar", mk_bq, waits=[t_bias], incs=[(sS, 1)])
        t_bq = (sS, sS.bump())

        p1 = contextlib.ExitStack()
        sb1 = lambda name, shape, dt: p1.enter_context(nc.sbuf_tensor(name, shape, dt))
        xb0 = sb1("xb0", [128, KD, TC], F32)
        xb1 = sb1("xb1", [128, KD, TC], F32)
        xbufs = [xb0, xb1]
        xn_sb = sb1("xn_sb", [128, KD, TC], F32)
        xnh = sb1("xnh", [128, KD, TC], BF16)
        xnl = sb1("xnl", [128, KD, TC], BF16)
        xah = sb1("xah", [128, KD, TC], BF16)
        xal = sb1("xal", [128, KD, TC], BF16)
        xs_sb = sb1("xs_sb", [128, KD, TC], F32)
        a_sb = sb1("a_sb", [128, KD, TC], F32)
        hq_sb = sb1("hq_sb", [128, KH, TC], FP8)
        wq_sb = sb1("wq_sb", [128, KH, 128], FP8)
        wq2_sb = sb1("wq2_sb", [128, KH, 128], FP8)

        hs = {}

        def emit_load_x(c):
            h = hs.setdefault(c, {})
            sx = sX2[c % 2]
            xw = []
            if c >= 2:
                xw.append((sSTp[c % 2], sSTp[c % 2].v))
            step("sync", lambda e, a=c * TC, b=(c + 1) * TC, xb=xbufs[c % 2]:
                 e.dma_start(out=xb[:, :, :], in_=xTv[:, :, a:b]),
                 waits=xw, incs=[(sx, 16)])
            h["x"] = (sx, sx.bump(16))

        def emit_stats(c):
            h = hs[c]
            xb = xbufs[c % 2]
            pw = []
            if c >= 1:
                pw += [(sXN, sXN.v), hs[c - 1]["xnsp"], hs[c - 1]["xa"]]
            step("tensor", lambda e: [e.matmul(pb[4][0:1, 0:TC], ones[:, 0:1],
                                               xb[:, kc, :], start=(kc == 0),
                                               stop=(kc == KD - 1))
                                      for kc in range(KD)][-1],
                 waits=[h["x"], t_ones], incs=[(sT, 1)])
            t_sumx = (sT, sT.bump())
            t_sq = []
            for kc in range(KD):
                scr = xn_sb[:, kc % 2, :]
                w_ = [h["x"]] + pw if kc < 2 else [(sT, t_sq[kc - 2][1])]
                step("scalar", lambda e, kc=kc, scr=scr: e.activation(
                    scr, xb[:, kc, :], AF.Square), waits=w_, incs=[(sS, 1)])
                t_s = (sS, sS.bump())
                step("tensor", lambda e, kc=kc, scr=scr: e.matmul(
                    pb[5][0:1, 0:TC], ones[:, 0:1], scr,
                    start=(kc == 0), stop=(kc == KD - 1)),
                     waits=[t_s], incs=[(sT, 1)])
                t_sq.append((sT, sT.bump()))
            t_sumsq = t_sq[-1]

            ew = [t_sumx, t_sumsq]
            if c >= 1:
                ew += [hs[c - 1]["bc1"], hs[c - 1]["bc2"]]

            def evrows(e):
                CI("scalar", e.copy(row(0), pb[4][0:1, 0:TC]))
                return e.copy(row(1), pb[5][0:1, 0:TC])
            step("scalar", evrows, waits=ew, incs=[(sS, 1)])
            h["_ev"] = (sS, sS.bump())

        def emit_rows1(c):
            h = hs[c]

            def rows1(e):
                ops = [
                    lambda: e.tensor_scalar(row(0), row(0), INV_D, None, OP.mult),
                    lambda: e.tensor_scalar(row(1), row(1), INV_D, None, OP.mult),
                    lambda: e.tensor_tensor(row(2), row(0), row(0), OP.mult),
                    lambda: e.tensor_tensor(row(1), row(1), row(2), OP.subtract),
                    lambda: e.tensor_scalar(row(2), row(1), 1e-5, None, OP.add),
                    lambda: e.tensor_scalar(row(3), row(2), 1e-6, None, OP.mult),
                ]
                for op in ops:
                    CI("vector", op())
                    CW("vector", e)
                return e.tensor_tensor(row(3), row(1), row(3), OP.add)
            step("vector", rows1, waits=[h["_ev"], t_onerow], incs=[(sV, 1)])
            t_r1 = (sV, sV.bump())

            def sqrts(e):
                CI("scalar", e.activation(row(1), row(2), AF.Sqrt))
                return e.activation(row(4), row(3), AF.Sqrt)
            step("scalar", sqrts, waits=[t_r1], incs=[(sS, 1)])
            h["_sqr"] = (sS, sS.bump())

        def emit_rows2a(c):
            h = hs[c]

            def rows2a(e):
                ops = [
                    lambda: e.reciprocal(row(5), row(1)),
                    lambda: e.tensor_tensor(row(1), row(5), row(5), OP.mult),
                    lambda: e.tensor_tensor(row(1), row(2), row(1), OP.mult),
                    lambda: e.tensor_scalar(row(1), row(1), -0.5, 1.5, OP.mult,
                                            OP.add),
                ]
                for op in ops:
                    CI("vector", op())
                    CW("vector", e)
                return e.tensor_tensor(row(1), row(5), row(1), OP.mult)
            step("vector", rows2a, waits=[h["_sqr"]], incs=[(sV, 1)])
            h["_r2a"] = (sV, sV.bump())

        def emit_rows2b(c):
            h = hs[c]

            def rows2b(e):
                ops = [
                    lambda: e.reciprocal(row(5), row(4)),
                    lambda: e.tensor_tensor(row(4), row(5), row(5), OP.mult),
                    lambda: e.tensor_tensor(row(4), row(3), row(4), OP.mult),
                    lambda: e.tensor_scalar(row(4), row(4), -0.5, 1.5, OP.mult,
                                            OP.add),
                    lambda: e.tensor_tensor(row(4), row(5), row(4), OP.mult),
                    lambda: e.tensor_tensor(row(0), row(0), row(1), OP.mult),
                    lambda: e.tensor_tensor(row(5), row(2), row(1), OP.mult),
                ]
                for op in ops:
                    CI("vector", op())
                    CW("vector", e)
                return e.tensor_tensor(row(5), row(5), row(4), OP.mult)
            step("vector", rows2b, waits=[h["_r2a"]], incs=[(sV, 1)])
            h["rows"] = (sV, sV.bump())

        def emit_xn_a(c):
            h = hs[c]
            xb = xbufs[c % 2]
            bw = [h["rows"]]
            if c >= 1:
                bw += [hs[c - 1]["xa"]]
            step("tensor", lambda e: (e.matmul(pb[6][:, 0:TC], ones[0:1, :], row(1),
                                               start=True, stop=True),
                                      e.matmul(pb[7][:, 0:TC], ones[0:1, :], row(0),
                                               start=True, stop=True))[-1],
                 waits=bw, incs=[(sT, 1)])
            h["bc1"] = (sT, sT.bump())

            def mk_xn(e):
                for kc in range(KD):
                    inst = e.tensor_tensor(xn_sb[:, kc, :], xb[:, kc, :],
                                           pb[6][:, 0:TC], OP.mult)
                CI("vector", inst)
                CW("vector", e)
                for kc in range(KD):
                    inst = e.tensor_tensor(xn_sb[:, kc, :], xn_sb[:, kc, :],
                                           pb[7][:, 0:TC], OP.subtract)
                return inst
            step("vector", mk_xn, waits=[h["bc1"], h["x"]], incs=[(sV, 1)])
            h["xn"] = (sV, sV.bump())
            step("scalar", lambda e, a=c * TC, b=(c + 1) * TC: e.dma_start(
                out=xnv[:, :, a:b], in_=xn_sb[:, :, :]), waits=[h["xn"]],
                 incs=[(sXN, 16)])
            h["spill"] = (sXN, sXN.bump(16))

        def emit_xn_b(c):
            h = hs[c]
            step("tensor", lambda e: e.matmul(pb[6][:, 0:TC], ones[0:1, :], row(5),
                                              start=True, stop=True),
                 waits=[h["xn"]], incs=[(sT, 1)])
            h["bc2"] = (sT, sT.bump())

        def emit_xn_splits(c):
            h = hs[c]
            t_prev = (sT, sT.v)
            t_xnl = None
            for kc in range(KD):
                xw = [h["xn"]] + ([t_prev] if kc == 0 else [])
                t_xnl = act_split(xn_sb[:, kc, :], xnh[:, kc, :], xnl[:, kc, :], xw)
            h["xnsp"] = t_xnl

        def emit_xa_splits(c):
            h = hs[c]
            t_prev = (sT, sT.v)
            t_vec = (sV, sV.v)
            scrs = [sg_sb, tv_sb, sl_sb, sq2_sb]
            toks = {}
            for kc in range(KD):
                scr = scrs[kc % 4]
                aw = [h["bc2"]]
                if kc == 0:
                    aw += [t_prev, t_vec]
                if kc >= 4:
                    aw.append(toks[kc - 4])
                step("vector", lambda e, kc=kc, scr=scr: e.scalar_tensor_tensor(
                    scr[:, :], xn_sb[:, kc, :], wmixs[:, kc:kc + 1],
                    pb[6][:, 0:TC], OP.mult, OP.mult),
                     waits=aw, incs=[(sV, 1)])
                t_tmp = (sV, sV.bump())
                toks[kc] = act_split(scr[:, :], xah[:, kc, :], xal[:, kc, :], [t_tmp])
            h["xa"] = toks[KD - 1]

        gvd_cfg = ((bgs, AF.Sigmoid, sg_sb), (bvs, AF.Tanh, tv_sb),
                   (bds, AF.Sigmoid, sl_sb))

        def add_chunk1(c):
            gst = {"evac": {0: None, 1: None, 2: None}, "xs": None, "ls": None}

            def gvd_consume(wh, wl, wtoks, i, c=c, gst=gst):
                m, pi = i // 3, i % 3
                bias_t, func, dst = gvd_cfg[pi]
                mw = wtoks + [hs[c]["xa"]]
                if gst["evac"][pi] is not None:
                    mw.append(gst["evac"][pi])
                step("tensor", mm3(pb[pi][:, 0:TC], wh, wl, xah, xal, KD),
                     waits=mw, incs=[(sT, 1)])
                t_mm = (sT, sT.bump())
                if pi < 2:
                    ew = [t_mm] + ([gst["xs"]] if gst["xs"] else [])
                else:
                    ew = [t_mm] + ([gst["ls"]] if gst["ls"] else [])
                step("scalar", lambda e, m=m, pi=pi, bias_t=bias_t, func=func,
                     dst=dst: e.activation(dst[:, :], pb[pi][:, 0:TC], func,
                                           bias=bias_t[:, m:m + 1]),
                     waits=ew, incs=[(sS, 1)])
                gst["evac"][pi] = (sS, sS.bump())
                if pi == 2:
                    spw = [(sSP, sSP.v)] if (m == 0 and c >= 1) else []
                    step("vector", lambda e, m=m: e.tensor_tensor(
                        xs_sb[:, m, :], sg_sb[:, :], tv_sb[:, :], OP.mult),
                         waits=[gst["evac"][0], gst["evac"][1]] + spw,
                         incs=[(sV, 1)])
                    gst["xs"] = (sV, sV.bump())
                    step("vector", lambda e, m=m: e.tensor_scalar(
                        a_sb[:, m, :], sl_sb[:, :], 0.998, 0.001, OP.mult, OP.add),
                         waits=[gst["evac"][2]] + spw, incs=[(sV, 1)])
                    t_a = (sV, sV.bump())
                    cwait = [t_car0] if c == 0 else []

                    def lscan(e, m=m):
                        CW("vector", e)
                        CI("vector", e.tensor_tensor_scan(
                            sl_sb[:, :], a_sb[:, m, :], xs_sb[:, m, :],
                            carry[:, m:m + 1], OP.mult, OP.add))
                        CW("vector", e)
                        return e.tensor_copy(carry[:, m:m + 1], sl_sb[:, TC - 1:TC])
                    step("vector", lscan,
                         waits=[gst["xs"], t_a, gst["evac"][2]] + cwait,
                         incs=[(sV, 1)])
                    gst["ls"] = (sV, sV.bump())
                return t_mm

            for i in range(3 * KD):
                src = (wgT, wvT, wdT)[i % 3]
                m = i // 3
                SU(src[m], KD,
                   (lambda i: lambda wh, wl, wt: gvd_consume(wh, wl, wt, i))(i))
                if c + 1 < NCH:
                    if i == 0:
                        HOOK(lambda c=c: emit_load_x(c + 1))
                    elif i == 2:
                        HOOK(lambda c=c: emit_stats(c + 1))
                    elif i == 5:
                        HOOK(lambda c=c: emit_rows1(c + 1))
                    elif i == 7:
                        HOOK(lambda c=c: emit_rows2a(c + 1))
                    elif i == 9:
                        HOOK(lambda c=c: emit_rows2b(c + 1))
                    elif i == 12:
                        HOOK(lambda c=c: emit_xn_a(c + 1))
                    elif i == 16:
                        HOOK(lambda c=c: emit_xn_b(c + 1))

            def end_gvd(c=c, gst=gst):
                t_gvd_all = (sV, sV.v)
                step("scalar", lambda e, a=c * TC, b=(c + 1) * TC: e.dma_start(
                    out=xsv[:, :, a:b], in_=xs_sb[:, :, :]), waits=[t_gvd_all],
                     incs=[(sSP, 16)])
                step("scalar", lambda e, a=c * TC, b=(c + 1) * TC: e.dma_start(
                    out=av[:, :, a:b], in_=a_sb[:, :, :]), waits=[t_gvd_all],
                     incs=[(sSP, 16)])
                sSP.bump(32)
                if c == NCH - 1:
                    step("gpsimd", lambda e: e.dma_start(out=cc_in.ap(),
                                                         in_=carry[:, :]),
                         waits=[gst["ls"]], incs=[(sGD, 16)])
                    t_ccin = (sGD, sGD.bump(16))
                    step("gpsimd", lambda e: e.collective_compute(
                        "AllGather", OP.bypass,
                        replica_groups=[[0, 1], [2, 3], [4, 5], [6, 7]],
                        ins=[cc_in.ap()], outs=[cc_out.ap()]),
                         waits=[t_ccin], incs=[(sCC, 1)])
                    t_cc = (sCC, sCC.bump())
                    step("gpsimd", lambda e: e.dma_start(
                        out=hprev[:, :], in_=cc_out.ap()[0, :, :]), waits=[t_cc],
                         incs=[(sGD, 16)])
                    t_hp = (sGD, sGD.bump(16))
                    step("vector", lambda e: e.tensor_scalar(
                        carry2[:, :], hprev[:, :], selps[:, 0:1], None, OP.mult),
                         waits=[t_hp], incs=[(sV, 1)])
                    hs["c2"] = (sV, sV.bump())
            HOOK(end_gvd)

            w1st = {"h": None}

            def w1_consume(wh, wl, wtoks, m, c=c, w1st=w1st):
                mw = wtoks + [hs[c]["xnsp"]]
                if w1st["h"] is not None:
                    mw.append(w1st["h"])
                step("tensor", mm3(pb[3][:, 0:TC], wh, wl, xnh, xnl, KD),
                     waits=mw, incs=[(sT, 1)])
                t_mm = (sT, sT.bump())
                step("scalar", lambda e, m=m: e.activation(
                    hq_sb[:, m, :], pb[3][:, 0:TC], AF.Relu, bias=b1s[:, m:m + 1]),
                     waits=[t_mm], incs=[(sS, 1)])
                w1st["h"] = (sS, sS.bump())
                return t_mm

            for m in range(KH):
                SU(w1T[m], KD,
                   (lambda m: lambda wh, wl, wt: w1_consume(wh, wl, wt, m))(m))
                if m == 2 and c + 1 < NCH:
                    HOOK(lambda c=c: emit_xa_splits(c + 1))
            if c + 1 < NCH:
                HOOK(lambda c=c: emit_xn_splits(c + 1))

            w2st = {"casts": [], "f": None, "part": None, "mm": None}

            def w2_tail(mo, t_mm, c=c, w2st=w2st):
                fw = [t_mm, t_bq]
                if w2st["part"] is not None:
                    fw.append(w2st["part"])
                step("scalar", lambda e, mo=mo: e.activation(
                    f16_sb[:, :], pb[3][:, 0:TC], AF.Identity,
                    bias=bq32[:, mo:mo + 1]), waits=fw, incs=[(sS, 1)])
                w2st["f"] = (sS, sS.bump())

                def mkpart(e, mo=mo, xb=xbufs[c % 2]):
                    CI("vector", e.tensor_copy(tmpf[:, :], f16_sb[:, :]))
                    CW("vector", e)
                    return e.tensor_tensor(xb[:, mo, :], xb[:, mo, :],
                                           tmpf[:, :], OP.add)
                step("vector", mkpart, waits=[w2st["f"]], incs=[(sV, 1)])
                w2st["part"] = (sV, sV.bump())

            def w2_consume(stga, t_ld, q, mo, c=c, w2st=w2st, w1st=w1st):
                vwt = [t_ld]
                if w2st["mm"] is not None:
                    vwt.append(w2st["mm"])
                if q == 0:
                    vwt.append((sWQ, sWQ.v))
                step("vector", lambda e, stga=stga, q=q: e.tensor_copy(
                    wq_sb[:, q * 8:(q + 1) * 8, :], stga[:, 0:8, :]),
                     waits=vwt, incs=[(sV, 1)])
                t_cast = (sV, sV.bump())
                w2st["casts"].append(t_cast)
                if q < 3:
                    return t_cast
                casts = list(w2st["casts"])
                w2st["casts"] = []
                step("scalar", lambda e, mo=mo: e.dma_start(
                    out=w2q8[mo], in_=wq_sb[:, :, :]), waits=casts,
                     incs=[(sWQ, 16)])
                sWQ.bump(16)
                mw = casts + [w1st["h"]]
                if w2st["f"] is not None:
                    mw.append(w2st["f"])

                def mmw2(e):
                    for j in range(KH // 2):
                        inst = e.matmul(pb[3][:, 0:TC],
                                        wq_sb[:, 2 * j:2 * j + 2, :],
                                        hq_sb[:, 2 * j:2 * j + 2, :],
                                        start=(j == 0), stop=(j == KH // 2 - 1),
                                        perf_mode=DR)
                    return inst
                step("tensor", mmw2, waits=mw, incs=[(sT, 1)])
                w2st["mm"] = (sT, sT.bump())
                w2_tail(mo, w2st["mm"])
                return w2st["mm"]

            if c == 0:
                for mo in range(KD):
                    for q in range(4):
                        RU(w2T[mo][:, q * 8:(q + 1) * 8, :], 8,
                           (lambda q, mo: lambda stga, t_ld:
                            w2_consume(stga, t_ld, q, mo))(q, mo))

            def w2_cached(c=c, w2st=w2st, w1st=w1st):
                mms = {}
                for mo in range(KD):
                    wq = wq_sb if mo % 2 == 0 else wq2_sb
                    sq = sQ2[mo % 2]
                    lw = [(sWQ, sWQ.v)]
                    if mo >= 2:
                        lw.append(mms[mo - 2])
                    else:
                        lw.append(hs[c - 1]["w2mm"][mo - 2 + KD])
                    step("sync", lambda e, mo=mo, wq=wq, sq=sq: e.dma_start(
                        out=wq[:, :, :], in_=w2q8[mo]).then_inc(sq.h, 16),
                         waits=lw)
                    t_l = (sq, sq.bump(16))
                    mw = [t_l, w1st["h"]]
                    if w2st["f"] is not None:
                        mw.append(w2st["f"])

                    def mmw2(e, wq=wq):
                        for j in range(KH // 2):
                            inst = e.matmul(pb[3][:, 0:TC],
                                            wq[:, 2 * j:2 * j + 2, :],
                                            hq_sb[:, 2 * j:2 * j + 2, :],
                                            start=(j == 0),
                                            stop=(j == KH // 2 - 1),
                                            perf_mode=DR)
                        return inst
                    step("tensor", mmw2, waits=mw, incs=[(sT, 1)])
                    w2st["mm"] = (sT, sT.bump())
                    mms[mo] = w2st["mm"]
                    w2_tail(mo, w2st["mm"])
                hs[c]["w2mm"] = mms

            def end_w2(c=c, w2st=w2st):
                if c == 0:
                    hs[0]["w2mm"] = {6: w2st["mm"], 7: w2st["mm"]}
                if c >= 1:
                    w2_cached()
                sst = sSTp[c % 2]
                step("scalar", lambda e, a=c * TC, b=(c + 1) * TC,
                     xb=xbufs[c % 2], sst=sst:
                     e.dma_start(out=outv[:, :, a:b], in_=xb[:, :, :]),
                     waits=[w2st["part"]], incs=[(sst, 16)])
                hs[c]["pstore"] = (sst, sst.bump(16))
            HOOK(end_w2)

        emit_load_x(0)
        emit_stats(0)
        emit_rows1(0)
        emit_rows2a(0)
        emit_rows2b(0)
        emit_xn_a(0)
        emit_xn_b(0)
        emit_xa_splits(0)
        emit_xn_splits(0)
        for c in range(NCH):
            add_chunk1(c)
        run_stream()

        emit_block()
        p1.close()

        p3 = contextlib.ExitStack()
        sb3 = lambda name, shape, dt: p3.enter_context(nc.sbuf_tensor(name, shape, dt))
        q_xn = sb3("q_xn", [128, KD, TC], F32)
        q_xs = sb3("q_xs", [128, KD, TC], F32)
        q_a = sb3("q_a", [128, KD, TC], F32)
        q_pa = sb3("q_pa", [128, KD, TC], F32)
        q_s = sb3("q_s", [128, KD, TC], F32)
        xbh = sb3("xbh", [128, KD, TC], BF16)
        xbl = sb3("xbl", [128, KD, TC], BF16)
        pmh = sb3("pmh", [128, KF, TC], BF16)
        pml = sb3("pml", [128, KF, TC], BF16)

        h3 = {}

        def p3_boundary():
            return [(sXN, sXN.v), (sSP, sSP.v), (sT, sT.v), (sS, sS.v),
                    (sV, sV.v), (sSTp[0], sSTp[0].v), (sSTp[1], sSTp[1].v)]

        def p3_load_xsa(c):
            h = h3.setdefault(c, {})
            lw = p3_boundary() if c == 0 else [h3[c - 1]["scan"]]
            step("sync", lambda e, a=c * TC, b=(c + 1) * TC: e.dma_start(
                out=q_xs[:, :, :], in_=xsv[:, :, a:b]), waits=lw, incs=[(sP3A, 16)])
            sP3A.bump(16)
            step("sync", lambda e, a=c * TC, b=(c + 1) * TC: e.dma_start(
                out=q_a[:, :, :], in_=av[:, :, a:b]), incs=[(sP3A, 16)])
            sP3A.bump(16)
            h["lxa"] = (sP3A, sP3A.v)

        def p3_scans(c):
            h = h3[c]
            sw = [h["lxa"]]
            if c == 0:
                sw.append(hs["c2"])
            else:
                sw.append(h3[c - 1]["part2"])

            def scans(e):
                for m in range(KD):
                    CW("vector", e)
                    CI("vector", e.tensor_tensor_scan(
                        q_s[:, m, :], q_a[:, m, :], q_xs[:, m, :],
                        carry2[:, m:m + 1], OP.mult, OP.add))
                    CW("vector", e)
                    inst = e.tensor_copy(carry2[:, m:m + 1], q_s[:, m, TC - 1:TC])
                    if m < KD - 1:
                        CI("vector", inst)
                return inst
            step("vector", scans, waits=sw, incs=[(sV, 1)])
            h["scan"] = (sV, sV.bump())

        def p3_load_xn(c):
            h = h3[c]
            lw = p3_boundary() if c == 0 else [h3[c - 1]["part1"]]
            step("sync", lambda e, a=c * TC, b=(c + 1) * TC: e.dma_start(
                out=q_xn[:, :, :], in_=xnv[:, :, a:b]), waits=lw, incs=[(sP3X, 16)])
            h["lxn"] = (sP3X, sP3X.bump(16))

        def p3_part1(c):
            h = h3[c]
            step("vector", lambda e: e.tensor_tensor(
                q_s[:, :, :], q_xn[:, :, :], q_s[:, :, :], OP.add),
                 waits=[h["lxn"], h["scan"]], incs=[(sV, 1)])
            h["part1"] = (sV, sV.bump())

        def p3_load_pa(c):
            h = h3.setdefault(c, {})
            lw = [(sSTp[0], sSTp[0].v), (sSTp[1], sSTp[1].v)]
            step("sync", lambda e, a=c * TC, b=(c + 1) * TC: e.dma_start(
                out=q_pa[:, :, :], in_=outv[:, :, a:b]), waits=lw, incs=[(sP3P, 16)])
            h["lpa"] = (sP3P, sP3P.bump(16))

        def p3_part2(c):
            h = h3[c]
            step("vector", lambda e: e.tensor_tensor(
                q_pa[:, :, :], q_pa[:, :, :], q_s[:, :, :], OP.add),
                 waits=[h["lpa"], h["part1"]], incs=[(sV, 1)])
            h["part2"] = (sV, sV.bump())

        def p3_stats(c):
            h = h3[c]
            t_sq3 = []
            fw1 = [h3[c - 1]["fin1"]] if c >= 1 else []
            for kc in range(KD):
                scr = sl_sb if kc % 2 == 0 else sq2_sb
                w_ = [h["part1"]] + fw1 if kc < 2 else [(sT, t_sq3[kc - 2][1])]
                step("scalar", lambda e, kc=kc, scr=scr: e.activation(
                    scr[:, :], q_s[:, kc, :], AF.Square), waits=w_, incs=[(sS, 1)])
                t_s3 = (sS, sS.bump())
                step("tensor", lambda e, kc=kc, scr=scr: e.matmul(
                    pb[2][0:1, 0:TC], ones[:, 0:1], scr[:, :],
                    start=(kc == 0), stop=(kc == KD - 1)), waits=[t_s3],
                     incs=[(sT, 1)])
                t_sq3.append((sT, sT.bump()))
            step("scalar", lambda e: e.copy(row(0), pb[2][0:1, 0:TC]),
                 waits=[t_sq3[-1]], incs=[(sS, 1)])
            h["_ev3"] = (sS, sS.bump())

        def p3_rows3a(c):
            h = h3[c]
            step("vector", lambda e: e.tensor_scalar(
                row(1), row(0), INV_D, 1e-6, OP.mult, OP.add),
                 waits=[h["_ev3"]], incs=[(sV, 1)])
            t_u3 = (sV, sV.bump())
            step("scalar", lambda e: e.activation(row(2), row(1), AF.Sqrt),
                 waits=[t_u3], incs=[(sS, 1)])
            h["_y3"] = (sS, sS.bump())

        def p3_rows3b(c):
            h = h3[c]

            def rows3(e):
                ops = [
                    lambda: e.reciprocal(row(3), row(2)),
                    lambda: e.tensor_tensor(row(2), row(3), row(3), OP.mult),
                    lambda: e.tensor_tensor(row(2), row(1), row(2), OP.mult),
                    lambda: e.tensor_scalar(row(2), row(2), -0.5, 1.5, OP.mult,
                                            OP.add),
                ]
                for op in ops:
                    CI("vector", op())
                    CW("vector", e)
                return e.tensor_tensor(row(4), row(3), row(2), OP.mult)
            step("vector", rows3, waits=[h["_y3"]], incs=[(sV, 1)])
            t_r3 = (sV, sV.bump())
            step("tensor", lambda e: e.matmul(pb[3][:, 0:TC], ones[0:1, :],
                                              row(4), start=True, stop=True),
                 waits=[t_r3], incs=[(sT, 1)])
            h["bc3"] = (sT, sT.bump())

        def p3_xbsplit(c):
            h = h3[c]
            scrs = [sg_sb, tv_sb, sl_sb, sq2_sb]
            toks = {}
            for kc in range(KD):
                scr = scrs[kc % 4]
                aw = [h["bc3"]]
                if kc == 0:
                    aw += [(sS, sS.v), (sV, sV.v)]
                if kc >= 4:
                    aw.append(toks[kc - 4])
                step("vector", lambda e, kc=kc, scr=scr: e.scalar_tensor_tensor(
                    scr[:, :], q_s[:, kc, :], wffns[:, kc:kc + 1],
                    pb[3][:, 0:TC], OP.mult, OP.mult),
                     waits=aw, incs=[(sV, 1)])
                t_tmp = (sV, sV.bump())
                toks[kc] = act_split(scr[:, :], xbh[:, kc, :], xbl[:, kc, :], [t_tmp])
            h["xb"] = toks[KD - 1]

        def add_chunk3(c):
            g3 = {"sil": None, "pm": None, "wo": None, "pml": None, "sw2": None}

            def g3_consume(wh, wl, wtoks, i, c=c, g3=g3):
                m, r = i // 3, i % 3
                if r == 0:
                    mw = wtoks + [h3[c]["xb"]]
                    if g3["sil"] is not None:
                        mw.append(g3["sil"])
                    if m == 0 and c >= 1:
                        mw.append(h3[c - 1]["fin2"])
                    step("tensor", mm3(pb[0][:, 0:TC], wh, wl, xbh, xbl, KD),
                         waits=mw, incs=[(sT, 1)])
                    t_g = (sT, sT.bump())
                    sw2 = [t_g] + ([g3["pm"]] if g3["pm"] else [])
                    step("scalar", lambda e: e.activation(
                        sg_sb[:, :], pb[0][:, 0:TC], AF.Silu),
                         waits=sw2, incs=[(sS, 1)])
                    g3["sil"] = (sS, sS.bump())
                    return t_g
                if r == 1:
                    mw2 = wtoks + [h3[c]["xb"]]
                    if g3["pm"] is not None:
                        mw2.append(g3["pm"])
                    if m == 0 and c >= 1:
                        mw2.append(h3[c - 1]["fin2"])
                    step("tensor", mm3(pb[1][:, 0:TC], wh, wl, xbh, xbl, KD),
                         waits=mw2, incs=[(sT, 1)])
                    t_u = (sT, sT.bump())
                    step("vector", lambda e: e.tensor_tensor(
                        tmpf[:, :], sg_sb[:, :], pb[1][:, 0:TC], OP.mult),
                         waits=[g3["sil"], t_u] + ([g3["pml"]] if g3["pml"] else []),
                         incs=[(sV, 1)])
                    g3["pm"] = (sV, sV.bump())
                    g3["pml"] = act_split(tmpf[:, :], pmh[:, m, :], pml[:, m, :],
                                          [g3["pm"]])
                    return t_u

                def mmwo(e, wh=wh, wl=wl, m=m):
                    for mo in range(6):
                        e.matmul(pb[2 + mo][:, 0:TC], wh[:, mo, :], pmh[:, m, :],
                                 start=(m == 0), stop=False)
                        e.matmul(pb[2 + mo][:, 0:TC], wh[:, mo, :], pml[:, m, :],
                                 start=False, stop=False)
                        inst = e.matmul(pb[2 + mo][:, 0:TC], wl[:, mo, :],
                                        pmh[:, m, :], start=False,
                                        stop=(m == KF - 1))
                    return inst
                wo_w = wtoks + [g3["pml"]]
                if m == 0 and c >= 1:
                    wo_w.append(h3[c - 1]["fin1"])
                step("tensor", mmwo, waits=wo_w, incs=[(sT, 1)])
                g3["wo"] = (sT, sT.bump())
                return g3["wo"]

            for i in range(3 * KF):
                m, r = i // 3, i % 3
                if r == 0:
                    src, ns = wgateT[m], KD
                elif r == 1:
                    src, ns = wupT[m], KD
                else:
                    src = woutv[m, :, 0:768].rearrange("p (s q) -> p s q", q=128)
                    ns = 6
                SU(src, ns,
                   (lambda i: lambda wh, wl, wt: g3_consume(wh, wl, wt, i))(i))
                if c + 1 < NCH:
                    if i == 2:
                        HOOK(lambda c=c: p3_load_xsa(c + 1))
                    elif i == 6:
                        HOOK(lambda c=c: p3_scans(c + 1))
                    elif i == 40:
                        HOOK(lambda c=c: p3_load_xn(c + 1))
                    elif i == 44:
                        HOOK(lambda c=c: p3_part1(c + 1))

            def fin1(c=c, g3=g3):
                def finev1(e):
                    for mo in range(6):
                        inst = e.tensor_tensor(q_pa[:, mo, :], q_pa[:, mo, :],
                                               pb[2 + mo][:, 0:TC], OP.add)
                    return inst
                step("vector", finev1, waits=[g3["wo"], h3[c]["part2"]],
                     incs=[(sV, 1)])
                h3[c]["fin1"] = (sV, sV.bump())
                if c + 1 < NCH:
                    p3_stats(c + 1)
            HOOK(fin1)

            def sw2_consume(wh, wl, wtoks, m, c=c, g3=g3):
                mw3 = list(wtoks)
                if m == 0:
                    mw3 += [g3["pm"], g3["sil"], g3["pml"], h3[c]["fin1"]]

                def mm2(e, wh=wh, wl=wl, m=m):
                    for mo in range(2):
                        e.matmul(pb[mo][:, 0:TC], wh[:, mo, :], pmh[:, m, :],
                                 start=(m == 0), stop=False)
                        e.matmul(pb[mo][:, 0:TC], wh[:, mo, :], pml[:, m, :],
                                 start=False, stop=False)
                        inst = e.matmul(pb[mo][:, 0:TC], wl[:, mo, :],
                                        pmh[:, m, :], start=False,
                                        stop=(m == KF - 1))
                    return inst
                step("tensor", mm2, waits=mw3, incs=[(sT, 1)])
                g3["sw2"] = (sT, sT.bump())
                return g3["sw2"]

            for m in range(KF):
                SU(woutv[m, :, 768:1024].rearrange("p (s q) -> p s q", q=128), 2,
                   (lambda m: lambda wh, wl, wt: sw2_consume(wh, wl, wt, m))(m))
                if c + 1 < NCH:
                    if m == 1:
                        HOOK(lambda c=c: p3_rows3a(c + 1))
                    elif m == 4:
                        HOOK(lambda c=c: p3_rows3b(c + 1))
                    elif m == 8:
                        HOOK(lambda c=c: p3_xbsplit(c + 1))

            def fin2(c=c, g3=g3):
                def finev2(e):
                    CI("vector", e.tensor_tensor(q_pa[:, 6, :], q_pa[:, 6, :],
                                                 pb[0][:, 0:TC], OP.add))
                    return e.tensor_tensor(q_pa[:, 7, :], q_pa[:, 7, :],
                                           pb[1][:, 0:TC], OP.add)
                step("vector", finev2, waits=[g3["sw2"]], incs=[(sV, 1)])
                h3[c]["fin2"] = (sV, sV.bump())
                sst = sSTp[c % 2]
                step("scalar", lambda e, a=c * TC, b=(c + 1) * TC, sst=sst:
                     e.dma_start(out=outv[:, :, a:b], in_=q_pa[:, :, :]),
                     waits=[h3[c]["fin1"], h3[c]["fin2"]], incs=[(sst, 16)])
                sst.bump(16)
                if c + 1 < NCH:
                    p3_load_pa(c + 1)
                    p3_part2(c + 1)
            HOOK(fin2)

        p3_load_xsa(0)
        p3_scans(0)
        p3_load_xn(0)
        p3_part1(0)
        p3_load_pa(0)
        p3_part2(0)
        p3_stats(0)
        p3_rows3a(0)
        p3_rows3b(0)
        p3_xbsplit(0)
        for c in range(NCH):
            add_chunk3(c)
        run_stream()

        step("gpsimd", None, waits=[(sSTp[0], sSTp[0].v), (sSTp[1], sSTp[1].v)])
        step("sync", None, waits=[(sSTp[0], sSTp[0].v), (sSTp[1], sSTp[1].v)])

        emit_block()
        p3.close()

    return nc


def kernel(L0_attn_x, L0_attn_rms_mix_w, L0_attn_rms_ffn_w, L0_attn_Wg, L0_attn_bg,
           L0_attn_Wv, L0_attn_bv, L0_attn_Wd, L0_attn_bd, L0_attn_W_gate,
           L0_attn_W_up, L0_attn_W_out, L0_ffn_w1, L0_ffn_b1, L0_ffn_w2, L0_ffn_b2):
    x = np.asarray(L0_attn_x, dtype=np.float32)

    def t(a):
        return np.ascontiguousarray(np.asarray(a, dtype=np.float32).T)

    def tw(a):
        w = np.asarray(a, dtype=np.float32)
        M, K = w.shape
        return np.ascontiguousarray(
            w.reshape(M // 128, 128, K // 128, 128).transpose(0, 3, 2, 1))

    def b2(a, k):
        return np.ascontiguousarray(np.asarray(a, dtype=np.float32).reshape(k, 128).T)

    shared = {
        "wgT": tw(L0_attn_Wg), "wvT": tw(L0_attn_Wv), "wdT": tw(L0_attn_Wd),
        "wgateT": tw(L0_attn_W_gate), "wupT": tw(L0_attn_W_up),
        "woutT": t(L0_attn_W_out),
        "w1T": tw(L0_ffn_w1), "w2T": tw(L0_ffn_w2),
        "bg2": b2(L0_attn_bg, KD), "bv2": b2(L0_attn_bv, KD), "bd2": b2(L0_attn_bd, KD),
        "b12": b2(L0_ffn_b1, KH), "b22": b2(L0_ffn_b2, KD),
        "wmix2": b2(L0_attn_rms_mix_w, KD), "wffn2": b2(L0_attn_rms_ffn_w, KD),
    }
    in_maps = []
    for core in range(8):
        b, h = core // 2, core % 2
        xc = np.ascontiguousarray(x[b, h * T:(h + 1) * T, :].T)
        d = dict(shared)
        d["xT"] = xc
        d["selp"] = np.full((128, 1), float(h), dtype=np.float32)
        in_maps.append(d)

    nc = build_nc()
    res = run_bass_kernel_spmd(nc, in_maps, core_ids=list(range(8)))
    out = np.empty((B, S, D), dtype=np.float32)
    for core in range(8):
        b, h = core // 2, core % 2
        out[b, h * T:(h + 1) * T, :] = res.results[core]["outT"].T
    return out


# version 20
# speedup vs baseline: 1.2359x; 1.1165x over previous
import contextlib

import numpy as np

import concourse.bass as bass
import concourse.mybir as mybir
from concourse.bass_utils import run_bass_kernel_spmd

F32 = mybir.dt.float32
DR = mybir.MatmulPerfMode.DoubleRow
BF16 = mybir.dt.bfloat16
FP8 = mybir.dt.float8e4
F16 = mybir.dt.float16
AF = mybir.ActivationFunctionType
OP = mybir.AluOpType

B, S, D, F, H = 4, 4096, 1024, 3072, 4096
T = 2048
KD, KF, KH = D // 128, F // 128, H // 128
TC, NCH = 512, 4
INV_D = 1.0 / D
PIPE_SLOTS = 8
NPIPE = 4
DEPTH = 3


class Sem:
    def __init__(self, nc, st, name):
        self.h = st.enter_context(nc.semaphore(name))
        self.v = 0

    def bump(self, n=1):
        self.v += n
        return self.v


def build_nc():
    nc = bass.Bass("TRN2", target_bir_lowering=False, debug=False)

    xT = nc.dram_tensor("xT", [D, T], F32, kind="ExternalInput").ap()
    wgT = nc.dram_tensor("wgT", [KD, 128, KD, 128], F32, kind="ExternalInput").ap()
    wvT = nc.dram_tensor("wvT", [KD, 128, KD, 128], F32, kind="ExternalInput").ap()
    wdT = nc.dram_tensor("wdT", [KD, 128, KD, 128], F32, kind="ExternalInput").ap()
    wgateT = nc.dram_tensor("wgateT", [KF, 128, KD, 128], F32, kind="ExternalInput").ap()
    wupT = nc.dram_tensor("wupT", [KF, 128, KD, 128], F32, kind="ExternalInput").ap()
    woutT = nc.dram_tensor("woutT", [F, D], F32, kind="ExternalInput").ap()
    w1T = nc.dram_tensor("w1T", [KH, 128, KD, 128], F32, kind="ExternalInput").ap()
    w2T = nc.dram_tensor("w2T", [KD, 128, KH, 128], F32, kind="ExternalInput").ap()
    w2q8 = nc.dram_tensor("w2q8", [KD, 128, KH, 128], FP8).ap()
    bg2 = nc.dram_tensor("bg2", [128, KD], F32, kind="ExternalInput").ap()
    bv2 = nc.dram_tensor("bv2", [128, KD], F32, kind="ExternalInput").ap()
    bd2 = nc.dram_tensor("bd2", [128, KD], F32, kind="ExternalInput").ap()
    b12 = nc.dram_tensor("b12", [128, KH], F32, kind="ExternalInput").ap()
    b22 = nc.dram_tensor("b22", [128, KD], F32, kind="ExternalInput").ap()
    wmix2 = nc.dram_tensor("wmix2", [128, KD], F32, kind="ExternalInput").ap()
    wffn2 = nc.dram_tensor("wffn2", [128, KD], F32, kind="ExternalInput").ap()
    selp = nc.dram_tensor("selp", [128, 1], F32, kind="ExternalInput").ap()

    outT = nc.dram_tensor("outT", [D, T], F32, kind="ExternalOutput").ap()

    xn_h = nc.dram_tensor("xn_h", [D, T], F32).ap()
    xs_h = nc.dram_tensor("xs_h", [D, T], F32).ap()
    a_h = nc.dram_tensor("a_h", [D, T], F32).ap()
    cc_in = nc.dram_tensor("cc_in", [128, KD], F32)
    cc_out = nc.dram_tensor("cc_out", [2, 128, KD], F32)

    def fm(ap_):
        return ap_.rearrange("(k p) t -> p k t", p=128)

    xTv, xnv, xsv, av, outv = fm(xT), fm(xn_h), fm(xs_h), fm(a_h), fm(outT)
    woutv = woutT.rearrange("(k p) m -> k p m", p=128)

    P = []

    def step(eng, body=None, waits=(), incs=()):
        waits = [(s.h, v) for (s, v) in waits]
        incs_ = [(s.h, n) for (s, n) in incs]

        def fn(e):
            for h, v in waits:
                e.wait_ge(h, v)
            inst = body(e) if body is not None else None
            if incs_:
                if inst is None:
                    assert body is None
                    inst = e.nop()
                for h, n in incs_:
                    inst.then_inc(h, n)

        P.append((eng, fn))

    def emit_block():
        with nc.Block() as blk:
            for eng_name in ("sync", "scalar", "vector", "tensor", "gpsimd"):
                def make(eng_name):
                    def emit(e):
                        for en, fn in P:
                            if en == eng_name:
                                fn(e)
                    return emit
                getattr(blk, eng_name)(make(eng_name))
        P.clear()

    with contextlib.ExitStack() as st:
        sb = lambda name, shape, dt: st.enter_context(nc.sbuf_tensor(name, shape, dt))

        ones = sb("ones", [128, 128], F32)
        bgs = sb("bgs", [128, KD], F32)
        bvs = sb("bvs", [128, KD], F32)
        bds = sb("bds", [128, KD], F32)
        b1s = sb("b1s", [128, KH], F32)
        b2s = sb("b2s", [128, KD], F32)
        bq16 = sb("bq16", [128, KD], F16)
        bq32 = sb("bq32", [128, KD], F32)
        wmixs = sb("wmixs", [128, KD], F32)
        wffns = sb("wffns", [128, KD], F32)
        selps = sb("selps", [128, 1], F32)
        carry = sb("carry", [128, KD], F32)
        carry2 = sb("carry2", [128, KD], F32)
        hprev = sb("hprev", [128, KD], F32)
        sg_sb = sb("sg_sb", [128, TC], F32)
        tv_sb = sb("tv_sb", [128, TC], F32)
        sl_sb = sb("sl_sb", [128, TC], F32)
        sq2_sb = sb("sq2_sb", [128, TC], F32)
        tmpf = sb("tmpf", [128, TC], F32)
        f16_sb = sb("f16_sb", [128, TC], F16)
        stat = sb("stat", [1, 7 * TC], F32)
        stg = [sb(f"stg{i}", [128, PIPE_SLOTS, 128], F32) for i in range(NPIPE)]
        wsh = [sb(f"wsh{i}", [128, PIPE_SLOTS, 128], BF16) for i in range(NPIPE)]
        wsl = [sb(f"wsl{i}", [128, PIPE_SLOTS, 128], BF16) for i in range(NPIPE)]

        pb = [st.enter_context(nc.psum_tensor(f"pb{i}", [128, 512], F32))
              for i in range(8)]

        sBIAS = Sem(nc, st, "sBIAS")
        sX2 = [Sem(nc, st, "sXa"), Sem(nc, st, "sXb")]
        sWp = [Sem(nc, st, f"sW{i}") for i in range(NPIPE)]
        sP3A = Sem(nc, st, "sP3A")
        sP3X = Sem(nc, st, "sP3X")
        sP3P = Sem(nc, st, "sP3P")
        sSTp = [Sem(nc, st, "sSTa"), Sem(nc, st, "sSTb")]
        sXN = Sem(nc, st, "sXN")
        sSP = Sem(nc, st, "sSP")
        sGD = Sem(nc, st, "sGD")
        sWQ = Sem(nc, st, "sWQ")
        sQ2 = [Sem(nc, st, "sQ2a"), Sem(nc, st, "sQ2b")]
        sT = Sem(nc, st, "sT")
        sV = Sem(nc, st, "sV")
        sS = Sem(nc, st, "sS")
        sG = Sem(nc, st, "sG")
        sCC = Sem(nc, st, "sCC")
        chain = {n: Sem(nc, st, f"ch_{n[:3]}") for n in ("vector", "scalar", "gpsimd")}

        def CI(en, inst):
            s = chain[en]
            s.v += 1
            inst.then_inc(s.h, 1)
            return inst

        def CW(en, e):
            s = chain[en]
            e.wait_ge(s.h, s.v)

        def row(i, n=TC):
            return stat[0:1, i * TC: i * TC + n]

        pipe_state = {i: {"free": []} for i in range(NPIPE)}
        pipe_par = [0]

        def pipe_load(src_ap, nslots):
            par = pipe_par[0]
            pipe_par[0] = (par + 1) % NPIPE
            ps = pipe_state[par]
            sw = sWp[par]
            stga, wha, wla = stg[par], wsh[par], wsl[par]
            step("sync", lambda e: e.dma_start(
                out=stga[:, 0:nslots, :], in_=src_ap).then_inc(sw.h, 16),
                 waits=list(ps["free"]))
            t_ld = (sw, sw.bump(16))
            step("scalar", lambda e: e.activation(
                wha[:, 0:nslots, :], stga[:, 0:nslots, :], AF.Copy),
                 waits=[t_ld], incs=[(sS, 1)])
            t_hi = (sS, sS.bump())
            step("vector", lambda e: e.tensor_tensor(
                wla[:, 0:nslots, :], stga[:, 0:nslots, :], wha[:, 0:nslots, :],
                OP.subtract), waits=[t_hi], incs=[(sV, 1)])
            t_lo = (sV, sV.bump())
            ps["free"] = [t_lo]
            return ("split", wha, wla, [t_hi, t_lo], par)

        def pipe_load_raw(src_ap, nslots):
            par = pipe_par[0]
            pipe_par[0] = (par + 1) % NPIPE
            ps = pipe_state[par]
            sw = sWp[par]
            stga = stg[par]
            step("sync", lambda e: e.dma_start(
                out=stga[:, 0:nslots, :], in_=src_ap).then_inc(sw.h, 16),
                 waits=list(ps["free"]))
            t_ld = (sw, sw.bump(16))
            ps["free"] = []
            return ("raw", stga, t_ld, None, par)

        stream = []
        hooks_after = {}

        def SU(src, nslots, consume):
            stream.append((False, src, nslots, consume))
            return len(stream) - 1

        def RU(src, nslots, consume):
            stream.append((True, src, nslots, consume))
            return len(stream) - 1

        def HOOK(fn):
            hooks_after.setdefault(len(stream) - 1, []).append(fn)

        def run_stream():
            n = len(stream)
            handles = {}

            def load(j):
                is_raw, src, ns, _ = stream[j]
                return pipe_load_raw(src, ns) if is_raw else pipe_load(src, ns)

            for j in range(min(DEPTH, n)):
                handles[j] = load(j)
            for i in range(n):
                kind, a1, a2, a3, par = handles.pop(i)
                consume = stream[i][3]
                if kind == "split":
                    tok = consume(a1, a2, a3)
                else:
                    tok = consume(a1, a2)
                pipe_state[par]["free"].append(tok)
                for fn in hooks_after.get(i, []):
                    fn()
                j = i + DEPTH
                if j < n:
                    handles[j] = load(j)
            stream.clear()
            hooks_after.clear()

        def mm3(ps_ap, wh, wl, xh, xl, nk):
            def body(e):
                for kc in range(nk):
                    e.matmul(ps_ap, wh[:, kc, :], xh[:, kc, :],
                             start=(kc == 0), stop=False)
                for kc in range(nk):
                    e.matmul(ps_ap, wh[:, kc, :], xl[:, kc, :],
                             start=False, stop=False)
                for kc in range(nk):
                    inst = e.matmul(ps_ap, wl[:, kc, :], xh[:, kc, :],
                                    start=False, stop=(kc == nk - 1))
                return inst
            return body

        def act_split(src_ap, hi_ap, lo_ap, waits):
            step("scalar", lambda e: e.activation(hi_ap, src_ap, AF.Copy),
                 waits=waits, incs=[(sS, 1)])
            t_hi = (sS, sS.bump())
            step("vector", lambda e: e.tensor_tensor(lo_ap, src_ap, hi_ap,
                                                     OP.subtract),
                 waits=[t_hi], incs=[(sV, 1)])
            return (sV, sV.bump())

        step("vector", lambda e: e.memset(ones[:, :], 1.0), incs=[(sV, 1)])
        t_ones = (sV, sV.bump())
        step("vector", lambda e: e.memset(carry[:, :], 0.0), incs=[(sV, 1)])
        t_car0 = (sV, sV.bump())
        step("vector", lambda e: e.memset(row(6), 1.0), incs=[(sV, 1)])
        t_onerow = (sV, sV.bump())

        for dst, src in ((bgs, bg2), (bvs, bv2), (bds, bd2), (b1s, b12),
                         (b2s, b22), (wmixs, wmix2), (wffns, wffn2), (selps, selp)):
            step("sync", lambda e, d=dst, s=src: e.dma_start(out=d[:, :], in_=s),
                 incs=[(sBIAS, 16)])
            sBIAS.bump(16)
        t_bias = (sBIAS, sBIAS.v)

        def mk_bq(e):
            CI("scalar", e.activation(bq16[:, :], b2s[:, :], AF.Copy))
            CW("scalar", e)
            return e.activation(bq32[:, :], bq16[:, :], AF.Copy)
        step("scalar", mk_bq, waits=[t_bias], incs=[(sS, 1)])
        t_bq = (sS, sS.bump())

        p1 = contextlib.ExitStack()
        sb1 = lambda name, shape, dt: p1.enter_context(nc.sbuf_tensor(name, shape, dt))
        xb0 = sb1("xb0", [128, KD, TC], F32)
        xb1 = sb1("xb1", [128, KD, TC], F32)
        xbufs = [xb0, xb1]
        xn_sb = sb1("xn_sb", [128, KD, TC], F32)
        xnh = sb1("xnh", [128, KD, TC], BF16)
        xnl = sb1("xnl", [128, KD, TC], BF16)
        xah = sb1("xah", [128, KD, TC], BF16)
        xal = sb1("xal", [128, KD, TC], BF16)
        xs_sb = sb1("xs_sb", [128, KD, TC], F32)
        a_sb = sb1("a_sb", [128, KD, TC], F32)
        hq_sb = sb1("hq_sb", [128, KH, TC], FP8)
        wq_sb = sb1("wq_sb", [128, KH, 128], FP8)
        wq2_sb = sb1("wq2_sb", [128, KH, 128], FP8)

        hs = {}

        def emit_load_x(c):
            h = hs.setdefault(c, {})
            sx = sX2[c % 2]
            xw = []
            if c >= 2:
                xw.append((sSTp[c % 2], sSTp[c % 2].v))
            step("sync", lambda e, a=c * TC, b=(c + 1) * TC, xb=xbufs[c % 2]:
                 e.dma_start(out=xb[:, :, :], in_=xTv[:, :, a:b]),
                 waits=xw, incs=[(sx, 16)])
            h["x"] = (sx, sx.bump(16))

        def emit_stats(c):
            h = hs[c]
            xb = xbufs[c % 2]
            pw = []
            if c >= 1:
                pw += [(sXN, sXN.v), hs[c - 1]["xnsp"], hs[c - 1]["xa"]]
            step("tensor", lambda e: [e.matmul(pb[4][0:1, 0:TC], ones[:, 0:1],
                                               xb[:, kc, :], start=(kc == 0),
                                               stop=(kc == KD - 1))
                                      for kc in range(KD)][-1],
                 waits=[h["x"], t_ones], incs=[(sT, 1)])
            t_sumx = (sT, sT.bump())
            t_sq = []
            for kc in range(KD):
                scr = xn_sb[:, kc % 2, :]
                w_ = [h["x"]] + pw if kc < 2 else [(sT, t_sq[kc - 2][1])]
                step("scalar", lambda e, kc=kc, scr=scr: e.activation(
                    scr, xb[:, kc, :], AF.Square), waits=w_, incs=[(sS, 1)])
                t_s = (sS, sS.bump())
                step("tensor", lambda e, kc=kc, scr=scr: e.matmul(
                    pb[5][0:1, 0:TC], ones[:, 0:1], scr,
                    start=(kc == 0), stop=(kc == KD - 1)),
                     waits=[t_s], incs=[(sT, 1)])
                t_sq.append((sT, sT.bump()))
            t_sumsq = t_sq[-1]

            ew = [t_sumx, t_sumsq]
            if c >= 1:
                ew += [hs[c - 1]["bc1"], hs[c - 1]["bc2"]]

            def evrows(e):
                CI("scalar", e.copy(row(0), pb[4][0:1, 0:TC]))
                return e.copy(row(1), pb[5][0:1, 0:TC])
            step("scalar", evrows, waits=ew, incs=[(sS, 1)])
            h["_ev"] = (sS, sS.bump())

        def emit_rows1(c):
            h = hs[c]

            def rows1(e):
                ops = [
                    lambda: e.tensor_scalar(row(0), row(0), INV_D, None, OP.mult),
                    lambda: e.tensor_scalar(row(1), row(1), INV_D, None, OP.mult),
                    lambda: e.tensor_tensor(row(2), row(0), row(0), OP.mult),
                    lambda: e.tensor_tensor(row(1), row(1), row(2), OP.subtract),
                    lambda: e.tensor_scalar(row(2), row(1), 1e-5, None, OP.add),
                    lambda: e.tensor_scalar(row(3), row(2), 1e-6, None, OP.mult),
                ]
                for op in ops:
                    CI("vector", op())
                    CW("vector", e)
                return e.tensor_tensor(row(3), row(1), row(3), OP.add)
            step("vector", rows1, waits=[h["_ev"], t_onerow], incs=[(sV, 1)])
            t_r1 = (sV, sV.bump())

            def sqrts(e):
                CI("scalar", e.activation(row(1), row(2), AF.Sqrt))
                return e.activation(row(4), row(3), AF.Sqrt)
            step("scalar", sqrts, waits=[t_r1], incs=[(sS, 1)])
            h["_sqr"] = (sS, sS.bump())

        def emit_rows2a(c):
            h = hs[c]

            def rows2a(e):
                ops = [
                    lambda: e.reciprocal(row(5), row(1)),
                    lambda: e.tensor_tensor(row(1), row(5), row(5), OP.mult),
                    lambda: e.tensor_tensor(row(1), row(2), row(1), OP.mult),
                    lambda: e.tensor_scalar(row(1), row(1), -0.5, 1.5, OP.mult,
                                            OP.add),
                ]
                for op in ops:
                    CI("vector", op())
                    CW("vector", e)
                return e.tensor_tensor(row(1), row(5), row(1), OP.mult)
            step("vector", rows2a, waits=[h["_sqr"]], incs=[(sV, 1)])
            h["_r2a"] = (sV, sV.bump())

        def emit_rows2b(c):
            h = hs[c]

            def rows2b(e):
                ops = [
                    lambda: e.reciprocal(row(5), row(4)),
                    lambda: e.tensor_tensor(row(4), row(5), row(5), OP.mult),
                    lambda: e.tensor_tensor(row(4), row(3), row(4), OP.mult),
                    lambda: e.tensor_scalar(row(4), row(4), -0.5, 1.5, OP.mult,
                                            OP.add),
                    lambda: e.tensor_tensor(row(4), row(5), row(4), OP.mult),
                    lambda: e.tensor_tensor(row(0), row(0), row(1), OP.mult),
                    lambda: e.tensor_tensor(row(5), row(2), row(1), OP.mult),
                ]
                for op in ops:
                    CI("vector", op())
                    CW("vector", e)
                return e.tensor_tensor(row(5), row(5), row(4), OP.mult)
            step("vector", rows2b, waits=[h["_r2a"]], incs=[(sV, 1)])
            h["rows"] = (sV, sV.bump())

        def emit_xn_a(c):
            h = hs[c]
            xb = xbufs[c % 2]
            bw = [h["rows"]]
            if c >= 1:
                bw += [hs[c - 1]["xa"]]
            step("tensor", lambda e: (e.matmul(pb[6][:, 0:TC], ones[0:1, :], row(1),
                                               start=True, stop=True),
                                      e.matmul(pb[7][:, 0:TC], ones[0:1, :], row(0),
                                               start=True, stop=True))[-1],
                 waits=bw, incs=[(sT, 1)])
            h["bc1"] = (sT, sT.bump())

            def mk_xn(e):
                for kc in range(KD):
                    inst = e.tensor_tensor(xn_sb[:, kc, :], xb[:, kc, :],
                                           pb[6][:, 0:TC], OP.mult)
                CI("vector", inst)
                CW("vector", e)
                for kc in range(KD):
                    inst = e.tensor_tensor(xn_sb[:, kc, :], xn_sb[:, kc, :],
                                           pb[7][:, 0:TC], OP.subtract)
                return inst
            step("vector", mk_xn, waits=[h["bc1"], h["x"]], incs=[(sV, 1)])
            h["xn"] = (sV, sV.bump())
            step("scalar", lambda e, a=c * TC, b=(c + 1) * TC: e.dma_start(
                out=xnv[:, :, a:b], in_=xn_sb[:, :, :]), waits=[h["xn"]],
                 incs=[(sXN, 16)])
            h["spill"] = (sXN, sXN.bump(16))

        def emit_xn_b(c):
            h = hs[c]
            step("tensor", lambda e: e.matmul(pb[6][:, 0:TC], ones[0:1, :], row(5),
                                              start=True, stop=True),
                 waits=[h["xn"]], incs=[(sT, 1)])
            h["bc2"] = (sT, sT.bump())

        def emit_xn_splits(c):
            h = hs[c]
            t_prev = (sT, sT.v)
            t_xnl = None
            for kc in range(KD):
                xw = [h["xn"]] + ([t_prev] if kc == 0 else [])
                t_xnl = act_split(xn_sb[:, kc, :], xnh[:, kc, :], xnl[:, kc, :], xw)
            h["xnsp"] = t_xnl

        def emit_xa_splits(c):
            h = hs[c]
            t_prev = (sT, sT.v)
            t_vec = (sV, sV.v)
            scrs = [sg_sb, tv_sb, sl_sb, sq2_sb]
            toks = {}
            for kc in range(KD):
                scr = scrs[kc % 4]
                aw = [h["bc2"]]
                if kc == 0:
                    aw += [t_prev, t_vec]
                if kc >= 4:
                    aw.append(toks[kc - 4])
                step("vector", lambda e, kc=kc, scr=scr: e.scalar_tensor_tensor(
                    scr[:, :], xn_sb[:, kc, :], wmixs[:, kc:kc + 1],
                    pb[6][:, 0:TC], OP.mult, OP.mult),
                     waits=aw, incs=[(sV, 1)])
                t_tmp = (sV, sV.bump())
                toks[kc] = act_split(scr[:, :], xah[:, kc, :], xal[:, kc, :], [t_tmp])
            h["xa"] = toks[KD - 1]

        gvd_cfg = ((bgs, AF.Sigmoid, sg_sb), (bvs, AF.Tanh, tv_sb),
                   (bds, AF.Sigmoid, sl_sb))

        def add_chunk1(c):
            gst = {"evac": {0: None, 1: None, 2: None}, "xs": None, "ls": None}

            def gvd_consume(wh, wl, wtoks, i, c=c, gst=gst):
                m, pi = i // 3, i % 3
                bias_t, func, dst = gvd_cfg[pi]
                mw = wtoks + [hs[c]["xa"]]
                if gst["evac"][pi] is not None:
                    mw.append(gst["evac"][pi])
                step("tensor", mm3(pb[pi][:, 0:TC], wh, wl, xah, xal, KD),
                     waits=mw, incs=[(sT, 1)])
                t_mm = (sT, sT.bump())
                if pi < 2:
                    ew = [t_mm] + ([gst["xs"]] if gst["xs"] else [])
                else:
                    ew = [t_mm] + ([gst["ls"]] if gst["ls"] else [])
                step("scalar", lambda e, m=m, pi=pi, bias_t=bias_t, func=func,
                     dst=dst: e.activation(dst[:, :], pb[pi][:, 0:TC], func,
                                           bias=bias_t[:, m:m + 1]),
                     waits=ew, incs=[(sS, 1)])
                gst["evac"][pi] = (sS, sS.bump())
                if pi == 2:
                    spw = [(sSP, sSP.v)] if (m == 0 and c >= 1) else []
                    step("vector", lambda e, m=m: e.tensor_tensor(
                        xs_sb[:, m, :], sg_sb[:, :], tv_sb[:, :], OP.mult),
                         waits=[gst["evac"][0], gst["evac"][1]] + spw,
                         incs=[(sV, 1)])
                    gst["xs"] = (sV, sV.bump())
                    step("vector", lambda e, m=m: e.tensor_scalar(
                        a_sb[:, m, :], sl_sb[:, :], 0.998, 0.001, OP.mult, OP.add),
                         waits=[gst["evac"][2]] + spw, incs=[(sV, 1)])
                    t_a = (sV, sV.bump())
                    cwait = [t_car0] if c == 0 else []

                    def lscan(e, m=m):
                        CW("vector", e)
                        CI("vector", e.tensor_tensor_scan(
                            sl_sb[:, :], a_sb[:, m, :], xs_sb[:, m, :],
                            carry[:, m:m + 1], OP.mult, OP.add))
                        CW("vector", e)
                        return e.tensor_copy(carry[:, m:m + 1], sl_sb[:, TC - 1:TC])
                    step("vector", lscan,
                         waits=[gst["xs"], t_a, gst["evac"][2]] + cwait,
                         incs=[(sV, 1)])
                    gst["ls"] = (sV, sV.bump())
                return t_mm

            for i in range(3 * KD):
                src = (wgT, wvT, wdT)[i % 3]
                m = i // 3
                SU(src[m], KD,
                   (lambda i: lambda wh, wl, wt: gvd_consume(wh, wl, wt, i))(i))
                if c + 1 < NCH:
                    if i == 0:
                        HOOK(lambda c=c: emit_load_x(c + 1))
                    elif i == 2:
                        HOOK(lambda c=c: emit_stats(c + 1))
                    elif i == 5:
                        HOOK(lambda c=c: emit_rows1(c + 1))
                    elif i == 7:
                        HOOK(lambda c=c: emit_rows2a(c + 1))
                    elif i == 9:
                        HOOK(lambda c=c: emit_rows2b(c + 1))
                    elif i == 12:
                        HOOK(lambda c=c: emit_xn_a(c + 1))
                    elif i == 16:
                        HOOK(lambda c=c: emit_xn_b(c + 1))

            def end_gvd(c=c, gst=gst):
                t_gvd_all = (sV, sV.v)
                step("scalar", lambda e, a=c * TC, b=(c + 1) * TC: e.dma_start(
                    out=xsv[:, :, a:b], in_=xs_sb[:, :, :]), waits=[t_gvd_all],
                     incs=[(sSP, 16)])
                step("scalar", lambda e, a=c * TC, b=(c + 1) * TC: e.dma_start(
                    out=av[:, :, a:b], in_=a_sb[:, :, :]), waits=[t_gvd_all],
                     incs=[(sSP, 16)])
                sSP.bump(32)
                if c == NCH - 1:
                    step("gpsimd", lambda e: e.dma_start(out=cc_in.ap(),
                                                         in_=carry[:, :]),
                         waits=[gst["ls"]], incs=[(sGD, 16)])
                    t_ccin = (sGD, sGD.bump(16))
                    step("gpsimd", lambda e: e.collective_compute(
                        "AllGather", OP.bypass,
                        replica_groups=[[0, 1], [2, 3], [4, 5], [6, 7]],
                        ins=[cc_in.ap()], outs=[cc_out.ap()]),
                         waits=[t_ccin], incs=[(sCC, 1)])
                    t_cc = (sCC, sCC.bump())
                    step("gpsimd", lambda e: e.dma_start(
                        out=hprev[:, :], in_=cc_out.ap()[0, :, :]), waits=[t_cc],
                         incs=[(sGD, 16)])
                    t_hp = (sGD, sGD.bump(16))
                    step("vector", lambda e: e.tensor_scalar(
                        carry2[:, :], hprev[:, :], selps[:, 0:1], None, OP.mult),
                         waits=[t_hp], incs=[(sV, 1)])
                    hs["c2"] = (sV, sV.bump())
            HOOK(end_gvd)

            w1st = {"h": None, "hb": {0: None, 1: None}}

            def w1_consume(wh, wl, wtoks, m, c=c, w1st=w1st):
                bk = 3 + m % 2
                mw = wtoks + [hs[c]["xnsp"]]
                if w1st["hb"][m % 2] is not None:
                    mw.append(w1st["hb"][m % 2])
                step("tensor", mm3(pb[bk][:, 0:TC], wh, wl, xnh, xnl, KD),
                     waits=mw, incs=[(sT, 1)])
                t_mm = (sT, sT.bump())
                step("scalar", lambda e, m=m, bk=bk: e.activation(
                    hq_sb[:, m, :], pb[bk][:, 0:TC], AF.Relu, bias=b1s[:, m:m + 1]),
                     waits=[t_mm], incs=[(sS, 1)])
                w1st["hb"][m % 2] = (sS, sS.bump())
                w1st["h"] = w1st["hb"][m % 2]
                return t_mm

            for m in range(KH):
                SU(w1T[m], KD,
                   (lambda m: lambda wh, wl, wt: w1_consume(wh, wl, wt, m))(m))
                if m == 2 and c + 1 < NCH:
                    HOOK(lambda c=c: emit_xa_splits(c + 1))
            if c + 1 < NCH:
                HOOK(lambda c=c: emit_xn_splits(c + 1))

            w2st = {"casts": [], "f": None, "part": None, "mm": None}

            def w2_tail(mo, t_mm, bk, c=c, w2st=w2st):
                fw = [t_mm, t_bq]
                if w2st["part"] is not None:
                    fw.append(w2st["part"])
                step("scalar", lambda e, mo=mo, bk=bk: e.activation(
                    f16_sb[:, :], pb[bk][:, 0:TC], AF.Identity,
                    bias=bq32[:, mo:mo + 1]), waits=fw, incs=[(sS, 1)])
                w2st["f"] = (sS, sS.bump())
                w2st.setdefault("fb", {})[mo % 2] = w2st["f"]

                def mkpart(e, mo=mo, xb=xbufs[c % 2]):
                    CI("vector", e.tensor_copy(tmpf[:, :], f16_sb[:, :]))
                    CW("vector", e)
                    return e.tensor_tensor(xb[:, mo, :], xb[:, mo, :],
                                           tmpf[:, :], OP.add)
                step("vector", mkpart, waits=[w2st["f"]], incs=[(sV, 1)])
                w2st["part"] = (sV, sV.bump())

            def w2_consume(stga, t_ld, q, mo, c=c, w2st=w2st, w1st=w1st):
                vwt = [t_ld]
                if w2st["mm"] is not None:
                    vwt.append(w2st["mm"])
                if q == 0:
                    vwt.append((sWQ, sWQ.v))
                step("vector", lambda e, stga=stga, q=q: e.tensor_copy(
                    wq_sb[:, q * 8:(q + 1) * 8, :], stga[:, 0:8, :]),
                     waits=vwt, incs=[(sV, 1)])
                t_cast = (sV, sV.bump())
                w2st["casts"].append(t_cast)
                if q < 3:
                    return t_cast
                casts = list(w2st["casts"])
                w2st["casts"] = []
                step("scalar", lambda e, mo=mo: e.dma_start(
                    out=w2q8[mo], in_=wq_sb[:, :, :]), waits=casts,
                     incs=[(sWQ, 16)])
                sWQ.bump(16)
                mw = casts + [w1st["h"]]
                fprev = w2st.get("fb", {}).get(mo % 2)
                if fprev is not None:
                    mw.append(fprev)
                elif mo % 2 == 0 and w1st["hb"][0] is not None:
                    mw.append(w1st["hb"][0])
                elif mo % 2 == 1 and w1st["hb"][1] is not None:
                    mw.append(w1st["hb"][1])

                bk = 3 + mo % 2

                def mmw2(e, bk=bk):
                    for j in range(KH // 2):
                        inst = e.matmul(pb[bk][:, 0:TC],
                                        wq_sb[:, 2 * j:2 * j + 2, :],
                                        hq_sb[:, 2 * j:2 * j + 2, :],
                                        start=(j == 0), stop=(j == KH // 2 - 1),
                                        perf_mode=DR)
                    return inst
                step("tensor", mmw2, waits=mw, incs=[(sT, 1)])
                w2st["mm"] = (sT, sT.bump())
                w2_tail(mo, w2st["mm"], bk)
                return w2st["mm"]

            if c == 0:
                for mo in range(KD):
                    for q in range(4):
                        RU(w2T[mo][:, q * 8:(q + 1) * 8, :], 8,
                           (lambda q, mo: lambda stga, t_ld:
                            w2_consume(stga, t_ld, q, mo))(q, mo))

            def w2_cached(c=c, w2st=w2st, w1st=w1st):
                mms = {}
                for mo in range(KD):
                    wq = wq_sb if mo % 2 == 0 else wq2_sb
                    sq = sQ2[mo % 2]
                    lw = [(sWQ, sWQ.v)]
                    if mo >= 2:
                        lw.append(mms[mo - 2])
                    else:
                        lw.append(hs[c - 1]["w2mm"][mo - 2 + KD])
                    step("sync", lambda e, mo=mo, wq=wq, sq=sq: e.dma_start(
                        out=wq[:, :, :], in_=w2q8[mo]).then_inc(sq.h, 16),
                         waits=lw)
                    t_l = (sq, sq.bump(16))
                    bk = 3 + mo % 2
                    mw = [t_l, w1st["h"]]
                    fprev = w2st.get("fb", {}).get(mo % 2)
                    if fprev is not None:
                        mw.append(fprev)
                    elif w1st["hb"][mo % 2] is not None:
                        mw.append(w1st["hb"][mo % 2])

                    def mmw2(e, wq=wq, bk=bk):
                        for j in range(KH // 2):
                            inst = e.matmul(pb[bk][:, 0:TC],
                                            wq[:, 2 * j:2 * j + 2, :],
                                            hq_sb[:, 2 * j:2 * j + 2, :],
                                            start=(j == 0),
                                            stop=(j == KH // 2 - 1),
                                            perf_mode=DR)
                        return inst
                    step("tensor", mmw2, waits=mw, incs=[(sT, 1)])
                    w2st["mm"] = (sT, sT.bump())
                    mms[mo] = w2st["mm"]
                    w2_tail(mo, w2st["mm"], bk)
                hs[c]["w2mm"] = mms

            def end_w2(c=c, w2st=w2st):
                if c == 0:
                    hs[0]["w2mm"] = {6: w2st["mm"], 7: w2st["mm"]}
                if c >= 1:
                    w2_cached()
                sst = sSTp[c % 2]
                step("scalar", lambda e, a=c * TC, b=(c + 1) * TC,
                     xb=xbufs[c % 2], sst=sst:
                     e.dma_start(out=outv[:, :, a:b], in_=xb[:, :, :]),
                     waits=[w2st["part"]], incs=[(sst, 16)])
                hs[c]["pstore"] = (sst, sst.bump(16))
            HOOK(end_w2)

        emit_load_x(0)
        emit_stats(0)
        emit_rows1(0)
        emit_rows2a(0)
        emit_rows2b(0)
        emit_xn_a(0)
        emit_xn_b(0)
        emit_xa_splits(0)
        emit_xn_splits(0)
        for c in range(NCH):
            add_chunk1(c)
        run_stream()

        emit_block()
        p1.close()

        p3 = contextlib.ExitStack()
        sb3 = lambda name, shape, dt: p3.enter_context(nc.sbuf_tensor(name, shape, dt))
        q_xn = sb3("q_xn", [128, KD, TC], F32)
        q_xs = sb3("q_xs", [128, KD, TC], F32)
        q_a = sb3("q_a", [128, KD, TC], F32)
        q_pa = sb3("q_pa", [128, KD, TC], F32)
        q_s = sb3("q_s", [128, KD, TC], F32)
        xbh = sb3("xbh", [128, KD, TC], BF16)
        xbl = sb3("xbl", [128, KD, TC], BF16)
        pmh = sb3("pmh", [128, KF, TC], BF16)
        pml = sb3("pml", [128, KF, TC], BF16)

        h3 = {}

        def p3_boundary():
            return [(sXN, sXN.v), (sSP, sSP.v), (sT, sT.v), (sS, sS.v),
                    (sV, sV.v), (sSTp[0], sSTp[0].v), (sSTp[1], sSTp[1].v)]

        def p3_load_xsa(c):
            h = h3.setdefault(c, {})
            lw = p3_boundary() if c == 0 else [h3[c - 1]["scan"]]
            step("sync", lambda e, a=c * TC, b=(c + 1) * TC: e.dma_start(
                out=q_xs[:, :, :], in_=xsv[:, :, a:b]), waits=lw, incs=[(sP3A, 16)])
            sP3A.bump(16)
            step("sync", lambda e, a=c * TC, b=(c + 1) * TC: e.dma_start(
                out=q_a[:, :, :], in_=av[:, :, a:b]), incs=[(sP3A, 16)])
            sP3A.bump(16)
            h["lxa"] = (sP3A, sP3A.v)

        def p3_scans(c):
            h = h3[c]
            sw = [h["lxa"]]
            if c == 0:
                sw.append(hs["c2"])
            else:
                sw.append(h3[c - 1]["part2"])

            def scans(e):
                for m in range(KD):
                    CW("vector", e)
                    CI("vector", e.tensor_tensor_scan(
                        q_s[:, m, :], q_a[:, m, :], q_xs[:, m, :],
                        carry2[:, m:m + 1], OP.mult, OP.add))
                    CW("vector", e)
                    inst = e.tensor_copy(carry2[:, m:m + 1], q_s[:, m, TC - 1:TC])
                    if m < KD - 1:
                        CI("vector", inst)
                return inst
            step("vector", scans, waits=sw, incs=[(sV, 1)])
            h["scan"] = (sV, sV.bump())

        def p3_load_xn(c):
            h = h3[c]
            lw = p3_boundary() if c == 0 else [h3[c - 1]["part1"]]
            step("sync", lambda e, a=c * TC, b=(c + 1) * TC: e.dma_start(
                out=q_xn[:, :, :], in_=xnv[:, :, a:b]), waits=lw, incs=[(sP3X, 16)])
            h["lxn"] = (sP3X, sP3X.bump(16))

        def p3_part1(c):
            h = h3[c]
            step("vector", lambda e: e.tensor_tensor(
                q_s[:, :, :], q_xn[:, :, :], q_s[:, :, :], OP.add),
                 waits=[h["lxn"], h["scan"]], incs=[(sV, 1)])
            h["part1"] = (sV, sV.bump())

        def p3_load_pa(c):
            h = h3.setdefault(c, {})
            lw = [(sSTp[0], sSTp[0].v), (sSTp[1], sSTp[1].v)]
            step("sync", lambda e, a=c * TC, b=(c + 1) * TC: e.dma_start(
                out=q_pa[:, :, :], in_=outv[:, :, a:b]), waits=lw, incs=[(sP3P, 16)])
            h["lpa"] = (sP3P, sP3P.bump(16))

        def p3_part2(c):
            h = h3[c]
            step("vector", lambda e: e.tensor_tensor(
                q_pa[:, :, :], q_pa[:, :, :], q_s[:, :, :], OP.add),
                 waits=[h["lpa"], h["part1"]], incs=[(sV, 1)])
            h["part2"] = (sV, sV.bump())

        def p3_stats(c):
            h = h3[c]
            t_sq3 = []
            fw1 = [h3[c - 1]["fin1"]] if c >= 1 else []
            for kc in range(KD):
                scr = sl_sb if kc % 2 == 0 else sq2_sb
                w_ = [h["part1"]] + fw1 if kc < 2 else [(sT, t_sq3[kc - 2][1])]
                step("scalar", lambda e, kc=kc, scr=scr: e.activation(
                    scr[:, :], q_s[:, kc, :], AF.Square), waits=w_, incs=[(sS, 1)])
                t_s3 = (sS, sS.bump())
                step("tensor", lambda e, kc=kc, scr=scr: e.matmul(
                    pb[2][0:1, 0:TC], ones[:, 0:1], scr[:, :],
                    start=(kc == 0), stop=(kc == KD - 1)), waits=[t_s3],
                     incs=[(sT, 1)])
                t_sq3.append((sT, sT.bump()))
            step("scalar", lambda e: e.copy(row(0), pb[2][0:1, 0:TC]),
                 waits=[t_sq3[-1]], incs=[(sS, 1)])
            h["_ev3"] = (sS, sS.bump())

        def p3_rows3a(c):
            h = h3[c]
            step("vector", lambda e: e.tensor_scalar(
                row(1), row(0), INV_D, 1e-6, OP.mult, OP.add),
                 waits=[h["_ev3"]], incs=[(sV, 1)])
            t_u3 = (sV, sV.bump())
            step("scalar", lambda e: e.activation(row(2), row(1), AF.Sqrt),
                 waits=[t_u3], incs=[(sS, 1)])
            h["_y3"] = (sS, sS.bump())

        def p3_rows3b(c):
            h = h3[c]

            def rows3(e):
                ops = [
                    lambda: e.reciprocal(row(3), row(2)),
                    lambda: e.tensor_tensor(row(2), row(3), row(3), OP.mult),
                    lambda: e.tensor_tensor(row(2), row(1), row(2), OP.mult),
                    lambda: e.tensor_scalar(row(2), row(2), -0.5, 1.5, OP.mult,
                                            OP.add),
                ]
                for op in ops:
                    CI("vector", op())
                    CW("vector", e)
                return e.tensor_tensor(row(4), row(3), row(2), OP.mult)
            step("vector", rows3, waits=[h["_y3"]], incs=[(sV, 1)])
            t_r3 = (sV, sV.bump())
            step("tensor", lambda e: e.matmul(pb[3][:, 0:TC], ones[0:1, :],
                                              row(4), start=True, stop=True),
                 waits=[t_r3], incs=[(sT, 1)])
            h["bc3"] = (sT, sT.bump())

        def p3_xbsplit(c):
            h = h3[c]
            scrs = [sg_sb, tv_sb, sl_sb, sq2_sb]
            toks = {}
            for kc in range(KD):
                scr = scrs[kc % 4]
                aw = [h["bc3"]]
                if kc == 0:
                    aw += [(sS, sS.v), (sV, sV.v)]
                if kc >= 4:
                    aw.append(toks[kc - 4])
                step("vector", lambda e, kc=kc, scr=scr: e.scalar_tensor_tensor(
                    scr[:, :], q_s[:, kc, :], wffns[:, kc:kc + 1],
                    pb[3][:, 0:TC], OP.mult, OP.mult),
                     waits=aw, incs=[(sV, 1)])
                t_tmp = (sV, sV.bump())
                toks[kc] = act_split(scr[:, :], xbh[:, kc, :], xbl[:, kc, :], [t_tmp])
            h["xb"] = toks[KD - 1]

        def add_chunk3(c):
            g3 = {"sil": None, "pm": None, "wo": None, "pml": None, "sw2": None}

            def g3_consume(wh, wl, wtoks, i, c=c, g3=g3):
                m, r = i // 3, i % 3
                if r == 0:
                    mw = wtoks + [h3[c]["xb"]]
                    if g3["sil"] is not None:
                        mw.append(g3["sil"])
                    if m == 0 and c >= 1:
                        mw.append(h3[c - 1]["fin2"])
                    step("tensor", mm3(pb[0][:, 0:TC], wh, wl, xbh, xbl, KD),
                         waits=mw, incs=[(sT, 1)])
                    t_g = (sT, sT.bump())
                    sw2 = [t_g] + ([g3["pm"]] if g3["pm"] else [])
                    step("scalar", lambda e: e.activation(
                        sg_sb[:, :], pb[0][:, 0:TC], AF.Silu),
                         waits=sw2, incs=[(sS, 1)])
                    g3["sil"] = (sS, sS.bump())
                    return t_g
                if r == 1:
                    mw2 = wtoks + [h3[c]["xb"]]
                    if g3["pm"] is not None:
                        mw2.append(g3["pm"])
                    if m == 0 and c >= 1:
                        mw2.append(h3[c - 1]["fin2"])
                    step("tensor", mm3(pb[1][:, 0:TC], wh, wl, xbh, xbl, KD),
                         waits=mw2, incs=[(sT, 1)])
                    t_u = (sT, sT.bump())
                    step("vector", lambda e: e.tensor_tensor(
                        tmpf[:, :], sg_sb[:, :], pb[1][:, 0:TC], OP.mult),
                         waits=[g3["sil"], t_u] + ([g3["pml"]] if g3["pml"] else []),
                         incs=[(sV, 1)])
                    g3["pm"] = (sV, sV.bump())
                    g3["pml"] = act_split(tmpf[:, :], pmh[:, m, :], pml[:, m, :],
                                          [g3["pm"]])
                    g3.setdefault("pml_m", {})[m] = g3["pml"]
                    return t_u

                def mmwo(e, wh=wh, wl=wl, m=m):
                    for mo in range(6):
                        e.matmul(pb[2 + mo][:, 0:TC], wh[:, mo, :], pmh[:, m, :],
                                 start=(m == 0), stop=False)
                        e.matmul(pb[2 + mo][:, 0:TC], wh[:, mo, :], pml[:, m, :],
                                 start=False, stop=False)
                        inst = e.matmul(pb[2 + mo][:, 0:TC], wl[:, mo, :],
                                        pmh[:, m, :], start=False,
                                        stop=(m == KF - 1))
                    return inst
                wo_w = wtoks + [g3["pml_m"][m]]
                if m == 0 and c >= 1:
                    wo_w.append(h3[c - 1]["fin1"])
                step("tensor", mmwo, waits=wo_w, incs=[(sT, 1)])
                g3["wo"] = (sT, sT.bump())
                return g3["wo"]

            descs = []
            for m in range(KF):
                descs.append((0, m))
                descs.append((1, m))
                if m >= 1:
                    descs.append((2, m - 1))
            descs.append((2, KF - 1))
            for i, (r, m) in enumerate(descs):
                if r == 0:
                    src, ns = wgateT[m], KD
                elif r == 1:
                    src, ns = wupT[m], KD
                else:
                    src = woutv[m, :, 0:768].rearrange("p (s q) -> p s q", q=128)
                    ns = 6
                SU(src, ns,
                   (lambda r, m: lambda wh, wl, wt:
                    g3_consume(wh, wl, wt, m * 3 + r))(r, m))
                if c + 1 < NCH:
                    if i == 2:
                        HOOK(lambda c=c: p3_load_xsa(c + 1))
                    elif i == 6:
                        HOOK(lambda c=c: p3_scans(c + 1))
                    elif i == 40:
                        HOOK(lambda c=c: p3_load_xn(c + 1))
                    elif i == 44:
                        HOOK(lambda c=c: p3_part1(c + 1))

            def fin1(c=c, g3=g3):
                def finev1(e):
                    for mo in range(6):
                        inst = e.tensor_tensor(q_pa[:, mo, :], q_pa[:, mo, :],
                                               pb[2 + mo][:, 0:TC], OP.add)
                    return inst
                step("vector", finev1, waits=[g3["wo"], h3[c]["part2"]],
                     incs=[(sV, 1)])
                h3[c]["fin1"] = (sV, sV.bump())
                if c + 1 < NCH:
                    p3_stats(c + 1)
            HOOK(fin1)

            def sw2_consume(wh, wl, wtoks, m, c=c, g3=g3):
                mw3 = list(wtoks)
                if m == 0:
                    mw3 += [g3["pm"], g3["sil"], g3["pml"], h3[c]["fin1"]]

                def mm2(e, wh=wh, wl=wl, m=m):
                    for mo in range(2):
                        e.matmul(pb[mo][:, 0:TC], wh[:, mo, :], pmh[:, m, :],
                                 start=(m == 0), stop=False)
                        e.matmul(pb[mo][:, 0:TC], wh[:, mo, :], pml[:, m, :],
                                 start=False, stop=False)
                        inst = e.matmul(pb[mo][:, 0:TC], wl[:, mo, :],
                                        pmh[:, m, :], start=False,
                                        stop=(m == KF - 1))
                    return inst
                step("tensor", mm2, waits=mw3, incs=[(sT, 1)])
                g3["sw2"] = (sT, sT.bump())
                return g3["sw2"]

            for m in range(KF):
                SU(woutv[m, :, 768:1024].rearrange("p (s q) -> p s q", q=128), 2,
                   (lambda m: lambda wh, wl, wt: sw2_consume(wh, wl, wt, m))(m))
                if c + 1 < NCH:
                    if m == 1:
                        HOOK(lambda c=c: p3_rows3a(c + 1))
                    elif m == 4:
                        HOOK(lambda c=c: p3_rows3b(c + 1))
                    elif m == 8:
                        HOOK(lambda c=c: p3_xbsplit(c + 1))

            def fin2(c=c, g3=g3):
                def finev2(e):
                    CI("vector", e.tensor_tensor(q_pa[:, 6, :], q_pa[:, 6, :],
                                                 pb[0][:, 0:TC], OP.add))
                    return e.tensor_tensor(q_pa[:, 7, :], q_pa[:, 7, :],
                                           pb[1][:, 0:TC], OP.add)
                step("vector", finev2, waits=[g3["sw2"]], incs=[(sV, 1)])
                h3[c]["fin2"] = (sV, sV.bump())
                sst = sSTp[c % 2]
                step("scalar", lambda e, a=c * TC, b=(c + 1) * TC, sst=sst:
                     e.dma_start(out=outv[:, :, a:b], in_=q_pa[:, :, :]),
                     waits=[h3[c]["fin1"], h3[c]["fin2"]], incs=[(sst, 16)])
                sst.bump(16)
                if c + 1 < NCH:
                    p3_load_pa(c + 1)
                    p3_part2(c + 1)
            HOOK(fin2)

        p3_load_xsa(0)
        p3_scans(0)
        p3_load_xn(0)
        p3_part1(0)
        p3_load_pa(0)
        p3_part2(0)
        p3_stats(0)
        p3_rows3a(0)
        p3_rows3b(0)
        p3_xbsplit(0)
        for c in range(NCH):
            add_chunk3(c)
        run_stream()

        step("gpsimd", None, waits=[(sSTp[0], sSTp[0].v), (sSTp[1], sSTp[1].v)])
        step("sync", None, waits=[(sSTp[0], sSTp[0].v), (sSTp[1], sSTp[1].v)])

        emit_block()
        p3.close()

    return nc


def kernel(L0_attn_x, L0_attn_rms_mix_w, L0_attn_rms_ffn_w, L0_attn_Wg, L0_attn_bg,
           L0_attn_Wv, L0_attn_bv, L0_attn_Wd, L0_attn_bd, L0_attn_W_gate,
           L0_attn_W_up, L0_attn_W_out, L0_ffn_w1, L0_ffn_b1, L0_ffn_w2, L0_ffn_b2):
    x = np.asarray(L0_attn_x, dtype=np.float32)

    def t(a):
        return np.ascontiguousarray(np.asarray(a, dtype=np.float32).T)

    def tw(a):
        w = np.asarray(a, dtype=np.float32)
        M, K = w.shape
        return np.ascontiguousarray(
            w.reshape(M // 128, 128, K // 128, 128).transpose(0, 3, 2, 1))

    def b2(a, k):
        return np.ascontiguousarray(np.asarray(a, dtype=np.float32).reshape(k, 128).T)

    shared = {
        "wgT": tw(L0_attn_Wg), "wvT": tw(L0_attn_Wv), "wdT": tw(L0_attn_Wd),
        "wgateT": tw(L0_attn_W_gate), "wupT": tw(L0_attn_W_up),
        "woutT": t(L0_attn_W_out),
        "w1T": tw(L0_ffn_w1), "w2T": tw(L0_ffn_w2),
        "bg2": b2(L0_attn_bg, KD), "bv2": b2(L0_attn_bv, KD), "bd2": b2(L0_attn_bd, KD),
        "b12": b2(L0_ffn_b1, KH), "b22": b2(L0_ffn_b2, KD),
        "wmix2": b2(L0_attn_rms_mix_w, KD), "wffn2": b2(L0_attn_rms_ffn_w, KD),
    }
    in_maps = []
    for core in range(8):
        b, h = core // 2, core % 2
        xc = np.ascontiguousarray(x[b, h * T:(h + 1) * T, :].T)
        d = dict(shared)
        d["xT"] = xc
        d["selp"] = np.full((128, 1), float(h), dtype=np.float32)
        in_maps.append(d)

    nc = build_nc()
    res = run_bass_kernel_spmd(nc, in_maps, core_ids=list(range(8)))
    out = np.empty((B, S, D), dtype=np.float32)
    for core in range(8):
        b, h = core // 2, core % 2
        out[b, h * T:(h + 1) * T, :] = res.results[core]["outT"].T
    return out


# version 21
# speedup vs baseline: 1.2539x; 1.0146x over previous
import contextlib

import numpy as np

import concourse.bass as bass
import concourse.mybir as mybir
from concourse.bass_utils import run_bass_kernel_spmd

F32 = mybir.dt.float32
DR = mybir.MatmulPerfMode.DoubleRow
BF16 = mybir.dt.bfloat16
FP8 = mybir.dt.float8e4
F16 = mybir.dt.float16
AF = mybir.ActivationFunctionType
OP = mybir.AluOpType

B, S, D, F, H = 4, 4096, 1024, 3072, 4096
T = 2048
KD, KF, KH = D // 128, F // 128, H // 128
TC, NCH = 512, 4
INV_D = 1.0 / D
PIPE_SLOTS = 8
NPIPE = 4
DEPTH = 3


class Sem:
    def __init__(self, nc, st, name):
        self.h = st.enter_context(nc.semaphore(name))
        self.v = 0

    def bump(self, n=1):
        self.v += n
        return self.v


def build_nc():
    nc = bass.Bass("TRN2", target_bir_lowering=False, debug=False)

    xT = nc.dram_tensor("xT", [D, T], F32, kind="ExternalInput").ap()
    wgT = nc.dram_tensor("wgT", [KD, 128, KD, 128], F32, kind="ExternalInput").ap()
    wvT = nc.dram_tensor("wvT", [KD, 128, KD, 128], F32, kind="ExternalInput").ap()
    wdT = nc.dram_tensor("wdT", [KD, 128, KD, 128], F32, kind="ExternalInput").ap()
    wgateT = nc.dram_tensor("wgateT", [KF, 128, KD, 128], F32, kind="ExternalInput").ap()
    wupT = nc.dram_tensor("wupT", [KF, 128, KD, 128], F32, kind="ExternalInput").ap()
    woutT = nc.dram_tensor("woutT", [F, D], F32, kind="ExternalInput").ap()
    w1T = nc.dram_tensor("w1T", [KH, 128, KD, 128], F32, kind="ExternalInput").ap()
    w2T = nc.dram_tensor("w2T", [KD, 128, KH, 128], F32, kind="ExternalInput").ap()
    w2q8 = nc.dram_tensor("w2q8", [KD, 128, KH, 128], FP8).ap()
    bg2 = nc.dram_tensor("bg2", [128, KD], F32, kind="ExternalInput").ap()
    bv2 = nc.dram_tensor("bv2", [128, KD], F32, kind="ExternalInput").ap()
    bd2 = nc.dram_tensor("bd2", [128, KD], F32, kind="ExternalInput").ap()
    b12 = nc.dram_tensor("b12", [128, KH], F32, kind="ExternalInput").ap()
    b22 = nc.dram_tensor("b22", [128, KD], F32, kind="ExternalInput").ap()
    wmix2 = nc.dram_tensor("wmix2", [128, KD], F32, kind="ExternalInput").ap()
    wffn2 = nc.dram_tensor("wffn2", [128, KD], F32, kind="ExternalInput").ap()
    selp = nc.dram_tensor("selp", [128, 1], F32, kind="ExternalInput").ap()

    outT = nc.dram_tensor("outT", [D, T], F32, kind="ExternalOutput").ap()

    xn_h = nc.dram_tensor("xn_h", [D, T], F32).ap()
    xs_h = nc.dram_tensor("xs_h", [D, T], F32).ap()
    a_h = nc.dram_tensor("a_h", [D, T], F32).ap()
    cc_in = nc.dram_tensor("cc_in", [128, KD], F32)
    cc_out = nc.dram_tensor("cc_out", [2, 128, KD], F32)

    def fm(ap_):
        return ap_.rearrange("(k p) t -> p k t", p=128)

    xTv, xnv, xsv, av, outv = fm(xT), fm(xn_h), fm(xs_h), fm(a_h), fm(outT)
    woutv = woutT.rearrange("(k p) m -> k p m", p=128)

    P = []

    def step(eng, body=None, waits=(), incs=()):
        waits = [(s.h, v) for (s, v) in waits]
        incs_ = [(s.h, n) for (s, n) in incs]

        def fn(e):
            for h, v in waits:
                e.wait_ge(h, v)
            inst = body(e) if body is not None else None
            if incs_:
                if inst is None:
                    assert body is None
                    inst = e.nop()
                for h, n in incs_:
                    inst.then_inc(h, n)

        P.append((eng, fn))

    def emit_block():
        with nc.Block() as blk:
            for eng_name in ("sync", "scalar", "vector", "tensor", "gpsimd"):
                def make(eng_name):
                    def emit(e):
                        for en, fn in P:
                            if en == eng_name:
                                fn(e)
                    return emit
                getattr(blk, eng_name)(make(eng_name))
        P.clear()

    with contextlib.ExitStack() as st:
        sb = lambda name, shape, dt: st.enter_context(nc.sbuf_tensor(name, shape, dt))

        ones = sb("ones", [128, 128], F32)
        bgs = sb("bgs", [128, KD], F32)
        bvs = sb("bvs", [128, KD], F32)
        bds = sb("bds", [128, KD], F32)
        b1s = sb("b1s", [128, KH], F32)
        b2s = sb("b2s", [128, KD], F32)
        bq16 = sb("bq16", [128, KD], F16)
        bq32 = sb("bq32", [128, KD], F32)
        wmixs = sb("wmixs", [128, KD], F32)
        wffns = sb("wffns", [128, KD], F32)
        selps = sb("selps", [128, 1], F32)
        carry = sb("carry", [128, KD], F32)
        carry2 = sb("carry2", [128, KD], F32)
        hprev = sb("hprev", [128, KD], F32)
        sg_sb = sb("sg_sb", [128, TC], F32)
        tv_sb = sb("tv_sb", [128, TC], F32)
        sl_sb = sb("sl_sb", [128, TC], F32)
        sq2_sb = sb("sq2_sb", [128, TC], F32)
        tmpf = sb("tmpf", [128, TC], F32)
        f16_sb = sb("f16_sb", [128, TC], F16)
        stat = sb("stat", [1, 7 * TC], F32)
        stg = [sb(f"stg{i}", [128, PIPE_SLOTS, 128], F32) for i in range(NPIPE)]
        wsh = [sb(f"wsh{i}", [128, PIPE_SLOTS, 128], BF16) for i in range(NPIPE)]
        wsl = [sb(f"wsl{i}", [128, PIPE_SLOTS, 128], BF16) for i in range(NPIPE)]

        pb = [st.enter_context(nc.psum_tensor(f"pb{i}", [128, 512], F32))
              for i in range(8)]

        sBIAS = Sem(nc, st, "sBIAS")
        sX2 = [Sem(nc, st, "sXa"), Sem(nc, st, "sXb")]
        sWp = [Sem(nc, st, f"sW{i}") for i in range(NPIPE)]
        sP3A = Sem(nc, st, "sP3A")
        sP3X = Sem(nc, st, "sP3X")
        sP3P = Sem(nc, st, "sP3P")
        sSTp = [Sem(nc, st, "sSTa"), Sem(nc, st, "sSTb")]
        sXN = Sem(nc, st, "sXN")
        sSP = Sem(nc, st, "sSP")
        sGD = Sem(nc, st, "sGD")
        sWQ = Sem(nc, st, "sWQ")
        sQ2 = [Sem(nc, st, "sQ2a"), Sem(nc, st, "sQ2b")]
        sT = Sem(nc, st, "sT")
        sV = Sem(nc, st, "sV")
        sS = Sem(nc, st, "sS")
        sG = Sem(nc, st, "sG")
        sCC = Sem(nc, st, "sCC")
        chain = {n: Sem(nc, st, f"ch_{n[:3]}") for n in ("vector", "scalar", "gpsimd")}

        def CI(en, inst):
            s = chain[en]
            s.v += 1
            inst.then_inc(s.h, 1)
            return inst

        def CW(en, e):
            s = chain[en]
            e.wait_ge(s.h, s.v)

        def row(i, n=TC):
            return stat[0:1, i * TC: i * TC + n]

        pipe_state = {i: {"free": []} for i in range(NPIPE)}
        pipe_par = [0]

        def pipe_load(src_ap, nslots):
            par = pipe_par[0]
            pipe_par[0] = (par + 1) % NPIPE
            ps = pipe_state[par]
            sw = sWp[par]
            stga, wha, wla = stg[par], wsh[par], wsl[par]
            step("sync", lambda e: e.dma_start(
                out=stga[:, 0:nslots, :], in_=src_ap).then_inc(sw.h, 16),
                 waits=list(ps["free"]))
            t_ld = (sw, sw.bump(16))
            step("scalar", lambda e: e.activation(
                wha[:, 0:nslots, :], stga[:, 0:nslots, :], AF.Copy),
                 waits=[t_ld], incs=[(sS, 1)])
            t_hi = (sS, sS.bump())
            step("vector", lambda e: e.tensor_tensor(
                wla[:, 0:nslots, :], stga[:, 0:nslots, :], wha[:, 0:nslots, :],
                OP.subtract), waits=[t_hi], incs=[(sV, 1)])
            t_lo = (sV, sV.bump())
            ps["free"] = [t_lo]
            return ("split", wha, wla, [t_hi, t_lo], par)

        def pipe_load_raw(src_ap, nslots):
            par = pipe_par[0]
            pipe_par[0] = (par + 1) % NPIPE
            ps = pipe_state[par]
            sw = sWp[par]
            stga = stg[par]
            step("sync", lambda e: e.dma_start(
                out=stga[:, 0:nslots, :], in_=src_ap).then_inc(sw.h, 16),
                 waits=list(ps["free"]))
            t_ld = (sw, sw.bump(16))
            ps["free"] = []
            return ("raw", stga, t_ld, None, par)

        stream = []
        hooks_after = {}

        def SU(src, nslots, consume):
            stream.append((False, src, nslots, consume))
            return len(stream) - 1

        def RU(src, nslots, consume):
            stream.append((True, src, nslots, consume))
            return len(stream) - 1

        def HOOK(fn):
            hooks_after.setdefault(len(stream) - 1, []).append(fn)

        def run_stream():
            n = len(stream)
            handles = {}

            def load(j):
                is_raw, src, ns, _ = stream[j]
                return pipe_load_raw(src, ns) if is_raw else pipe_load(src, ns)

            for j in range(min(DEPTH, n)):
                handles[j] = load(j)
            for i in range(n):
                kind, a1, a2, a3, par = handles.pop(i)
                consume = stream[i][3]
                if kind == "split":
                    tok = consume(a1, a2, a3)
                else:
                    tok = consume(a1, a2)
                pipe_state[par]["free"].append(tok)
                for fn in hooks_after.get(i, []):
                    fn()
                j = i + DEPTH
                if j < n:
                    handles[j] = load(j)
            stream.clear()
            hooks_after.clear()

        def mm3(ps_ap, wh, wl, xh, xl, nk):
            def body(e):
                for kc in range(nk):
                    e.matmul(ps_ap, wh[:, kc, :], xh[:, kc, :],
                             start=(kc == 0), stop=False)
                for kc in range(nk):
                    e.matmul(ps_ap, wh[:, kc, :], xl[:, kc, :],
                             start=False, stop=False)
                for kc in range(nk):
                    inst = e.matmul(ps_ap, wl[:, kc, :], xh[:, kc, :],
                                    start=False, stop=(kc == nk - 1))
                return inst
            return body

        def act_split(src_ap, hi_ap, lo_ap, waits):
            step("scalar", lambda e: e.activation(hi_ap, src_ap, AF.Copy),
                 waits=waits, incs=[(sS, 1)])
            t_hi = (sS, sS.bump())
            step("vector", lambda e: e.tensor_tensor(lo_ap, src_ap, hi_ap,
                                                     OP.subtract),
                 waits=[t_hi], incs=[(sV, 1)])
            return (sV, sV.bump())

        step("vector", lambda e: e.memset(ones[:, :], 1.0), incs=[(sV, 1)])
        t_ones = (sV, sV.bump())
        step("vector", lambda e: e.memset(carry[:, :], 0.0), incs=[(sV, 1)])
        t_car0 = (sV, sV.bump())
        step("vector", lambda e: e.memset(row(6), 1.0), incs=[(sV, 1)])
        t_onerow = (sV, sV.bump())

        for dst, src in ((bgs, bg2), (bvs, bv2), (bds, bd2), (b1s, b12),
                         (b2s, b22), (wmixs, wmix2), (wffns, wffn2), (selps, selp)):
            step("sync", lambda e, d=dst, s=src: e.dma_start(out=d[:, :], in_=s),
                 incs=[(sBIAS, 16)])
            sBIAS.bump(16)
        t_bias = (sBIAS, sBIAS.v)

        def mk_bq(e):
            CI("scalar", e.activation(bq16[:, :], b2s[:, :], AF.Copy))
            CW("scalar", e)
            return e.activation(bq32[:, :], bq16[:, :], AF.Copy)
        step("scalar", mk_bq, waits=[t_bias], incs=[(sS, 1)])
        t_bq = (sS, sS.bump())

        p1 = contextlib.ExitStack()
        sb1 = lambda name, shape, dt: p1.enter_context(nc.sbuf_tensor(name, shape, dt))
        xb0 = sb1("xb0", [128, KD, TC], F32)
        xb1 = sb1("xb1", [128, KD, TC], F32)
        xbufs = [xb0, xb1]
        xn_sb = sb1("xn_sb", [128, KD, TC], F32)
        xnh = sb1("xnh", [128, KD, TC], BF16)
        xnl = sb1("xnl", [128, KD, TC], BF16)
        xah = sb1("xah", [128, KD, TC], BF16)
        xal = sb1("xal", [128, KD, TC], BF16)
        xs_sb = sb1("xs_sb", [128, KD, TC], F32)
        a_sb = sb1("a_sb", [128, KD, TC], F32)
        hq_sb = sb1("hq_sb", [128, KH, TC], FP8)
        wq_sb = sb1("wq_sb", [128, KH, 128], FP8)
        wq2_sb = sb1("wq2_sb", [128, KH, 128], FP8)

        hs = {}

        def emit_load_x(c):
            h = hs.setdefault(c, {})
            sx = sX2[c % 2]
            xw = []
            if c >= 2:
                xw.append((sSTp[c % 2], sSTp[c % 2].v))
            step("sync", lambda e, a=c * TC, b=(c + 1) * TC, xb=xbufs[c % 2]:
                 e.dma_start(out=xb[:, :, :], in_=xTv[:, :, a:b]),
                 waits=xw, incs=[(sx, 16)])
            h["x"] = (sx, sx.bump(16))

        def emit_stats(c):
            h = hs[c]
            xb = xbufs[c % 2]
            pw = []
            if c >= 1:
                pw += [(sXN, sXN.v), hs[c - 1]["xnsp"], hs[c - 1]["xa"]]
            step("tensor", lambda e: [e.matmul(pb[4][0:1, 0:TC], ones[:, 0:1],
                                               xb[:, kc, :], start=(kc == 0),
                                               stop=(kc == KD - 1))
                                      for kc in range(KD)][-1],
                 waits=[h["x"], t_ones], incs=[(sT, 1)])
            t_sumx = (sT, sT.bump())
            t_sq = []
            for kc in range(KD):
                scr = xn_sb[:, kc % 2, :]
                w_ = [h["x"]] + pw if kc < 2 else [(sT, t_sq[kc - 2][1])]
                step("scalar", lambda e, kc=kc, scr=scr: e.activation(
                    scr, xb[:, kc, :], AF.Square), waits=w_, incs=[(sS, 1)])
                t_s = (sS, sS.bump())
                step("tensor", lambda e, kc=kc, scr=scr: e.matmul(
                    pb[5][0:1, 0:TC], ones[:, 0:1], scr,
                    start=(kc == 0), stop=(kc == KD - 1)),
                     waits=[t_s], incs=[(sT, 1)])
                t_sq.append((sT, sT.bump()))
            t_sumsq = t_sq[-1]

            ew = [t_sumx, t_sumsq]
            if c >= 1:
                ew += [hs[c - 1]["bc1"], hs[c - 1]["bc2"]]

            def evrows(e):
                CI("scalar", e.copy(row(0), pb[4][0:1, 0:TC]))
                return e.copy(row(1), pb[5][0:1, 0:TC])
            step("scalar", evrows, waits=ew, incs=[(sS, 1)])
            h["_ev"] = (sS, sS.bump())

        def emit_rows1(c):
            h = hs[c]

            def rows1(e):
                ops = [
                    lambda: e.tensor_scalar(row(0), row(0), INV_D, None, OP.mult),
                    lambda: e.tensor_scalar(row(1), row(1), INV_D, None, OP.mult),
                    lambda: e.tensor_tensor(row(2), row(0), row(0), OP.mult),
                    lambda: e.tensor_tensor(row(1), row(1), row(2), OP.subtract),
                    lambda: e.tensor_scalar(row(2), row(1), 1e-5, None, OP.add),
                    lambda: e.tensor_scalar(row(3), row(2), 1e-6, None, OP.mult),
                ]
                for op in ops:
                    CI("vector", op())
                    CW("vector", e)
                return e.tensor_tensor(row(3), row(1), row(3), OP.add)
            step("vector", rows1, waits=[h["_ev"], t_onerow], incs=[(sV, 1)])
            t_r1 = (sV, sV.bump())

            def sqrts(e):
                CI("scalar", e.activation(row(1), row(2), AF.Sqrt))
                return e.activation(row(4), row(3), AF.Sqrt)
            step("scalar", sqrts, waits=[t_r1], incs=[(sS, 1)])
            h["_sqr"] = (sS, sS.bump())

        def emit_rows2a(c):
            h = hs[c]

            def rows2a(e):
                ops = [
                    lambda: e.reciprocal(row(5), row(1)),
                    lambda: e.tensor_tensor(row(1), row(5), row(5), OP.mult),
                    lambda: e.tensor_tensor(row(1), row(2), row(1), OP.mult),
                    lambda: e.tensor_scalar(row(1), row(1), -0.5, 1.5, OP.mult,
                                            OP.add),
                ]
                for op in ops:
                    CI("vector", op())
                    CW("vector", e)
                return e.tensor_tensor(row(1), row(5), row(1), OP.mult)
            step("vector", rows2a, waits=[h["_sqr"]], incs=[(sV, 1)])
            h["_r2a"] = (sV, sV.bump())

        def emit_rows2b(c):
            h = hs[c]

            def rows2b(e):
                ops = [
                    lambda: e.reciprocal(row(5), row(4)),
                    lambda: e.tensor_tensor(row(4), row(5), row(5), OP.mult),
                    lambda: e.tensor_tensor(row(4), row(3), row(4), OP.mult),
                    lambda: e.tensor_scalar(row(4), row(4), -0.5, 1.5, OP.mult,
                                            OP.add),
                    lambda: e.tensor_tensor(row(4), row(5), row(4), OP.mult),
                    lambda: e.tensor_tensor(row(0), row(0), row(1), OP.mult),
                    lambda: e.tensor_tensor(row(5), row(2), row(1), OP.mult),
                ]
                for op in ops:
                    CI("vector", op())
                    CW("vector", e)
                return e.tensor_tensor(row(5), row(5), row(4), OP.mult)
            step("vector", rows2b, waits=[h["_r2a"]], incs=[(sV, 1)])
            h["rows"] = (sV, sV.bump())

        def emit_xn_a(c):
            h = hs[c]
            xb = xbufs[c % 2]
            bw = [h["rows"]]
            if c >= 1:
                bw += [hs[c - 1]["xa"]]
            step("tensor", lambda e: (e.matmul(pb[6][:, 0:TC], ones[0:1, :], row(1),
                                               start=True, stop=True),
                                      e.matmul(pb[7][:, 0:TC], ones[0:1, :], row(0),
                                               start=True, stop=True))[-1],
                 waits=bw, incs=[(sT, 1)])
            h["bc1"] = (sT, sT.bump())

            def mk_xn(e):
                for kc in range(KD):
                    inst = e.tensor_tensor(xn_sb[:, kc, :], xb[:, kc, :],
                                           pb[6][:, 0:TC], OP.mult)
                CI("vector", inst)
                CW("vector", e)
                for kc in range(KD):
                    inst = e.tensor_tensor(xn_sb[:, kc, :], xn_sb[:, kc, :],
                                           pb[7][:, 0:TC], OP.subtract)
                return inst
            step("vector", mk_xn, waits=[h["bc1"], h["x"]], incs=[(sV, 1)])
            h["xn"] = (sV, sV.bump())
            step("scalar", lambda e, a=c * TC, b=(c + 1) * TC: e.dma_start(
                out=xnv[:, :, a:b], in_=xn_sb[:, :, :]), waits=[h["xn"]],
                 incs=[(sXN, 16)])
            h["spill"] = (sXN, sXN.bump(16))

        def emit_xn_b(c):
            h = hs[c]
            step("tensor", lambda e: e.matmul(pb[6][:, 0:TC], ones[0:1, :], row(5),
                                              start=True, stop=True),
                 waits=[h["xn"]], incs=[(sT, 1)])
            h["bc2"] = (sT, sT.bump())

        def emit_xn_splits(c):
            h = hs[c]
            t_prev = (sT, sT.v)
            t_xnl = None
            for kc in range(KD):
                xw = [h["xn"]] + ([t_prev] if kc == 0 else [])
                t_xnl = act_split(xn_sb[:, kc, :], xnh[:, kc, :], xnl[:, kc, :], xw)
            h["xnsp"] = t_xnl

        def emit_xa_splits(c):
            h = hs[c]
            t_prev = (sT, sT.v)
            sttoks = {}
            for kc in range(KD):
                aw = [h["bc2"]]
                if kc == 0:
                    aw += [t_prev, (sSP, sSP.v)]
                step("vector", lambda e, kc=kc: e.scalar_tensor_tensor(
                    a_sb[:, kc, :], xn_sb[:, kc, :], wmixs[:, kc:kc + 1],
                    pb[6][:, 0:TC], OP.mult, OP.mult),
                     waits=aw, incs=[(sV, 1)])
                sttoks[kc] = (sV, sV.bump())
            for kc in range(KD):
                h["xa"] = act_split(a_sb[:, kc, :], xah[:, kc, :], xal[:, kc, :],
                                    [sttoks[kc]])

        gvd_cfg = ((bgs, AF.Sigmoid, sg_sb), (bvs, AF.Tanh, tv_sb),
                   (bds, AF.Sigmoid, sl_sb))

        def add_chunk1(c):
            gst = {"evac": {0: None, 1: None, 2: None}, "xs": None, "ls": None}

            def gvd_consume(wh, wl, wtoks, i, c=c, gst=gst):
                m, pi = i // 3, i % 3
                bias_t, func, dst = gvd_cfg[pi]
                mw = wtoks + [hs[c]["xa"]]
                if gst["evac"][pi] is not None:
                    mw.append(gst["evac"][pi])
                step("tensor", mm3(pb[pi][:, 0:TC], wh, wl, xah, xal, KD),
                     waits=mw, incs=[(sT, 1)])
                t_mm = (sT, sT.bump())
                if pi < 2:
                    ew = [t_mm] + ([gst["xs"]] if gst["xs"] else [])
                else:
                    ew = [t_mm] + ([gst["ls"]] if gst["ls"] else [])
                step("scalar", lambda e, m=m, pi=pi, bias_t=bias_t, func=func,
                     dst=dst: e.activation(dst[:, :], pb[pi][:, 0:TC], func,
                                           bias=bias_t[:, m:m + 1]),
                     waits=ew, incs=[(sS, 1)])
                gst["evac"][pi] = (sS, sS.bump())
                if pi == 2:
                    spw = [(sSP, sSP.v)] if (m == 0 and c >= 1) else []
                    step("vector", lambda e, m=m: e.tensor_tensor(
                        xs_sb[:, m, :], sg_sb[:, :], tv_sb[:, :], OP.mult),
                         waits=[gst["evac"][0], gst["evac"][1]] + spw,
                         incs=[(sV, 1)])
                    gst["xs"] = (sV, sV.bump())
                    step("vector", lambda e, m=m: e.tensor_scalar(
                        a_sb[:, m, :], sl_sb[:, :], 0.998, 0.001, OP.mult, OP.add),
                         waits=[gst["evac"][2]] + spw, incs=[(sV, 1)])
                    t_a = (sV, sV.bump())
                    cwait = [t_car0] if c == 0 else []

                    def lscan(e, m=m):
                        CW("vector", e)
                        CI("vector", e.tensor_tensor_scan(
                            sl_sb[:, :], a_sb[:, m, :], xs_sb[:, m, :],
                            carry[:, m:m + 1], OP.mult, OP.add))
                        CW("vector", e)
                        return e.tensor_copy(carry[:, m:m + 1], sl_sb[:, TC - 1:TC])
                    step("vector", lscan,
                         waits=[gst["xs"], t_a, gst["evac"][2]] + cwait,
                         incs=[(sV, 1)])
                    gst["ls"] = (sV, sV.bump())
                return t_mm

            for i in range(3 * KD):
                src = (wgT, wvT, wdT)[i % 3]
                m = i // 3
                SU(src[m], KD,
                   (lambda i: lambda wh, wl, wt: gvd_consume(wh, wl, wt, i))(i))
                if c + 1 < NCH:
                    if i == 0:
                        HOOK(lambda c=c: emit_load_x(c + 1))
                    elif i == 2:
                        HOOK(lambda c=c: emit_stats(c + 1))
                    elif i == 5:
                        HOOK(lambda c=c: emit_rows1(c + 1))
                    elif i == 7:
                        HOOK(lambda c=c: emit_rows2a(c + 1))
                    elif i == 9:
                        HOOK(lambda c=c: emit_rows2b(c + 1))
                    elif i == 12:
                        HOOK(lambda c=c: emit_xn_a(c + 1))
                    elif i == 16:
                        HOOK(lambda c=c: emit_xn_b(c + 1))

            def end_gvd(c=c, gst=gst):
                t_gvd_all = (sV, sV.v)
                step("scalar", lambda e, a=c * TC, b=(c + 1) * TC: e.dma_start(
                    out=xsv[:, :, a:b], in_=xs_sb[:, :, :]), waits=[t_gvd_all],
                     incs=[(sSP, 16)])
                step("scalar", lambda e, a=c * TC, b=(c + 1) * TC: e.dma_start(
                    out=av[:, :, a:b], in_=a_sb[:, :, :]), waits=[t_gvd_all],
                     incs=[(sSP, 16)])
                sSP.bump(32)
                if c == NCH - 1:
                    step("gpsimd", lambda e: e.dma_start(out=cc_in.ap(),
                                                         in_=carry[:, :]),
                         waits=[gst["ls"]], incs=[(sGD, 16)])
                    t_ccin = (sGD, sGD.bump(16))
                    step("gpsimd", lambda e: e.collective_compute(
                        "AllGather", OP.bypass,
                        replica_groups=[[0, 1], [2, 3], [4, 5], [6, 7]],
                        ins=[cc_in.ap()], outs=[cc_out.ap()]),
                         waits=[t_ccin], incs=[(sCC, 1)])
                    t_cc = (sCC, sCC.bump())
                    step("gpsimd", lambda e: e.dma_start(
                        out=hprev[:, :], in_=cc_out.ap()[0, :, :]), waits=[t_cc],
                         incs=[(sGD, 16)])
                    t_hp = (sGD, sGD.bump(16))
                    step("vector", lambda e: e.tensor_scalar(
                        carry2[:, :], hprev[:, :], selps[:, 0:1], None, OP.mult),
                         waits=[t_hp], incs=[(sV, 1)])
                    hs["c2"] = (sV, sV.bump())
            HOOK(end_gvd)

            w1st = {"h": None, "hb": {0: None, 1: None}}

            def w1_consume(wh, wl, wtoks, m, c=c, w1st=w1st):
                bk = 3 + m % 2
                mw = wtoks + [hs[c]["xnsp"]]
                if w1st["hb"][m % 2] is not None:
                    mw.append(w1st["hb"][m % 2])
                step("tensor", mm3(pb[bk][:, 0:TC], wh, wl, xnh, xnl, KD),
                     waits=mw, incs=[(sT, 1)])
                t_mm = (sT, sT.bump())
                step("scalar", lambda e, m=m, bk=bk: e.activation(
                    hq_sb[:, m, :], pb[bk][:, 0:TC], AF.Relu, bias=b1s[:, m:m + 1]),
                     waits=[t_mm], incs=[(sS, 1)])
                w1st["hb"][m % 2] = (sS, sS.bump())
                w1st["h"] = w1st["hb"][m % 2]
                return t_mm

            for m in range(KH):
                SU(w1T[m], KD,
                   (lambda m: lambda wh, wl, wt: w1_consume(wh, wl, wt, m))(m))
                if m == 2 and c + 1 < NCH:
                    HOOK(lambda c=c: emit_xa_splits(c + 1))
            if c + 1 < NCH:
                HOOK(lambda c=c: emit_xn_splits(c + 1))

            w2st = {"casts": [], "f": None, "part": None, "mm": None}

            def w2_tail(mo, t_mm, bk, c=c, w2st=w2st):
                fw = [t_mm, t_bq]
                if w2st["part"] is not None:
                    fw.append(w2st["part"])
                step("scalar", lambda e, mo=mo, bk=bk: e.activation(
                    f16_sb[:, :], pb[bk][:, 0:TC], AF.Identity,
                    bias=bq32[:, mo:mo + 1]), waits=fw, incs=[(sS, 1)])
                w2st["f"] = (sS, sS.bump())
                w2st.setdefault("fb", {})[mo % 2] = w2st["f"]

                def mkpart(e, mo=mo, xb=xbufs[c % 2]):
                    CI("vector", e.tensor_copy(tmpf[:, :], f16_sb[:, :]))
                    CW("vector", e)
                    return e.tensor_tensor(xb[:, mo, :], xb[:, mo, :],
                                           tmpf[:, :], OP.add)
                step("vector", mkpart, waits=[w2st["f"]], incs=[(sV, 1)])
                w2st["part"] = (sV, sV.bump())

            def w2_consume(stga, t_ld, q, mo, c=c, w2st=w2st, w1st=w1st):
                vwt = [t_ld]
                if w2st["mm"] is not None:
                    vwt.append(w2st["mm"])
                if q == 0:
                    vwt.append((sWQ, sWQ.v))
                step("vector", lambda e, stga=stga, q=q: e.tensor_copy(
                    wq_sb[:, q * 8:(q + 1) * 8, :], stga[:, 0:8, :]),
                     waits=vwt, incs=[(sV, 1)])
                t_cast = (sV, sV.bump())
                w2st["casts"].append(t_cast)
                if q < 3:
                    return t_cast
                casts = list(w2st["casts"])
                w2st["casts"] = []
                step("scalar", lambda e, mo=mo: e.dma_start(
                    out=w2q8[mo], in_=wq_sb[:, :, :]), waits=casts,
                     incs=[(sWQ, 16)])
                sWQ.bump(16)
                mw = casts + [w1st["h"]]
                fprev = w2st.get("fb", {}).get(mo % 2)
                if fprev is not None:
                    mw.append(fprev)
                elif mo % 2 == 0 and w1st["hb"][0] is not None:
                    mw.append(w1st["hb"][0])
                elif mo % 2 == 1 and w1st["hb"][1] is not None:
                    mw.append(w1st["hb"][1])

                bk = 3 + mo % 2

                def mmw2(e, bk=bk):
                    for j in range(KH // 2):
                        inst = e.matmul(pb[bk][:, 0:TC],
                                        wq_sb[:, 2 * j:2 * j + 2, :],
                                        hq_sb[:, 2 * j:2 * j + 2, :],
                                        start=(j == 0), stop=(j == KH // 2 - 1),
                                        perf_mode=DR)
                    return inst
                step("tensor", mmw2, waits=mw, incs=[(sT, 1)])
                w2st["mm"] = (sT, sT.bump())
                w2_tail(mo, w2st["mm"], bk)
                return w2st["mm"]

            if c == 0:
                for mo in range(KD):
                    for q in range(4):
                        RU(w2T[mo][:, q * 8:(q + 1) * 8, :], 8,
                           (lambda q, mo: lambda stga, t_ld:
                            w2_consume(stga, t_ld, q, mo))(q, mo))

            def w2_cached(c=c, w2st=w2st, w1st=w1st):
                mms = {}
                for mo in range(KD):
                    wq = wq_sb if mo % 2 == 0 else wq2_sb
                    sq = sQ2[mo % 2]
                    lw = [(sWQ, sWQ.v)]
                    if mo >= 2:
                        lw.append(mms[mo - 2])
                    else:
                        lw.append(hs[c - 1]["w2mm"][mo - 2 + KD])
                    step("sync", lambda e, mo=mo, wq=wq, sq=sq: e.dma_start(
                        out=wq[:, :, :], in_=w2q8[mo]).then_inc(sq.h, 16),
                         waits=lw)
                    t_l = (sq, sq.bump(16))
                    bk = 3 + mo % 2
                    mw = [t_l, w1st["h"]]
                    fprev = w2st.get("fb", {}).get(mo % 2)
                    if fprev is not None:
                        mw.append(fprev)
                    elif w1st["hb"][mo % 2] is not None:
                        mw.append(w1st["hb"][mo % 2])

                    def mmw2(e, wq=wq, bk=bk):
                        for j in range(KH // 2):
                            inst = e.matmul(pb[bk][:, 0:TC],
                                            wq[:, 2 * j:2 * j + 2, :],
                                            hq_sb[:, 2 * j:2 * j + 2, :],
                                            start=(j == 0),
                                            stop=(j == KH // 2 - 1),
                                            perf_mode=DR)
                        return inst
                    step("tensor", mmw2, waits=mw, incs=[(sT, 1)])
                    w2st["mm"] = (sT, sT.bump())
                    mms[mo] = w2st["mm"]
                    w2_tail(mo, w2st["mm"], bk)
                hs[c]["w2mm"] = mms

            def end_w2(c=c, w2st=w2st):
                if c == 0:
                    hs[0]["w2mm"] = {6: w2st["mm"], 7: w2st["mm"]}
                if c >= 1:
                    w2_cached()
                sst = sSTp[c % 2]
                step("scalar", lambda e, a=c * TC, b=(c + 1) * TC,
                     xb=xbufs[c % 2], sst=sst:
                     e.dma_start(out=outv[:, :, a:b], in_=xb[:, :, :]),
                     waits=[w2st["part"]], incs=[(sst, 16)])
                hs[c]["pstore"] = (sst, sst.bump(16))
            HOOK(end_w2)

        emit_load_x(0)
        emit_stats(0)
        emit_rows1(0)
        emit_rows2a(0)
        emit_rows2b(0)
        emit_xn_a(0)
        emit_xn_b(0)
        emit_xa_splits(0)
        emit_xn_splits(0)
        for c in range(NCH):
            add_chunk1(c)
        run_stream()

        emit_block()
        p1.close()

        p3 = contextlib.ExitStack()
        sb3 = lambda name, shape, dt: p3.enter_context(nc.sbuf_tensor(name, shape, dt))
        q_xn = sb3("q_xn", [128, KD, TC], F32)
        q_xs = sb3("q_xs", [128, KD, TC], F32)
        q_a = sb3("q_a", [128, KD, TC], F32)
        q_pa = sb3("q_pa", [128, KD, TC], F32)
        q_s = sb3("q_s", [128, KD, TC], F32)
        xbh = sb3("xbh", [128, KD, TC], BF16)
        xbl = sb3("xbl", [128, KD, TC], BF16)
        pmh = sb3("pmh", [128, KF, TC], BF16)
        pml = sb3("pml", [128, KF, TC], BF16)

        h3 = {}

        def p3_boundary():
            return [(sXN, sXN.v), (sSP, sSP.v), (sT, sT.v), (sS, sS.v),
                    (sV, sV.v), (sSTp[0], sSTp[0].v), (sSTp[1], sSTp[1].v)]

        def p3_load_xsa(c):
            h = h3.setdefault(c, {})
            lw = (p3_boundary() if c == 0 else
                  [h3[c - 1]["scan"], h3[c - 1]["xb"]])
            step("sync", lambda e, a=c * TC, b=(c + 1) * TC: e.dma_start(
                out=q_xs[:, :, :], in_=xsv[:, :, a:b]), waits=lw, incs=[(sP3A, 16)])
            sP3A.bump(16)
            step("sync", lambda e, a=c * TC, b=(c + 1) * TC: e.dma_start(
                out=q_a[:, :, :], in_=av[:, :, a:b]), incs=[(sP3A, 16)])
            sP3A.bump(16)
            h["lxa"] = (sP3A, sP3A.v)

        def p3_scans(c):
            h = h3[c]
            sw = [h["lxa"]]
            if c == 0:
                sw.append(hs["c2"])
            else:
                sw.append(h3[c - 1]["part2"])

            def scans(e):
                for m in range(KD):
                    CW("vector", e)
                    CI("vector", e.tensor_tensor_scan(
                        q_s[:, m, :], q_a[:, m, :], q_xs[:, m, :],
                        carry2[:, m:m + 1], OP.mult, OP.add))
                    CW("vector", e)
                    inst = e.tensor_copy(carry2[:, m:m + 1], q_s[:, m, TC - 1:TC])
                    if m < KD - 1:
                        CI("vector", inst)
                return inst
            step("vector", scans, waits=sw, incs=[(sV, 1)])
            h["scan"] = (sV, sV.bump())

        def p3_load_xn(c):
            h = h3[c]
            lw = p3_boundary() if c == 0 else [h3[c - 1]["part1"]]
            step("sync", lambda e, a=c * TC, b=(c + 1) * TC: e.dma_start(
                out=q_xn[:, :, :], in_=xnv[:, :, a:b]), waits=lw, incs=[(sP3X, 16)])
            h["lxn"] = (sP3X, sP3X.bump(16))

        def p3_part1(c):
            h = h3[c]
            step("vector", lambda e: e.tensor_tensor(
                q_s[:, :, :], q_xn[:, :, :], q_s[:, :, :], OP.add),
                 waits=[h["lxn"], h["scan"]], incs=[(sV, 1)])
            h["part1"] = (sV, sV.bump())

        def p3_load_pa(c):
            h = h3.setdefault(c, {})
            lw = [(sSTp[0], sSTp[0].v), (sSTp[1], sSTp[1].v)]
            step("sync", lambda e, a=c * TC, b=(c + 1) * TC: e.dma_start(
                out=q_pa[:, :, :], in_=outv[:, :, a:b]), waits=lw, incs=[(sP3P, 16)])
            h["lpa"] = (sP3P, sP3P.bump(16))

        def p3_part2(c):
            h = h3[c]
            step("vector", lambda e: e.tensor_tensor(
                q_pa[:, :, :], q_pa[:, :, :], q_s[:, :, :], OP.add),
                 waits=[h["lpa"], h["part1"]], incs=[(sV, 1)])
            h["part2"] = (sV, sV.bump())

        def p3_stats(c):
            h = h3[c]
            t_sq3 = []
            fw1 = [h3[c - 1]["fin1"]] if c >= 1 else []
            for kc in range(KD):
                scr = sl_sb if kc % 2 == 0 else sq2_sb
                w_ = [h["part1"]] + fw1 if kc < 2 else [(sT, t_sq3[kc - 2][1])]
                step("scalar", lambda e, kc=kc, scr=scr: e.activation(
                    scr[:, :], q_s[:, kc, :], AF.Square), waits=w_, incs=[(sS, 1)])
                t_s3 = (sS, sS.bump())
                step("tensor", lambda e, kc=kc, scr=scr: e.matmul(
                    pb[2][0:1, 0:TC], ones[:, 0:1], scr[:, :],
                    start=(kc == 0), stop=(kc == KD - 1)), waits=[t_s3],
                     incs=[(sT, 1)])
                t_sq3.append((sT, sT.bump()))
            step("scalar", lambda e: e.copy(row(0), pb[2][0:1, 0:TC]),
                 waits=[t_sq3[-1]], incs=[(sS, 1)])
            h["_ev3"] = (sS, sS.bump())

        def p3_rows3a(c):
            h = h3[c]
            step("vector", lambda e: e.tensor_scalar(
                row(1), row(0), INV_D, 1e-6, OP.mult, OP.add),
                 waits=[h["_ev3"]], incs=[(sV, 1)])
            t_u3 = (sV, sV.bump())
            step("scalar", lambda e: e.activation(row(2), row(1), AF.Sqrt),
                 waits=[t_u3], incs=[(sS, 1)])
            h["_y3"] = (sS, sS.bump())

        def p3_rows3b(c):
            h = h3[c]

            def rows3(e):
                ops = [
                    lambda: e.reciprocal(row(3), row(2)),
                    lambda: e.tensor_tensor(row(2), row(3), row(3), OP.mult),
                    lambda: e.tensor_tensor(row(2), row(1), row(2), OP.mult),
                    lambda: e.tensor_scalar(row(2), row(2), -0.5, 1.5, OP.mult,
                                            OP.add),
                ]
                for op in ops:
                    CI("vector", op())
                    CW("vector", e)
                return e.tensor_tensor(row(4), row(3), row(2), OP.mult)
            step("vector", rows3, waits=[h["_y3"]], incs=[(sV, 1)])
            t_r3 = (sV, sV.bump())
            step("tensor", lambda e: e.matmul(pb[3][:, 0:TC], ones[0:1, :],
                                              row(4), start=True, stop=True),
                 waits=[t_r3], incs=[(sT, 1)])
            h["bc3"] = (sT, sT.bump())

        def p3_xbsplit(c):
            h = h3[c]
            sttoks = {}
            for kc in range(KD):
                aw = [h["bc3"]]
                if kc == 0:
                    aw += [h["scan"]]
                step("vector", lambda e, kc=kc: e.scalar_tensor_tensor(
                    q_a[:, kc, :], q_s[:, kc, :], wffns[:, kc:kc + 1],
                    pb[3][:, 0:TC], OP.mult, OP.mult),
                     waits=aw, incs=[(sV, 1)])
                sttoks[kc] = (sV, sV.bump())
            for kc in range(KD):
                h["xb"] = act_split(q_a[:, kc, :], xbh[:, kc, :], xbl[:, kc, :],
                                    [sttoks[kc]])

        def add_chunk3(c):
            g3 = {"sil": None, "pm": None, "wo": None, "pml": None, "sw2": None}

            def g3_consume(wh, wl, wtoks, i, c=c, g3=g3):
                m, r = i // 3, i % 3
                if r == 0:
                    mw = wtoks + [h3[c]["xb"]]
                    if g3["sil"] is not None:
                        mw.append(g3["sil"])
                    if m == 0 and c >= 1:
                        mw.append(h3[c - 1]["fin2"])
                    step("tensor", mm3(pb[0][:, 0:TC], wh, wl, xbh, xbl, KD),
                         waits=mw, incs=[(sT, 1)])
                    t_g = (sT, sT.bump())
                    sw2 = [t_g] + ([g3["pm"]] if g3["pm"] else [])
                    step("scalar", lambda e: e.activation(
                        sg_sb[:, :], pb[0][:, 0:TC], AF.Silu),
                         waits=sw2, incs=[(sS, 1)])
                    g3["sil"] = (sS, sS.bump())
                    return t_g
                if r == 1:
                    mw2 = wtoks + [h3[c]["xb"]]
                    if g3["pm"] is not None:
                        mw2.append(g3["pm"])
                    if m == 0 and c >= 1:
                        mw2.append(h3[c - 1]["fin2"])
                    step("tensor", mm3(pb[1][:, 0:TC], wh, wl, xbh, xbl, KD),
                         waits=mw2, incs=[(sT, 1)])
                    t_u = (sT, sT.bump())
                    step("vector", lambda e: e.tensor_tensor(
                        tmpf[:, :], sg_sb[:, :], pb[1][:, 0:TC], OP.mult),
                         waits=[g3["sil"], t_u] + ([g3["pml"]] if g3["pml"] else []),
                         incs=[(sV, 1)])
                    g3["pm"] = (sV, sV.bump())
                    g3["pml"] = act_split(tmpf[:, :], pmh[:, m, :], pml[:, m, :],
                                          [g3["pm"]])
                    g3.setdefault("pml_m", {})[m] = g3["pml"]
                    return t_u

                def mmwo(e, wh=wh, wl=wl, m=m):
                    for mo in range(6):
                        e.matmul(pb[2 + mo][:, 0:TC], wh[:, mo, :], pmh[:, m, :],
                                 start=(m == 0), stop=False)
                        e.matmul(pb[2 + mo][:, 0:TC], wh[:, mo, :], pml[:, m, :],
                                 start=False, stop=False)
                        inst = e.matmul(pb[2 + mo][:, 0:TC], wl[:, mo, :],
                                        pmh[:, m, :], start=False,
                                        stop=(m == KF - 1))
                    return inst
                wo_w = wtoks + [g3["pml_m"][m]]
                if m == 0 and c >= 1:
                    wo_w.append(h3[c - 1]["fin1"])
                step("tensor", mmwo, waits=wo_w, incs=[(sT, 1)])
                g3["wo"] = (sT, sT.bump())
                return g3["wo"]

            descs = []
            for m in range(KF):
                descs.append((0, m))
                descs.append((1, m))
                if m >= 1:
                    descs.append((2, m - 1))
            descs.append((2, KF - 1))
            for i, (r, m) in enumerate(descs):
                if r == 0:
                    src, ns = wgateT[m], KD
                elif r == 1:
                    src, ns = wupT[m], KD
                else:
                    src = woutv[m, :, 0:768].rearrange("p (s q) -> p s q", q=128)
                    ns = 6
                SU(src, ns,
                   (lambda r, m: lambda wh, wl, wt:
                    g3_consume(wh, wl, wt, m * 3 + r))(r, m))
                if c + 1 < NCH:
                    if i == 2:
                        HOOK(lambda c=c: p3_load_xsa(c + 1))
                    elif i == 6:
                        HOOK(lambda c=c: p3_scans(c + 1))
                    elif i == 40:
                        HOOK(lambda c=c: p3_load_xn(c + 1))
                    elif i == 44:
                        HOOK(lambda c=c: p3_part1(c + 1))

            def fin1(c=c, g3=g3):
                def finev1(e):
                    for mo in range(6):
                        inst = e.tensor_tensor(q_pa[:, mo, :], q_pa[:, mo, :],
                                               pb[2 + mo][:, 0:TC], OP.add)
                    return inst
                step("vector", finev1, waits=[g3["wo"], h3[c]["part2"]],
                     incs=[(sV, 1)])
                h3[c]["fin1"] = (sV, sV.bump())
                if c + 1 < NCH:
                    p3_stats(c + 1)
            HOOK(fin1)

            def sw2_consume(wh, wl, wtoks, m, c=c, g3=g3):
                mw3 = list(wtoks)
                if m == 0:
                    mw3 += [g3["pm"], g3["sil"], g3["pml"], h3[c]["fin1"]]

                def mm2(e, wh=wh, wl=wl, m=m):
                    for mo in range(2):
                        e.matmul(pb[mo][:, 0:TC], wh[:, mo, :], pmh[:, m, :],
                                 start=(m == 0), stop=False)
                        e.matmul(pb[mo][:, 0:TC], wh[:, mo, :], pml[:, m, :],
                                 start=False, stop=False)
                        inst = e.matmul(pb[mo][:, 0:TC], wl[:, mo, :],
                                        pmh[:, m, :], start=False,
                                        stop=(m == KF - 1))
                    return inst
                step("tensor", mm2, waits=mw3, incs=[(sT, 1)])
                g3["sw2"] = (sT, sT.bump())
                return g3["sw2"]

            for m in range(KF):
                SU(woutv[m, :, 768:1024].rearrange("p (s q) -> p s q", q=128), 2,
                   (lambda m: lambda wh, wl, wt: sw2_consume(wh, wl, wt, m))(m))
                if c + 1 < NCH:
                    if m == 1:
                        HOOK(lambda c=c: p3_rows3a(c + 1))
                    elif m == 4:
                        HOOK(lambda c=c: p3_rows3b(c + 1))
                    elif m == 8:
                        HOOK(lambda c=c: p3_xbsplit(c + 1))

            def fin2(c=c, g3=g3):
                def finev2(e):
                    CI("vector", e.tensor_tensor(q_pa[:, 6, :], q_pa[:, 6, :],
                                                 pb[0][:, 0:TC], OP.add))
                    return e.tensor_tensor(q_pa[:, 7, :], q_pa[:, 7, :],
                                           pb[1][:, 0:TC], OP.add)
                step("vector", finev2, waits=[g3["sw2"]], incs=[(sV, 1)])
                h3[c]["fin2"] = (sV, sV.bump())
                sst = sSTp[c % 2]
                step("scalar", lambda e, a=c * TC, b=(c + 1) * TC, sst=sst:
                     e.dma_start(out=outv[:, :, a:b], in_=q_pa[:, :, :]),
                     waits=[h3[c]["fin1"], h3[c]["fin2"]], incs=[(sst, 16)])
                sst.bump(16)
                if c + 1 < NCH:
                    p3_load_pa(c + 1)
                    p3_part2(c + 1)
            HOOK(fin2)

        p3_load_xsa(0)
        p3_scans(0)
        p3_load_xn(0)
        p3_part1(0)
        p3_load_pa(0)
        p3_part2(0)
        p3_stats(0)
        p3_rows3a(0)
        p3_rows3b(0)
        p3_xbsplit(0)
        for c in range(NCH):
            add_chunk3(c)
        run_stream()

        step("gpsimd", None, waits=[(sSTp[0], sSTp[0].v), (sSTp[1], sSTp[1].v)])
        step("sync", None, waits=[(sSTp[0], sSTp[0].v), (sSTp[1], sSTp[1].v)])

        emit_block()
        p3.close()

    return nc


def kernel(L0_attn_x, L0_attn_rms_mix_w, L0_attn_rms_ffn_w, L0_attn_Wg, L0_attn_bg,
           L0_attn_Wv, L0_attn_bv, L0_attn_Wd, L0_attn_bd, L0_attn_W_gate,
           L0_attn_W_up, L0_attn_W_out, L0_ffn_w1, L0_ffn_b1, L0_ffn_w2, L0_ffn_b2):
    x = np.asarray(L0_attn_x, dtype=np.float32)

    def t(a):
        return np.ascontiguousarray(np.asarray(a, dtype=np.float32).T)

    def tw(a):
        w = np.asarray(a, dtype=np.float32)
        M, K = w.shape
        return np.ascontiguousarray(
            w.reshape(M // 128, 128, K // 128, 128).transpose(0, 3, 2, 1))

    def b2(a, k):
        return np.ascontiguousarray(np.asarray(a, dtype=np.float32).reshape(k, 128).T)

    shared = {
        "wgT": tw(L0_attn_Wg), "wvT": tw(L0_attn_Wv), "wdT": tw(L0_attn_Wd),
        "wgateT": tw(L0_attn_W_gate), "wupT": tw(L0_attn_W_up),
        "woutT": t(L0_attn_W_out),
        "w1T": tw(L0_ffn_w1), "w2T": tw(L0_ffn_w2),
        "bg2": b2(L0_attn_bg, KD), "bv2": b2(L0_attn_bv, KD), "bd2": b2(L0_attn_bd, KD),
        "b12": b2(L0_ffn_b1, KH), "b22": b2(L0_ffn_b2, KD),
        "wmix2": b2(L0_attn_rms_mix_w, KD), "wffn2": b2(L0_attn_rms_ffn_w, KD),
    }
    in_maps = []
    for core in range(8):
        b, h = core // 2, core % 2
        xc = np.ascontiguousarray(x[b, h * T:(h + 1) * T, :].T)
        d = dict(shared)
        d["xT"] = xc
        d["selp"] = np.full((128, 1), float(h), dtype=np.float32)
        in_maps.append(d)

    nc = build_nc()
    res = run_bass_kernel_spmd(nc, in_maps, core_ids=list(range(8)))
    out = np.empty((B, S, D), dtype=np.float32)
    for core in range(8):
        b, h = core // 2, core % 2
        out[b, h * T:(h + 1) * T, :] = res.results[core]["outT"].T
    return out
